# version 02
# baseline (speedup 1.0000x reference)
import math
from contextlib import ExitStack

import numpy as np
import ml_dtypes

import concourse.bass as bass
import concourse.mybir as mybir
from concourse.bass_utils import run_bass_kernel_spmd

F32 = mybir.dt.float32
BF16 = mybir.dt.bfloat16
I32 = mybir.dt.int32
AF = mybir.ActivationFunctionType
ALU = mybir.AluOpType
AX = mybir.AxisListType

S = 2048
D = 2048
NT = 16
NH = 16
QLORA = 1536
SC192 = 1.0 / math.sqrt(192.0)
INNER = 8192
QW = 512
NQ = S // QW

ENGS = ("sync", "tensor", "vector", "scalar", "gpsimd")


class Prog:
    def __init__(self, nc):
        self.nc = nc
        self.recs = {e: [] for e in ENGS}
        self.sem_handles = {}
        self.cnt = {}
        self.ctx = ExitStack()

    def sem(self, name):
        if name not in self.sem_handles:
            self.sem_handles[name] = self.ctx.enter_context(self.nc.semaphore(name))
            self.cnt[name] = 0
        return self.sem_handles[name]

    def wait(self, eng, semname, target=None):
        h = self.sem(semname)
        if target is None:
            target = self.cnt[semname]
        assert target <= self.cnt[semname], (semname, target, self.cnt[semname])
        if target > 0:
            self.recs[eng].append(lambda e, hh=h, t=target: e.wait_ge(hh, t))

    def emit(self, eng, build, inc=None, n=1):
        if inc is not None:
            h = self.sem(inc)
            self.cnt[inc] += n
            self.recs[eng].append(lambda e, b=build, hh=h, nn=n: b(e).then_inc(hh, nn))
        else:
            self.recs[eng].append(build)

    def dma(self, eng, out, in_, inc, **kw):
        self.emit(eng, lambda e, o=out, i=in_, k=dict(kw): e.dma_start(out=o, in_=i, **k),
                  inc=inc, n=16)

    def finalize(self, block):
        for eng in ENGS:
            recs = self.recs[eng]
            getattr(block, eng)(lambda e, r=recs: [f(e) for f in r])


def host_consts():
    ident = np.eye(128, dtype=np.float32)
    mask = np.where(np.arange(128)[None, :] > np.arange(128)[:, None],
                    np.float32(-1e30), np.float32(0.0)).astype(np.float32)
    P = np.zeros((64, 64), dtype=np.float32)
    P[:32, 32:] = -np.eye(32, dtype=np.float32)
    P[32:, :32] = np.eye(32, dtype=np.float32)
    cols = np.zeros((128, 8), dtype=np.float32)
    cols[:, 0] = 1e-5
    cols[:, 1] = 1e-6
    cols[:, 2] = 0.5
    cols[:, 3] = -0.5
    cols[:, 4] = -math.log(2.0 * math.pi)
    cols[:, 5] = 0.25
    iota = np.arange(2048, dtype=np.float32)[None, :]
    return dict(
        c_ident=ident,
        c_identb=ident.astype(ml_dtypes.bfloat16),
        c_mask=mask,
        c_rot=P.T.astype(ml_dtypes.bfloat16),
        c_cols=cols,
        c_iota=iota,
        c_onescol=np.ones((128, 1), dtype=np.float32),
        c_onescolb=np.ones((128, 1), dtype=ml_dtypes.bfloat16),
        c_onesrow=np.ones((1, 128), dtype=np.float32),
    )


def build_program(dbg=(), upto="F"):
    PHASES = ["A", "B", "C", "D", "E", "F"]
    stop_at = PHASES.index(upto)

    nc = bass.Bass("TRN2", target_bir_lowering=False, detect_race_conditions=False)
    p = Prog(nc)
    p.open_stacks = []
    EP = p.ctx.enter_context

    ap = {}
    def din(name, shape, dt=F32):
        ap[name] = nc.dram_tensor(name, shape, dt, kind="ExternalInput").ap()
    din("x", [S, D])
    din("wqa", [D, QLORA]); din("wqb", [QLORA, NH * 192]); din("wkva", [D, 576])
    din("wkvb", [512, NH * 256]); din("wo", [D, D])
    din("qlnw", [12, 128]); din("kvlnw", [4, 128])
    din("w1t", [D, INNER]); din("w2t", [INNER, D])
    din("b1", [64, 128]); din("b2", [16, 128]); din("b2row", [1, D])
    din("gln", [1, INNER]); din("bln", [1, INNER])
    din("c_ident", [128, 128]); din("c_identb", [128, 128], BF16)
    din("c_mask", [128, 128]); din("c_rot", [64, 64], BF16)
    din("c_cols", [128, 8]); din("c_iota", [1, 2048])
    din("c_onescol", [128, 1]); din("c_onescolb", [128, 1], BF16)
    din("c_onesrow", [1, 128])

    out = nc.dram_tensor("out", [S, D], F32, kind="ExternalOutput").ap()

    dbg_out = {}
    def dbg_decl(name, shape, dt=F32):
        if name in dbg:
            dbg_out[name] = nc.dram_tensor("dbg_" + name, shape, dt,
                                           kind="ExternalOutput").ap()
    for nm, sh, dt in [
        ("lnx", [128, 16 * 2048], BF16), ("q_a", [128, 12 * 2048], BF16),
        ("ckv_n", [128, 4 * 2048], BF16), ("kpe_r", [64, 2048], BF16),
        ("cs_pack", [128, 2048], BF16), ("escale", [128, 16], F32),
        ("attnout", [128, 16 * 2048], BF16), ("x2", [S, D], F32),
        ("int1", [D, S], BF16), ("osc1", [128, 16], F32),
        ("tern1", [128, 2048], BF16), ("wm", [128, 2], F32),
        ("h0", [128, 64 * QW], BF16), ("int2_0", [128, 64 * QW], BF16),
        ("osc2", [128, 16], F32),
    ]:
        dbg_decl(nm, sh, dt)
    def DBG(name):
        return name in dbg_out

    x2s = nc.dram_tensor("x2s", [S, D], F32, kind="Internal").ap()
    int1s = nc.dram_tensor("int1s", [D, S], BF16, kind="Internal").ap()
    w1q = nc.dram_tensor("w1q", [64, D, 128], BF16, kind="Internal").ap()
    w2q = nc.dram_tensor("w2q", [16, INNER, 128], BF16, kind="Internal").ap()
    rowscr = nc.dram_tensor("rowscr", [16, 128], F32, kind="Internal").ap()

    ident = EP(nc.sbuf_tensor("ident", [128, 128], F32))
    identb = EP(nc.sbuf_tensor("identb", [128, 128], BF16))
    maskt = EP(nc.sbuf_tensor("maskt", [128, 128], F32))
    rotm = EP(nc.sbuf_tensor("rotm", [64, 64], BF16))
    ccols = EP(nc.sbuf_tensor("ccols", [128, 8], F32))
    onescol = EP(nc.sbuf_tensor("onescol", [128, 1], F32))
    onescolb = EP(nc.sbuf_tensor("onescolb", [128, 1], BF16))
    onesrow = EP(nc.sbuf_tensor("onesrow", [1, 128], F32))
    qlnwc = EP(nc.sbuf_tensor("qlnwc", [128, 12], F32))
    kvlnwc = EP(nc.sbuf_tensor("kvlnwc", [128, 4], F32))
    b1c = EP(nc.sbuf_tensor("b1c", [128, 64], F32))
    b2c = EP(nc.sbuf_tensor("b2c", [128, 16], F32))
    stats = EP(nc.sbuf_tensor("stats", [128, 48], F32))
    _statmap = {}
    def scol(name, n=1):
        if name not in _statmap:
            base = sum(v[1] for v in _statmap.values())
            assert base + n <= 48, (name, base)
            _statmap[name] = (base, n)
        b, nn = _statmap[name]
        return stats[:, b:b + nn]
    wder = EP(nc.sbuf_tensor("wder", [2, 4], F32))
    wch = [EP(nc.sbuf_tensor(f"wch{i}", [128, 64], F32)) for i in range(2)]
    def scol2(name):
        if name not in _statmap:
            base = sum(v[1] for v in _statmap.values())
            assert base + 1 <= 48
            _statmap[name] = (base, 1)
        b, _n = _statmap[name]
        return stats[0:2, b:b + 1]
    for _nm in ("wtmp", "w1s", "w2s"):
        scol(_nm)
    cs_pack = EP(nc.sbuf_tensor("cs_pack", [128, 2048], BF16))
    kpe_r = EP(nc.sbuf_tensor("kpe_r", [64, 2048], BF16))
    escale = EP(nc.sbuf_tensor("escale", [128, 16], F32))
    osc1c = EP(nc.sbuf_tensor("osc1c", [128, 16], F32))
    osc2c = EP(nc.sbuf_tensor("osc2c", [128, 16], F32))
    mstage = EP(nc.sbuf_tensor("mstage", [16, 128], F32))

    col_eps5 = ccols[:, 0:1]
    col_eps6 = ccols[:, 1:2]
    col_h = ccols[:, 2:3]
    col_nh = ccols[:, 3:4]
    col_l2pi = ccols[:, 4:5]
    col_q = ccols[:, 5:6]

    def MM(dst, lhsT, rhs, start, stop, inc=None):
        p.emit("tensor", lambda e, d=dst, l=lhsT, r=rhs, s1=start, s2=stop:
               e.matmul(out=d, lhsT=l, rhs=r, start=s1, stop=s2), inc=inc)

    def TR(dst, src, idn, inc=None):
        p.emit("tensor", lambda e, d=dst, s=src, i=idn:
               e.transpose(out=d, in_=s, identity=i), inc=inc)

    def ACT(dst, src, func, bias=0.0, scale=1.0, accum=None, inc=None):
        p.emit("scalar", lambda e, d=dst, s=src, f=func, b=bias, sc=scale, a=accum:
               e.activation(out=d, in_=s, func=f, bias=b, scale=sc, accum_out=a),
               inc=inc)

    def TT(eng, dst, a, b, op, inc=None):
        p.emit(eng, lambda e, d=dst, x=a, y=b, o=op:
               e.tensor_tensor(out=d, in0=x, in1=y, op=o), inc=inc)

    def TS(eng, dst, a, s1, op1, s2=None, op2=None, inc=None, accum=None):
        if s2 is None:
            p.emit(eng, lambda e, d=dst, x=a, u=s1, o1=op1, ac=accum:
                   e.tensor_scalar(out=d, in0=x, scalar1=u, scalar2=None, op0=o1,
                                   accum_out=ac), inc=inc)
        else:
            p.emit(eng, lambda e, d=dst, x=a, u=s1, o1=op1, v=s2, o2=op2, ac=accum:
                   e.tensor_scalar(out=d, in0=x, scalar1=u, scalar2=v, op0=o1,
                                   op1=o2, accum_out=ac), inc=inc)

    def STT(eng, dst, in0, scalar, in1, op0, op1, inc=None):
        p.emit(eng, lambda e, d=dst, a=in0, s=scalar, b=in1, o0=op0, o1=op1:
               e.scalar_tensor_tensor(out=d, in0=a, scalar=s, in1=b, op0=o0, op1=o1),
               inc=inc)

    def CP(eng, dst, src, inc=None):
        p.emit(eng, lambda e, d=dst, s=src: e.tensor_copy(out=d, in_=s), inc=inc)

    def RED(eng, dst, src, op, absval=False, neg=False, inc=None):
        p.emit(eng, lambda e, d=dst, s=src, o=op, a=absval, n=neg:
               e.tensor_reduce(out=d, in_=s, axis=AX.X, op=o,
                               apply_absolute_value=a, negate=n), inc=inc)

    def RECIP(dst, src, inc=None):
        p.emit("vector", lambda e, d=dst, s=src: e.reciprocal(out=d, in_=s), inc=inc)

    def NOP(eng, inc):
        p.emit(eng, lambda e: e.nop(), inc=inc)

    def dump(name, src):
        if not DBG(name):
            return
        p.wait("scalar", "VE"); p.wait("scalar", "SE"); p.wait("scalar", "PO")
        p.wait("scalar", "PE")
        p.dma("scalar", dbg_out[name], src, inc="ST")
        for eng in ("vector", "gpsimd", "tensor", "scalar", "sync"):
            p.wait(eng, "ST")

    def SP(n=2):
        for _ in range(n):
            p.emit("vector", lambda e, d=scol("junkV"), s=ccols[:, 7:8]:
                   e.tensor_copy(out=d, in_=s))

    def VEM():
        SP(1)
        p.emit("vector", lambda e, d=scol("junkV"), s=ccols[:, 7:8]:
               e.tensor_copy(out=d, in_=s), inc="VE")

    def phase_barrier():
        p.emit("vector", lambda e, d=scol("junkV"), s=ccols[:, 7:8]:
               e.tensor_copy(out=d, in_=s), inc="BAR")
        p.emit("gpsimd", lambda e, d=scol("junkP"), s=ccols[:, 6:7]:
               e.tensor_copy(out=d, in_=s), inc="BAR")
        p.emit("scalar", lambda e, d=scol("junkS"), s=ccols[:, 7:8]:
               e.activation(out=d, in_=s, func=AF.Copy), inc="BAR")
        p.emit("tensor", lambda e: e.nop(), inc="BAR")
        for _e in ENGS:
            p.wait(_e, "BAR")

    def ACTACC(dump_dst, src, func, acc_final, bias=0.0, scale=1.0):
        ACT(dump_dst, src, func, bias=bias, scale=scale, accum=scol("accraw"))
        ACT(acc_final, scol("accraw"), AF.Copy, inc="SE")

    def lnstats(xt, scra, width=D):
        RED("vector", scol("sx"), xt, ALU.add, inc="VE")
        p.wait("scalar", "VE")
        ACTACC(scra, xt, AF.Square, scol("sq"))
        p.wait("vector", "SE")
        TS("vector", scol("mu"), scol("sx"), 1.0 / width, ALU.mult)
        TS("vector", scol("m2"), scol("sq"), 1.0 / width, ALU.mult)
        SP()
        STT("vector", scol("v2"), scol("mu"), scol("mu"), scol("m2"),
            ALU.mult, ALU.subtract)
        VEM()
        p.wait("scalar", "VE")
        ACT(scol("v1"), scol("v2"), AF.Sqrt, bias=col_eps5, scale=-1.0, inc="SE")
        p.wait("vector", "SE")
        RECIP(scol("rv"), scol("v1"))
        SP()

    wtA = [("1", kc, sc_) for kc in range(16) for sc_ in range(4)] + \
          [("2", kc, 0) for kc in range(64)]
    wtB = list(wtA)
    wstate = {"phase": "A", "i": 0, "po_guard": 0, "ws_guard": [0, 0]}
    wbufs = {}

    def wsrc(t):
        nm, kc, sc_ = t
        if nm == "1":
            return ap["w1t"][128 * kc:128 * (kc + 1), 2048 * sc_:2048 * (sc_ + 1)]
        return ap["w2t"][128 * kc:128 * (kc + 1), :]

    def wprep_step(n=1):
        wsub, wg = wbufs["wsub"], wbufs["wg"]
        for _ in range(n):
            i = wstate["i"]
            if wstate["phase"] == "A":
                if i >= len(wtA):
                    return
                t = wtA[i]
                nm = t[0]
                p.wait("sync", "WVE", wstate["po_guard"])
                p.dma("sync", wsub[:], wsrc(t), inc="WLA")
                p.wait("vector", "WLA", 16 * (i + 1))
                RED("vector", osc1c[:], wsub[:].rearrange("p (a b) -> p a b", b=128),
                    ALU.add, absval=True)
                SP()
                wi = 0 if nm == "1" else 1
                ci = i if nm == "1" else t[1]
                RED("vector", wch[wi][:, ci:ci + 1], osc1c[:], ALU.add)
                p.emit("vector", lambda e, d=scol("junkV"), s=ccols[:, 7:8]:
                       e.tensor_copy(out=d, in_=s), inc="WVE")
                wstate["po_guard"] = p.cnt["WVE"]
                wstate["i"] = i + 1
            else:
                if i >= len(wtB):
                    return
                t = wtB[i]
                nm, kc, sc_ = t
                sl = i % 2
                p.wait("sync", "PO", wstate["po_guard"])
                p.dma("sync", wsub[:], wsrc(t), inc="WLB")
                p.wait("gpsimd", "WLB", 16 * (i + 1))
                if i == 0:
                    p.wait("gpsimd", "WF4", wstate["wf4"])
                p.wait("gpsimd", "WSS", wstate["ws_guard"][sl])
                thr = scol("thr1") if nm == "1" else scol("thr2")
                nthr = scol("nthr1") if nm == "1" else scol("nthr2")
                TS("gpsimd", wg[sl][:], wsub[:], thr, ALU.is_gt)
                TS("gpsimd", wbufs["wg2"][:], wsub[:], nthr, ALU.is_lt)
                TT("gpsimd", wg[sl][:], wg[sl][:], wbufs["wg2"][:], ALU.subtract,
                   inc="PO")
                wstate["po_guard"] = p.cnt["PO"]
                p.wait("scalar", "PO")
                if nm == "1":
                    dst = w1q[16 * sc_:16 * (sc_ + 1), 128 * kc:128 * (kc + 1), :]
                else:
                    dst = w2q[:, 128 * kc:128 * (kc + 1), :]
                p.dma("scalar", dst.rearrange("a i j -> i a j"),
                      wg[sl][:].rearrange("i (a j) -> i a j", j=128), inc="WSS")
                wstate["ws_guard"][sl] = p.cnt["WSS"]
                if DBG("tern1") and nm == "1" and kc == 0 and sc_ == 0:
                    p.dma("scalar", dbg_out["tern1"], wg[sl][:], inc="ST")
                wstate["i"] = i + 1

    def wprep_finalize():
        assert wstate["phase"] == "A" and wstate["i"] >= len(wtA)
        b1_ = _statmap["w1s"][0]
        assert _statmap["w2s"][0] == b1_ + 1
        p.wait("vector", "WVE")
        for wi in range(2):
            RED("vector", osc2c[:, 0:8], wch[wi][:].rearrange("p (a b) -> p a b", b=8),
                ALU.add)
            SP()
            RED("vector", stats[:, b1_ + wi:b1_ + wi + 1], osc2c[:, 0:8], ALU.add)
            SP()
        p.emit("vector", lambda e, d=scol("junkV"), s=ccols[:, 7:8]:
               e.tensor_copy(out=d, in_=s), inc="WVE")
        p.wait("scalar", "WVE")
        p.emit("scalar", lambda e, o=rowscr[8:10, :].rearrange("c i -> i c"),
               i_=stats[:, b1_:b1_ + 2]:
               e.dma_start(out=o, in_=i_, allow_slow_non_contiguous=True),
               inc="WF", n=16)
        p.wait("sync", "WF")
        p.dma("sync", mstage[0:2, :], rowscr[8:10, :], inc="WF2")
        p.wait("vector", "WF2")
        RED("vector", osc2c[0:2, 0:8], mstage[0:2, :].rearrange("p (a b) -> p a b", b=16),
            ALU.add)
        SP()
        RED("vector", scol2("wsum2"), osc2c[0:2, 0:8], ALU.add)
        SP()
        TS("vector", wder[0:2, 0:1], scol2("wsum2"), 1.0 / (2048.0 * 8192.0),
           ALU.mult, 1e-5, ALU.max)
        SP()
        TS("vector", wder[0:2, 1:2], wder[0:2, 0:1], 0.5, ALU.mult)
        TS("vector", wder[0:2, 2:3], wder[0:2, 0:1], 1.0 / 127.0, ALU.mult)
        SP()
        TS("vector", wder[0:2, 3:4], wder[0:2, 1:2], -1.0, ALU.mult)
        p.emit("vector", lambda e, d=scol("junkV"), s=ccols[:, 7:8]:
               e.tensor_copy(out=d, in_=s), inc="VE")
        p.wait("scalar", "VE")
        p.emit("scalar", lambda e, o=rowscr[10:12, 0:4], i_=wder[:]:
               e.dma_start(out=o, in_=i_), inc="WF3", n=16)
        p.wait("sync", "WF3")
        for col, (r_, c_) in (("thr1", (10, 1)), ("thr2", (11, 1)),
                              ("m127_1", (10, 2)), ("m127_2", (11, 2)),
                              ("nthr1", (10, 3)), ("nthr2", (11, 3))):
            p.dma("sync", scol(col),
                  rowscr[r_:r_ + 1, c_:c_ + 1].broadcast_to([128, 1]), inc="WF4")
        wstate["wf4"] = p.cnt["WF4"]
        wstate["po_guard"] = p.cnt.get("PO", 0)
        wstate["phase"] = "B"
        wstate["i"] = 0

    with ExitStack() as st0:
        IP = st0.enter_context(nc.psum_tensor("IP", [128, 2048], F32))
        ist = st0.enter_context(nc.sbuf_tensor("ist", [128, 512], F32))
        irow = st0.enter_context(nc.sbuf_tensor("irow", [1, 2048], F32))
        iw64 = st0.enter_context(nc.sbuf_tensor("iw64", [1, 64], F32))
        iu = st0.enter_context(nc.sbuf_tensor("iu", [64, 2048], F32))
        iup = st0.enter_context(nc.sbuf_tensor("iup", [64, 2048], F32))
        isA = st0.enter_context(nc.sbuf_tensor("isA", [64, 2048], F32))
        isB = st0.enter_context(nc.sbuf_tensor("isB", [64, 2048], F32))
        isI = st0.enter_context(nc.sbuf_tensor("isI", [64, 2048], I32))
        for nm, dst in [("c_ident", ident[:]), ("c_identb", identb[:]),
                        ("c_mask", maskt[:]), ("c_rot", rotm[:]),
                        ("c_cols", ccols[:]), ("c_onescol", onescol[:]),
                        ("c_onescolb", onescolb[:]), ("c_onesrow", onesrow[:]),
                        ("c_iota", irow[:])]:
            p.dma("sync", dst, ap[nm], inc="LD")
        p.dma("sync", ist[0:12, 0:128], ap["qlnw"], inc="LD")
        p.dma("sync", ist[0:4, 128:256], ap["kvlnw"], inc="LD")
        p.dma("sync", ist[0:64, 256:384], ap["b1"], inc="LD")
        p.dma("sync", ist[0:16, 384:512], ap["b2"], inc="LD")
        p.wait("tensor", "LD")
        TR(IP[:, 0:12], ist[0:12, 0:128], ident[0:12, 0:12])
        TR(IP[:, 12:16], ist[0:4, 128:256], ident[0:4, 0:4])
        TR(IP[:, 16:80], ist[0:64, 256:384], ident[0:64, 0:64])
        TR(IP[:, 80:96], ist[0:16, 384:512], ident[0:16, 0:16], inc="PE")
        p.wait("scalar", "PE")
        ACT(qlnwc[:], IP[:, 0:12], AF.Copy)
        ACT(kvlnwc[:], IP[:, 12:16], AF.Copy)
        ACT(b1c[:], IP[:, 16:80], AF.Copy)
        ACT(b2c[:], IP[:, 80:96], AF.Copy, inc="SE")
        ACT(iw64[0:1, 0:32], irow[0:1, 0:32], AF.Exp,
            bias=col_l2pi[0:1, :], scale=-math.log(10000.0) / 32.0)
        ACT(iw64[0:1, 32:64], irow[0:1, 0:32], AF.Exp,
            bias=col_l2pi[0:1, :], scale=-math.log(10000.0) / 32.0, inc="SE")
        p.wait("tensor", "SE")
        for sl in range(4):
            MM(IP[0:64, 512 * sl:512 * (sl + 1)], iw64[:],
               irow[0:1, 512 * sl:512 * (sl + 1)], True, True,
               inc=("PE" if sl == 3 else None))
        p.wait("vector", "PE")
        CP("vector", iu[:], IP[0:64, :])
        TS("vector", iup[:], IP[0:64, :], col_q[0:64, :], ALU.add, inc="VE")
        def rope_half(u_ap, dstrow):
            TS("vector", isI[:], u_ap, 1.0, ALU.mult)
            TS("vector", isB[:], isI[:], 0.0, ALU.add)
            TT("vector", isA[:], u_ap, isB[:], ALU.subtract)
            TS("vector", isB[:], isA[:], col_h[0:64, :], ALU.is_gt)
            TT("vector", isA[:], isA[:], isB[:], ALU.subtract)
            TS("vector", isB[:], isA[:], col_nh[0:64, :], ALU.is_lt)
            TT("vector", isA[:], isA[:], isB[:], ALU.add, inc="VE")
            p.wait("scalar", "VE")
            ACT(dstrow, isA[:], AF.Sin, scale=2.0 * math.pi, inc="SE")
        rope_half(iu[:], cs_pack[64:128, :])
        rope_half(iup[:], cs_pack[0:64, :])
        p.wait("sync", "SE")
        p.wait("gpsimd", "SE")
    dump("cs_pack", cs_pack[:])

    stk_w = ExitStack(); p.open_stacks.append(stk_w)
    wbufs["wsub"] = stk_w.enter_context(nc.sbuf_tensor("wsub", [128, 2048], F32, side="left"))
    wbufs["wg"] = [stk_w.enter_context(nc.sbuf_tensor("wgA", [128, 2048], BF16, side="left")),
                   stk_w.enter_context(nc.sbuf_tensor("wgB", [128, 2048], BF16, side="left"))]
    wbufs["wg2"] = stk_w.enter_context(nc.sbuf_tensor("wg2", [128, 2048], BF16, side="left"))
    stk_qa = ExitStack(); p.open_stacks.append(stk_qa)
    q_a = stk_qa.enter_context(nc.sbuf_tensor("q_a", [128, 12 * 2048], BF16, side="left"))
    ckv_n = stk_qa.enter_context(nc.sbuf_tensor("ckv_n", [128, 4 * 2048], BF16, side="left"))
    stk_lnx = ExitStack(); p.open_stacks.append(stk_lnx)
    lnx = stk_lnx.enter_context(nc.sbuf_tensor("lnx", [128, 16 * 2048], BF16, side="right"))

    with ExitStack() as stA:
        PTa = stA.enter_context(nc.psum_tensor("PTaA", [128, 1024], BF16))
        PTb = stA.enter_context(nc.psum_tensor("PTbA", [128, 1024], BF16))
        xstg = [stA.enter_context(nc.sbuf_tensor(f"xstgA{i}", [128, 2048], F32))
                for i in range(2)]
        lnts = stA.enter_context(nc.sbuf_tensor("lnts", [128, 2048], BF16))
        scra = stA.enter_context(nc.sbuf_tensor("scraA", [128, 2048], F32))

        xa_guard = [0, 0]
        pt_guard = [0, 0]
        tr_done = 0
        for T in range(NT):
            sl = T % 2
            semx = "XLA" if sl == 0 else "XLB"
            p.wait("sync", "VE", xa_guard[sl])
            p.dma("sync", xstg[sl][:], ap["x"][128 * T:128 * (T + 1), :], inc=semx)
            p.wait("vector", semx)
            xt = xstg[sl][:]
            lnstats(xt, scra[:])
            p.wait("vector", "PE", tr_done)
            TS("vector", lnts[:], xt, scol("mu"), ALU.subtract, scol("rv"), ALU.mult,
               inc="VE")
            xa_guard[sl] = p.cnt["VE"]
            p.wait("tensor", "VE")
            for g, PT in ((0, PTa), (1, PTb)):
                p.wait("tensor", "SE", pt_guard[g])
                for c in range(8):
                    cc = 8 * g + c
                    TR(PT[:, 128 * c:128 * (c + 1)], lnts[:, 128 * cc:128 * (cc + 1)],
                       identb[:], inc=("PE" if c == 7 else None))
                p.wait("scalar", "PE")
                dst = lnx[:].rearrange("i (a t) -> i a t", t=2048)[
                    :, 8 * g:8 * (g + 1), 128 * T:128 * (T + 1)]
                ACT(dst, PT[:].rearrange("i (a t) -> i a t", t=128), AF.Copy, inc="SE")
                pt_guard[g] = p.cnt["SE"]
            tr_done = p.cnt["PE"]
            wprep_step(4)
    dump("lnx", lnx[:])
    if stop_at <= 0:
        return _finish(p, nc, dbg_out, out, early=True)

    phase_barrier()
    with ExitStack() as stB:
        PBm = stB.enter_context(nc.psum_tensor("PBmB", [128, 2048], F32))
        Q2 = stB.enter_context(nc.psum_tensor("Q2B", [1, 2048], F32))
        wst = stB.enter_context(nc.sbuf_tensor("wstB", [128, 2048], F32))
        wbf = [stB.enter_context(nc.sbuf_tensor(f"wbfB{i}", [128, 2048], BF16))
               for i in range(2)]
        sqb = wbufs["wg2"]
        rrow = stB.enter_context(nc.sbuf_tensor("rrowB", [1, 2048], F32))

        p.wait("tensor", "VE")
        p.wait("tensor", "SE")
        cast_guard = [0, 0]
        mm_done = [0, 0]
        evac_guard = 0
        for m in range(12):
            sl = m % 2
            p.wait("sync", "POB", cast_guard[1 - sl])
            p.dma("sync", wst[:].rearrange("p (a j) -> p a j", j=128),
                  ap["wqa"][:, 128 * m:128 * (m + 1)].rearrange("(a p) j -> p a j", p=128),
                  inc="WLQ")
            p.wait("gpsimd", "WLQ", 16 * (m + 1))
            p.wait("gpsimd", "PE", mm_done[sl])
            CP("gpsimd", wbf[sl][:], wst[:], inc="POB")
            cast_guard[sl] = p.cnt["POB"]
            p.wait("tensor", "POB", cast_guard[sl])
            p.wait("tensor", "SE", evac_guard)
            for kc in range(16):
                for s4 in range(4):
                    MM(PBm[:, 512 * s4:512 * (s4 + 1)],
                       wbf[sl][:, 128 * kc:128 * (kc + 1)],
                       lnx[:, 2048 * kc + 512 * s4:2048 * kc + 512 * (s4 + 1)],
                       kc == 0, kc == 15,
                       inc=("PE" if (kc == 15 and s4 == 3) else None))
            mm_done[sl] = p.cnt["PE"]
            p.wait("scalar", "PE")
            ACT(q_a[:, 2048 * m:2048 * (m + 1)], PBm[:], AF.Copy)
            ACT(sqb[:], PBm[:], AF.Square, inc="SE")
            evac_guard = p.cnt["SE"]
            p.wait("tensor", "SE")
            for s4 in range(4):
                MM(Q2[0:1, 512 * s4:512 * (s4 + 1)], onescolb[:],
                   sqb[:, 512 * s4:512 * (s4 + 1)], m == 0, m == 11,
                   inc=("PE" if (m == 11 and s4 == 3) else None))
            wprep_step(3)
        p.wait("vector", "PE")
        TS("vector", rrow[:], Q2[0:1, :], 1.0 / QLORA, ALU.mult)
        VEM()
        p.wait("scalar", "VE")
        ACT(rrow[:], rrow[:], AF.Sqrt, bias=col_eps6[0:1, :], inc="SE")
        p.wait("vector", "SE")
        RECIP(rrow[:], rrow[:])
        VEM()
        p.wait("tensor", "VE")
        for qi in range(16):
            TR(PBm[:, 1024 + qi:1024 + qi + 1], rrow[0:1, 128 * qi:128 * (qi + 1)],
               ident[0:1, 0:1], inc=("PE" if qi == 15 else None))
        p.wait("vector", "PE")
        TS("vector", escale[:], PBm[:, 1024:1040], SC192, ALU.mult)
        VEM()
    dump("q_a", q_a[:])
    dump("escale", escale[:])
    if stop_at <= 1:
        return _finish(p, nc, dbg_out, out, early=True)

    phase_barrier()
    with ExitStack() as stC:
        PBm = stC.enter_context(nc.psum_tensor("PBmC", [128, 2048], F32))
        Q2 = stC.enter_context(nc.psum_tensor("Q2C", [1, 2048], F32))
        wst = stC.enter_context(nc.sbuf_tensor("wstC", [128, 2048], F32))
        wbf = [stC.enter_context(nc.sbuf_tensor(f"wbfC{i}", [128, 2048], BF16))
               for i in range(2)]
        sqb = wbufs["wg2"]
        rrow = stC.enter_context(nc.sbuf_tensor("rrowC", [1, 2048], F32))
        rbc = stC.enter_context(nc.sbuf_tensor("rbcC", [128, 2048], F32))
        kpebf = stC.enter_context(nc.sbuf_tensor("kpebf", [64, 2048], BF16))
        scrc = stC.enter_context(nc.sbuf_tensor("scrC", [64, 2048], F32))

        p.wait("tensor", "VE")
        p.wait("tensor", "SE")
        cast_guard = [0, 0]
        mm_doneC = [0, 0]
        evac_guard = 0
        for m in range(5):
            ncols = 128 if m < 4 else 64
            sl = m % 2
            p.wait("sync", "POC", cast_guard[1 - sl])
            p.dma("sync", wst[:, 0:16 * ncols].rearrange("p (a j) -> p a j", a=16),
                  ap["wkva"][:, 128 * m:128 * m + ncols].rearrange(
                      "(a p) j -> p a j", p=128), inc="WLC")
            p.wait("gpsimd", "WLC", 16 * (m + 1))
            p.wait("gpsimd", "PE", mm_doneC[sl])
            CP("gpsimd", wbf[sl][:, 0:16 * ncols], wst[:, 0:16 * ncols], inc="POC")
            cast_guard[sl] = p.cnt["POC"]
            p.wait("tensor", "POC", cast_guard[sl])
            p.wait("tensor", "SE", evac_guard)
            for kc in range(16):
                for s4 in range(4):
                    MM(PBm[0:ncols, 512 * s4:512 * (s4 + 1)],
                       wbf[sl][:, ncols * kc:ncols * (kc + 1)],
                       lnx[:, 2048 * kc + 512 * s4:2048 * kc + 512 * (s4 + 1)],
                       kc == 0, kc == 15,
                       inc=("PE" if (kc == 15 and s4 == 3) else None))
            mm_doneC[sl] = p.cnt["PE"]
            p.wait("scalar", "PE")
            if m < 4:
                ACT(ckv_n[:, 2048 * m:2048 * (m + 1)], PBm[:], AF.Copy)
                ACT(sqb[:], PBm[:], AF.Square, inc="SE")
                evac_guard = p.cnt["SE"]
                p.wait("tensor", "SE")
                for s4 in range(4):
                    MM(Q2[0:1, 512 * s4:512 * (s4 + 1)], onescolb[:],
                       sqb[:, 512 * s4:512 * (s4 + 1)], m == 0, m == 3,
                       inc=("PE" if (m == 3 and s4 == 3) else None))
            else:
                ACT(kpebf[:], PBm[0:64, :], AF.Copy, inc="SE")
                evac_guard = p.cnt["SE"]
            wprep_step(3)
        p.wait("vector", "PE")
        TS("vector", rrow[:], Q2[0:1, :], 1.0 / 512.0, ALU.mult)
        VEM()
        p.wait("scalar", "VE")
        ACT(rrow[:], rrow[:], AF.Sqrt, bias=col_eps6[0:1, :], inc="SE")
        p.wait("vector", "SE")
        RECIP(rrow[:], rrow[:])
        VEM()
        p.wait("tensor", "VE")
        p.wait("tensor", "SE")
        for s4 in range(4):
            MM(PBm[:, 512 * s4:512 * (s4 + 1)], onesrow[:],
               rrow[0:1, 512 * s4:512 * (s4 + 1)], True, True,
               inc=("PE" if s4 == 3 else None))
        p.wait("vector", "PE")
        CP("vector", rbc[:], PBm[:], inc="VE")
        for c in range(4):
            TT("vector", ckv_n[:, 2048 * c:2048 * (c + 1)],
               ckv_n[:, 2048 * c:2048 * (c + 1)], rbc[:], ALU.mult,
               inc=("VE" if c == 3 else None))
        p.wait("tensor", "SE")
        p.wait("tensor", "VE")
        for s4 in range(4):
            MM(PBm[0:64, 512 * s4:512 * (s4 + 1)], rotm[:],
               kpebf[:, 512 * s4:512 * (s4 + 1)], True, True,
               inc=("PE" if s4 == 3 else None))
        p.wait("vector", "PE")
        TT("vector", scrc[:], PBm[0:64, :], cs_pack[64:128, :], ALU.mult)
        TT("vector", kpe_r[:], kpebf[:], cs_pack[0:64, :], ALU.mult)
        TT("vector", kpe_r[:], kpe_r[:], scrc[:], ALU.add, inc="VE")
        p.wait("gpsimd", "VE")
        p.wait("sync", "VE")
    dump("ckv_n", ckv_n[:])
    dump("kpe_r", kpe_r[:])

    while wstate["phase"] == "A" and wstate["i"] < len(wtA):
        wprep_step(4)
    wprep_finalize()
    if stop_at <= 2:
        return _finish(p, nc, dbg_out, out, early=True)

    phase_barrier()
    stk_lnx.close(); p.open_stacks.remove(stk_lnx)
    stk_ao = ExitStack(); p.open_stacks.append(stk_ao)
    attnout = stk_ao.enter_context(
        nc.sbuf_tensor("attnout", [128, 16 * 2048], BF16, side="right"))

    with ExitStack() as stD:
        PB = stD.enter_context(nc.psum_tensor("PBD", [128, 2048], F32))
        PTa = stD.enter_context(nc.psum_tensor("PTaD", [128, 1024], BF16))
        PTb = stD.enter_context(nc.psum_tensor("PTbD", [128, 1024], BF16))
        PO1 = stD.enter_context(nc.psum_tensor("PO1D", [128, 512], F32))
        PO2 = stD.enter_context(nc.psum_tensor("PO2D", [128, 512], F32))
        qnope = stD.enter_context(nc.sbuf_tensor("qnope", [128, 2048], BF16))
        qper = stD.enter_context(nc.sbuf_tensor("qper", [64, 2048], BF16))
        knope = stD.enter_context(nc.sbuf_tensor("knope", [128, 2048], BF16))
        vtm = stD.enter_context(nc.sbuf_tensor("vtm", [128, 2048], BF16))
        psb = stD.enter_context(nc.sbuf_tensor("psb", [128, 2048], BF16))
        ptsb = stD.enter_context(nc.sbuf_tensor("ptsb", [128, 2048], BF16))
        scrd = stD.enter_context(nc.sbuf_tensor("scrD", [64, 2048], F32))
        wqbst = stD.enter_context(nc.sbuf_tensor("wqbst", [128, 256], F32))
        wqbbf = [stD.enter_context(nc.sbuf_tensor(f"wqbbf{i}", [128, 12 * 192], BF16))
                 for i in range(2)]
        wkvbbf = [stD.enter_context(nc.sbuf_tensor(f"wkvbbf{i}", [128, 1024], BF16))
                  for i in range(2)]

        head_ready = {}
        hw_state = {"POq": 0, "mmq": [0, 0]}

        def load_head_weights(h):
            sl = h % 2
            for kc in range(12):
                p.wait("sync", "POq", hw_state["POq"])
                p.dma("sync", wqbst[:, 0:192],
                      ap["wqb"][128 * kc:128 * (kc + 1), 192 * h:192 * (h + 1)],
                      inc="WQ")
                p.wait("gpsimd", "WQ")
                if kc == 0:
                    p.wait("gpsimd", "PE", p.cnt["PE"])
                TS("gpsimd", wqbbf[sl][:, 192 * kc:192 * (kc + 1)],
                   wqbst[:, 0:192], qlnwc[:, kc:kc + 1], ALU.mult, inc="POq")
                hw_state["POq"] = p.cnt["POq"]
            for kc in range(4):
                p.wait("sync", "POq", hw_state["POq"])
                p.dma("sync", wqbst[:, 0:256],
                      ap["wkvb"][128 * kc:128 * (kc + 1), 256 * h:256 * (h + 1)],
                      inc="WQ")
                p.wait("gpsimd", "WQ")
                TS("gpsimd", wkvbbf[sl][:, 256 * kc:256 * (kc + 1)],
                   wqbst[:, 0:256], kvlnwc[:, kc:kc + 1], ALU.mult, inc="POq")
                hw_state["POq"] = p.cnt["POq"]
            head_ready[h] = p.cnt["POq"]

        p.wait("tensor", "VE")
        p.wait("tensor", "SE")
        load_head_weights(0)
        pb_se = 0
        pt_se = 0
        out_ve = [0, 0]
        vev = [0, 0]

        for h in range(NH):
            sl = h % 2
            wbf = wqbbf[sl]
            kvbf = wkvbbf[sl]
            p.wait("tensor", "POq", head_ready[h])

            p.wait("tensor", "SE", pb_se)
            for kc in range(12):
                for s4 in range(4):
                    MM(PB[:, 512 * s4:512 * (s4 + 1)],
                       wbf[:, 192 * kc:192 * kc + 128],
                       q_a[:, 2048 * kc + 512 * s4:2048 * kc + 512 * (s4 + 1)],
                       kc == 0, kc == 11,
                       inc=("PE" if (kc == 11 and s4 == 3) else None))
            p.wait("scalar", "PE")
            ACT(qnope[:], PB[:], AF.Copy, inc="SE")
            pb_se = p.cnt["SE"]
            p.wait("tensor", "SE", pb_se)
            for kc in range(12):
                for s4 in range(4):
                    MM(PB[0:64, 512 * s4:512 * (s4 + 1)],
                       wbf[:, 192 * kc + 128:192 * (kc + 1)],
                       q_a[:, 2048 * kc + 512 * s4:2048 * kc + 512 * (s4 + 1)],
                       kc == 0, kc == 11,
                       inc=("PE" if (kc == 11 and s4 == 3) else None))
            p.wait("scalar", "PE")
            ACT(qper[:], PB[0:64, :], AF.Copy, inc="SE")
            pb_se = p.cnt["SE"]
            p.wait("tensor", "SE", pb_se)
            for s4 in range(4):
                MM(PB[0:64, 512 * s4:512 * (s4 + 1)], rotm[:],
                   qper[:, 512 * s4:512 * (s4 + 1)], True, True,
                   inc=("PE" if s4 == 3 else None))
            p.wait("vector", "PE")
            TT("vector", scrd[:], PB[0:64, :], cs_pack[64:128, :], ALU.mult)
            TT("vector", qper[:], qper[:], cs_pack[0:64, :], ALU.mult)
            TT("vector", qper[:], qper[:], scrd[:], ALU.add, inc="VE")
            pb_ve = p.cnt["VE"]
            p.wait("tensor", "VE", pb_ve)
            for kc in range(4):
                for s4 in range(4):
                    MM(PB[:, 512 * s4:512 * (s4 + 1)],
                       kvbf[:, 256 * kc:256 * kc + 128],
                       ckv_n[:, 2048 * kc + 512 * s4:2048 * kc + 512 * (s4 + 1)],
                       kc == 0, kc == 3,
                       inc=("PE" if (kc == 3 and s4 == 3) else None))
            p.wait("scalar", "PE")
            ACT(knope[:], PB[:], AF.Copy, inc="SE")
            pb_se = p.cnt["SE"]
            for T in range(NT):
                POx = PO1 if T % 2 == 0 else PO2
                p.wait("tensor", "SE", vev[T % 2])
                p.wait("tensor", "VE", out_ve[T % 2])
                for kc in range(4):
                    MM(POx[:, 0:128],
                       ckv_n[:, 2048 * kc + 128 * T:2048 * kc + 128 * (T + 1)],
                       kvbf[:, 256 * kc + 128:256 * (kc + 1)],
                       kc == 0, kc == 3, inc=("PE" if kc == 3 else None))
                p.wait("scalar", "PE")
                ACT(vtm[:, 128 * T:128 * (T + 1)], POx[:, 0:128], AF.Copy, inc="SE")
                vev[T % 2] = p.cnt["SE"]
            if h + 1 < NH:
                load_head_weights(h + 1)
            for qi in range(NT):
                Eext = 128 * (qi + 1)
                nsl = (Eext + 511) // 512
                p.wait("tensor", "SE", pb_se)
                for s4 in range(nsl):
                    w = min(512, Eext - 512 * s4)
                    MM(PB[:, 512 * s4:512 * s4 + w], qnope[:, 128 * qi:128 * (qi + 1)],
                       knope[:, 512 * s4:512 * s4 + w], True, False)
                for s4 in range(nsl):
                    w = min(512, Eext - 512 * s4)
                    MM(PB[:, 512 * s4:512 * s4 + w], qper[:, 128 * qi:128 * (qi + 1)],
                       kpe_r[:, 512 * s4:512 * s4 + w], False, True,
                       inc=("PE" if s4 == nsl - 1 else None))
                p.wait("vector", "PE")
                TT("vector", PB[:, 128 * qi:128 * (qi + 1)],
                   PB[:, 128 * qi:128 * (qi + 1)], maskt[:], ALU.add)
                RED("vector", scol("ngm"), PB[:, 0:Eext], ALU.max, neg=True)
                SP()
                TT("vector", scol("ngs"), scol("ngm"), escale[:, qi:qi + 1],
                   ALU.mult)
                VEM()
                p.wait("scalar", "VE")
                ACT(psb[:, 0:Eext], PB[:, 0:Eext], AF.Exp,
                    bias=scol("ngs"), scale=escale[:, qi:qi + 1],
                    accum=scol("accraw"))
                ACT(scol("ell"), scol("accraw"), AF.Copy, inc="SE")
                pb_se = p.cnt["SE"]
                p.wait("vector", "SE")
                RECIP(scol("linv"), scol("ell"))
                VEM()
                p.wait("gpsimd", "VE")
                TS("gpsimd", psb[:, 0:Eext], psb[:, 0:Eext], scol("linv"),
                   ALU.mult, inc="PO")
                p.wait("tensor", "PO")
                p.wait("tensor", "SE", pt_se)
                for j in range(qi + 1):
                    PT = PTa if j < 8 else PTb
                    TR(PT[:, 128 * (j % 8):128 * (j % 8) + 128],
                       psb[:, 128 * j:128 * (j + 1)], identb[:],
                       inc=("PE" if j == qi else None))
                p.wait("scalar", "PE")
                ACT(ptsb[:, 0:min(Eext, 1024)], PTa[:, 0:min(Eext, 1024)], AF.Copy,
                    inc=("SE" if qi < 8 else None))
                if qi >= 8:
                    ACT(ptsb[:, 1024:Eext], PTb[:, 0:Eext - 1024], AF.Copy, inc="SE")
                pt_se = p.cnt["SE"]
                POx = PO1 if qi % 2 == 0 else PO2
                p.wait("tensor", "SE", pt_se)
                p.wait("tensor", "VE", out_ve[qi % 2])
                for j in range(qi + 1):
                    MM(POx[:, 0:128], vtm[:, 128 * j:128 * (j + 1)],
                       ptsb[:, 128 * j:128 * (j + 1)], j == 0, j == qi,
                       inc=("PE" if j == qi else None))
                p.wait("vector", "PE")
                CP("vector",
                   attnout[:, 2048 * h + 128 * qi:2048 * h + 128 * (qi + 1)],
                   POx[:, 0:128], inc="VE")
                out_ve[qi % 2] = p.cnt["VE"]
                if qi % 4 == 3:
                    wprep_step(1)
    dump("attnout", attnout[:])
    if stop_at <= 3:
        return _finish(p, nc, dbg_out, out, early=True)

    phase_barrier()
    stk_qa.close(); p.open_stacks.remove(stk_qa)

    with ExitStack() as stE:
        PB = stE.enter_context(nc.psum_tensor("PBE", [128, 2048], F32))
        PTa = stE.enter_context(nc.psum_tensor("PTaE", [128, 1024], BF16))
        PTb = stE.enter_context(nc.psum_tensor("PTbE", [128, 1024], BF16))
        wo_bf = stE.enter_context(nc.sbuf_tensor("wo_bf", [128, 16 * 2048], BF16))
        xstg1 = stE.enter_context(nc.sbuf_tensor("xstgE", [128, 2048], F32))
        xstg = [xstg1, xstg1]
        x2t = stE.enter_context(nc.sbuf_tensor("x2t", [128, 2048], F32))
        segA = stE.enter_context(nc.sbuf_tensor("segAE", [128, 2048], F32))
        segB = stE.enter_context(nc.sbuf_tensor("segBE", [128, 2048], F32))
        segI = stE.enter_context(nc.sbuf_tensor("segIE", [128, 2048], I32))
        i1tm = stE.enter_context(nc.sbuf_tensor("i1tm", [128, 2048], BF16))
        i1st = stE.enter_context(nc.sbuf_tensor("i1st", [128, 2048], BF16))

        p.wait("tensor", "VE")
        p.wait("tensor", "SE")
        p.wait("vector", "WF4", p.cnt.get("WF4", 0))
        wost = segB
        wo_cast = []
        for c in range(16):
            p.wait("sync", "POw", p.cnt.get("POw", 0))
            p.dma("sync", wost[:], ap["wo"][128 * c:128 * (c + 1), :], inc="WOL")
            p.wait("gpsimd", "WOL", 16 * (c + 1))
            CP("gpsimd", wo_bf[:, 2048 * c:2048 * (c + 1)], wost[:], inc="POw")
            wo_cast.append(p.cnt["POw"])
            wprep_step(2)

        xe_guard = [0, 0]
        pb_free = 0
        pt_guard = [0, 0]
        for T in range(NT):
            sl = 0
            semx = "XEA"
            p.wait("sync", "VE", xe_guard[sl])
            p.dma("sync", xstg[sl][:], ap["x"][128 * T:128 * (T + 1), :], inc=semx)
            p.wait("tensor", "VE", pb_free)
            if T == 0:
                p.wait("tensor", "POw", wo_cast[15])
            for c in range(16):
                for s4 in range(4):
                    MM(PB[:, 512 * s4:512 * (s4 + 1)],
                       attnout[:, 2048 * c + 128 * T:2048 * c + 128 * (T + 1)],
                       wo_bf[:, 2048 * c + 512 * s4:2048 * c + 512 * (s4 + 1)],
                       c == 0, c == 15,
                       inc=("PE" if (c == 15 and s4 == 3) else None))
            p.wait("vector", semx)
            p.wait("vector", "PE")
            p.wait("vector", "X2S", p.cnt.get("X2S", 0))
            TT("vector", x2t[:], PB[:], xstg[sl][:], ALU.add, inc="VE")
            pb_free = p.cnt["VE"]
            xe_guard[sl] = p.cnt["VE"]
            p.wait("scalar", "VE")
            p.dma("scalar", x2s[128 * T:128 * (T + 1), :], x2t[:], inc="X2S")
            if DBG("x2"):
                p.dma("scalar", dbg_out["x2"][128 * T:128 * (T + 1), :], x2t[:],
                      inc="ST")
            p.wait("vector", "PO", p.cnt["PO"])
            p.wait("scalar", "PO", p.cnt["PO"])
            lnstats(x2t[:], segA[:])
            TS("vector", segA[:], x2t[:], scol("mu"), ALU.subtract, scol("rv"),
               ALU.mult, inc="VE")
            p.wait("scalar", "VE")
            ACTACC(segB[:], segA[:], AF.Square, scol("q2"))
            p.wait("vector", "SE")
            TS("vector", scol("m2l"), scol("q2"), 1.0 / D, ALU.mult)
            VEM()
            p.wait("scalar", "VE")
            ACT(scol("sdl"), scol("m2l"), AF.Sqrt, bias=col_eps5, inc="SE")
            p.wait("vector", "SE")
            RECIP(scol("r2"), scol("sdl"))
            RED("vector", scol("mx"), segA[:], ALU.max, absval=True)
            SP()
            TT("vector", scol("rm"), scol("r2"), scol("mx"), ALU.mult)
            SP()
            TS("vector", scol("rmc"), scol("rm"), 1e-5, ALU.max)
            SP()
            RECIP(scol("irmc"), scol("rmc"))
            SP()
            TS("vector", osc1c[:, T:T + 1], scol("rmc"), scol("m127_1"), ALU.mult)
            STT("vector", scol("c1"), scol("irmc"), 127.0, scol("r2"),
                ALU.mult, ALU.mult)
            SP()
            TS("vector", segI[:], segA[:], scol("c1"), ALU.mult)
            VEM()
            p.wait("gpsimd", "VE")
            CP("gpsimd", i1tm[:], segI[:], inc="PO")
            p.wait("vector", "PO")
            STT("vector", segB[:], segA[:], scol("c1"), i1tm[:], ALU.mult,
                ALU.subtract)
            TS("vector", segA[:], segB[:], col_h, ALU.is_gt)
            TS("vector", segB[:], segB[:], col_nh, ALU.is_lt)
            TT("vector", segA[:], segA[:], segB[:], ALU.subtract)
            VEM()
            p.wait("gpsimd", "VE")
            TT("gpsimd", i1tm[:], i1tm[:], segA[:], ALU.add, inc="PO")
            p.wait("tensor", "PO")
            for g, PT in ((0, PTa), (1, PTb)):
                p.wait("tensor", "SE", pt_guard[g])
                for c in range(8):
                    cc = 8 * g + c
                    TR(PT[:, 128 * c:128 * (c + 1)], i1tm[:, 128 * cc:128 * (cc + 1)],
                       identb[:], inc=("PE" if c == 7 else None))
                p.wait("scalar", "PE")
                p.wait("scalar", "I1S", p.cnt.get("I1S", 0))
                ACT(i1st[:, 1024 * g:1024 * (g + 1)], PT[:], AF.Copy, inc="SE")
                pt_guard[g] = p.cnt["SE"]
            p.wait("scalar", "SE")
            p.dma("scalar",
                  int1s.rearrange("(c i) t -> i c t", i=128)[:, :, 128 * T:128 * (T + 1)],
                  i1st[:].rearrange("i (c t) -> i c t", t=128), inc="I1S")
            wprep_step(3)
    dump("osc1", osc1c[:])
    if DBG("int1"):
        pass
    if stop_at <= 4:
        return _finish(p, nc, dbg_out, out, early=True)

    while wstate["i"] < len(wtB):
        wprep_step(4)
    phase_barrier()
    stk_w.close(); p.open_stacks.remove(stk_w)
    stk_ao.close(); p.open_stacks.remove(stk_ao)

    with ExitStack() as stF:
        PB = stF.enter_context(nc.psum_tensor("PBF", [128, 2048], F32))
        PTa = stF.enter_context(nc.psum_tensor("PTaF", [128, 1024], BF16))
        PTb = stF.enter_context(nc.psum_tensor("PTbF", [128, 1024], BF16))
        PH = stF.enter_context(nc.psum_tensor("PHF", [128, 1024], F32))
        hbuf = stF.enter_context(nc.sbuf_tensor("hbuf", [128, 64 * QW], BF16))
        int1h = stF.enter_context(nc.sbuf_tensor("int1h", [128, 16 * QW], BF16))
        big = [stF.enter_context(nc.sbuf_tensor(f"bigF{i}", [128, INNER], BF16))
               for i in range(2)]
        htm = stF.enter_context(nc.sbuf_tensor("htmF", [128, INNER], BF16))
        segA = stF.enter_context(nc.sbuf_tensor("segAF", [128, 2048], F32))
        segB = stF.enter_context(nc.sbuf_tensor("segBF", [128, 2048], F32))
        segI = stF.enter_context(nc.sbuf_tensor("segIF", [128, 2048], I32))
        w1sb = [stF.enter_context(nc.sbuf_tensor(f"w1sb{i}", [128, 2048], BF16))
                for i in range(2)]
        s1b = stF.enter_context(nc.sbuf_tensor("s1b", [128, QW], F32))
        out2 = stF.enter_context(nc.sbuf_tensor("out2F", [128, 16 * QW], BF16))
        x3t = stF.enter_context(nc.sbuf_tensor("x3t", [128, 2048], F32))
        b2bc = stF.enter_context(nc.sbuf_tensor("b2bc", [128, 2048], F32))
        rrow = stF.enter_context(nc.sbuf_tensor("rrowF", [1, QW], F32))

        p.dma("sync", segA[0:1, 0:2048], ap["b2row"], inc="GL")
        w1q_done = p.cnt["WSS"]
        i1_done = p.cnt["I1S"]
        x2_done = p.cnt["X2S"]
        p.wait("tensor", "GL")
        p.wait("tensor", "VE")
        p.wait("tensor", "SE")
        for s4 in range(4):
            MM(PB[:, 512 * s4:512 * (s4 + 1)], onesrow[:],
               segA[0:1, 512 * s4:512 * (s4 + 1)], True, True,
               inc=("PE" if s4 == 3 else None))
        p.wait("vector", "PE")
        CP("vector", b2bc[:], PB[:], inc="VE")
        p.wait("tensor", "VE")
        TR(PB[0:16, 0:128], osc1c[:], ident[:], inc="PE")
        p.wait("vector", "PE")
        CP("vector", mstage[:], PB[0:16, 0:128], inc="VE")
        p.wait("scalar", "VE")
        p.dma("scalar", rowscr, mstage[:], inc="RS")

        def build_bcast(dst_bf, src_dram):
            for blk in range(4):
                p.wait("sync", "SE", p.cnt["SE"])
                p.wait("sync", "VE", p.cnt["VE"])
                p.dma("sync", segA[0:1, 0:2048],
                      src_dram[0:1, 2048 * blk:2048 * (blk + 1)], inc="GBL")
                p.wait("tensor", "GBL")
                p.wait("tensor", "SE", p.cnt["SE"])
                p.wait("tensor", "VE", p.cnt["VE"])
                for s4 in range(4):
                    MM(PB[:, 512 * s4:512 * (s4 + 1)], onesrow[:],
                       segA[0:1, 512 * s4:512 * (s4 + 1)], True, True,
                       inc=("PE" if s4 == 3 else None))
                p.wait("scalar", "PE")
                ACT(dst_bf[:, 2048 * blk:2048 * (blk + 1)], PB[:], AF.Copy, inc="SE")
            p.wait("tensor", "SE")

        mm1_guard = [0, 0]
        silu_guard = 0
        pt_guard = [0, 0]
        f3_guard = [0, 0]
        merge_done = [0]
        for q in range(NQ):
            t0 = QW * q
            p.wait("sync", "RS")
            p.dma("sync", rrow[:],
                  rowscr.rearrange("a b -> (a b)").rearrange("(o n) -> o n", o=1)[
                      0:1, t0:t0 + QW], inc="RR")
            p.wait("tensor", "RR")
            p.wait("tensor", "VE", p.cnt["VE"])
            MM(PB[:, 0:512], onesrow[:], rrow[0:1, 0:512], True, True, inc="PE")
            p.wait("vector", "PE")
            CP("vector", s1b[:], PB[:, 0:512], inc="VE")
            if q == 0:
                p.wait("sync", "I1S", i1_done)
                p.wait("sync", "WSS", w1q_done)
            p.wait("sync", "VE", merge_done[0])
            for kc in range(16):
                p.dma("sync", int1h[:, QW * kc:QW * (kc + 1)],
                      int1s[128 * kc:128 * (kc + 1), t0:t0 + QW], inc="I1R")
            for oc in range(64):
                sl = oc % 2
                p.wait("sync", "PE", mm1_guard[sl])
                p.dma("sync", w1sb[sl][:].rearrange("p (a j) -> p a j", j=128),
                      w1q[oc].rearrange("(a p) j -> p a j", p=128),
                      inc=("W1A" if sl == 0 else "W1B"))
                p.wait("tensor", "W1A" if sl == 0 else "W1B")
                if oc == 0:
                    p.wait("tensor", "I1R", 16 * 16 * (q + 1))
                    p.wait("tensor", "VE", p.cnt["VE"])
                p.wait("tensor", "SE", silu_guard)
                ph = PH[:, 512 * sl:512 * (sl + 1)]
                for kc in range(16):
                    MM(ph, w1sb[sl][:, 128 * kc:128 * (kc + 1)],
                       int1h[:, QW * kc:QW * (kc + 1)], kc == 0, kc == 15,
                       inc=("PE" if kc == 15 else None))
                mm1_guard[sl] = p.cnt["PE"]
                p.wait("vector", "PE")
                TT("vector", segA[:, 0:512], ph, s1b[:], ALU.mult, inc="VE")
                p.wait("scalar", "VE")
                ACT(segB[:, 0:512], segA[:, 0:512], AF.Silu, bias=b1c[:, oc:oc + 1])
                ACT(hbuf[:, QW * oc:QW * (oc + 1)], segB[:, 0:512], AF.Gelu,
                    inc="SE")
                silu_guard = p.cnt["SE"]
            if q == 0:
                dump("h0", hbuf[:])
            p.wait("tensor", "W2A", p.cnt.get("W2A", 0))
            p.wait("tensor", "W2B", p.cnt.get("W2B", 0))
            p.wait("tensor", "PO", p.cnt["PO"])
            build_bcast(big[0], ap["gln"])
            build_bcast(big[1], ap["bln"])
            for Ti in range(QW // 128):
                p.wait("scalar", "PO", p.cnt["PO"])
                p.wait("vector", "PO", p.cnt["PO"])
                for c in range(64):
                    g = (c // 8) % 2
                    PT = PTa if g == 0 else PTb
                    if c % 8 == 0:
                        p.wait("tensor", "SE", pt_guard[g])
                    TR(PT[:, 128 * (c % 8):128 * (c % 8) + 128],
                       hbuf[:, QW * c + 128 * Ti:QW * c + 128 * (Ti + 1)],
                       identb[:], inc=("PE" if c % 8 == 7 else None))
                    if c % 8 == 7:
                        p.wait("scalar", "PE")
                        ACT(htm[:, 1024 * (c // 8):1024 * (c // 8 + 1)], PT[:],
                            AF.Copy, inc="SE")
                        pt_guard[g] = p.cnt["SE"]
                p.wait("vector", "SE")
                for sgi in range(4):
                    sg = htm[:, 2048 * sgi:2048 * (sgi + 1)]
                    RED("vector", scol("p1"), sg, ALU.add, inc="VE")
                    p.wait("scalar", "VE")
                    ACTACC(segA[:], sg, AF.Square, scol("p2"))
                    p.wait("vector", "SE")
                    SP()
                    if sgi == 0:
                        CP("vector", scol("sx"), scol("p1"))
                        CP("vector", scol("sq"), scol("p2"))
                    else:
                        TT("vector", scol("sx"), scol("sx"), scol("p1"), ALU.add)
                        TT("vector", scol("sq"), scol("sq"), scol("p2"), ALU.add)
                    SP()
                TS("vector", scol("mu"), scol("sx"), 1.0 / INNER, ALU.mult)
                TS("vector", scol("m2"), scol("sq"), 1.0 / INNER, ALU.mult)
                SP()
                STT("vector", scol("v2"), scol("mu"), scol("mu"), scol("m2"),
                    ALU.mult, ALU.subtract)
                VEM()
                p.wait("scalar", "VE")
                ACT(scol("v1"), scol("v2"), AF.Sqrt, bias=col_eps5, scale=-1.0,
                    inc="SE")
                p.wait("vector", "SE")
                RECIP(scol("rv"), scol("v1"))
                VEM()
                for sgi in range(4):
                    sg = htm[:, 2048 * sgi:2048 * (sgi + 1)]
                    TS("vector", segA[:], sg, scol("mu"), ALU.subtract, scol("rv"),
                       ALU.mult)
                    TT("vector", segA[:], segA[:], big[0][:, 2048 * sgi:2048 * (sgi + 1)],
                       ALU.mult)
                    TT("vector", segA[:], segA[:], big[1][:, 2048 * sgi:2048 * (sgi + 1)],
                       ALU.add, inc="VE")
                    RED("vector", scol("p1"), segA[:], ALU.max, absval=True)
                    p.wait("scalar", "VE")
                    ACTACC(segB[:], segA[:], AF.Square, scol("p2"))
                    p.wait("vector", "SE")
                    SP()
                    if sgi == 0:
                        CP("vector", scol("zq"), scol("p2"))
                        CP("vector", scol("zm"), scol("p1"))
                    else:
                        TT("vector", scol("zq"), scol("zq"), scol("p2"), ALU.add)
                        TT("vector", scol("zm"), scol("zm"), scol("p1"), ALU.max)
                    SP()
                TS("vector", scol("m2z"), scol("zq"), 1.0 / INNER, ALU.mult)
                VEM()
                p.wait("scalar", "VE")
                ACT(scol("sdz"), scol("m2z"), AF.Sqrt, bias=col_eps5, inc="SE")
                p.wait("vector", "SE")
                RECIP(scol("rho"), scol("sdz"))
                SP()
                TT("vector", scol("rm"), scol("rho"), scol("zm"), ALU.mult)
                SP()
                TS("vector", scol("rmc"), scol("rm"), 1e-5, ALU.max)
                SP()
                RECIP(scol("irmc"), scol("rmc"))
                SP()
                TS("vector", osc2c[:, 4 * q + Ti:4 * q + Ti + 1], scol("rmc"),
                   scol("m127_2"), ALU.mult)
                STT("vector", scol("c2"), scol("irmc"), 127.0, scol("rho"),
                    ALU.mult, ALU.mult)
                SP()
                for sgi in range(4):
                    sg = htm[:, 2048 * sgi:2048 * (sgi + 1)]
                    p.wait("vector", "PO", p.cnt["PO"])
                    TS("vector", segA[:], sg, scol("mu"), ALU.subtract, scol("rv"),
                       ALU.mult)
                    TT("vector", segA[:], segA[:], big[0][:, 2048 * sgi:2048 * (sgi + 1)],
                       ALU.mult)
                    TT("vector", segA[:], segA[:], big[1][:, 2048 * sgi:2048 * (sgi + 1)],
                       ALU.add)
                    TS("vector", segI[:], segA[:], scol("c2"), ALU.mult)
                    VEM()
                    p.wait("gpsimd", "VE")
                    CP("gpsimd", sg, segI[:], inc="PO")
                    p.wait("vector", "PO")
                    STT("vector", segB[:], segA[:], scol("c2"), sg, ALU.mult,
                        ALU.subtract)
                    TS("vector", segA[:], segB[:], col_h, ALU.is_gt)
                    TS("vector", segB[:], segB[:], col_nh, ALU.is_lt)
                    TT("vector", segA[:], segA[:], segB[:], ALU.subtract)
                    VEM()
                    p.wait("gpsimd", "VE")
                    TT("gpsimd", sg, sg, segA[:], ALU.add, inc="PO")
                p.wait("tensor", "PO")
                for c in range(64):
                    g = (c // 8) % 2
                    PT = PTa if g == 0 else PTb
                    if c % 8 == 0:
                        p.wait("tensor", "SE", pt_guard[g])
                    TR(PT[:, 128 * (c % 8):128 * (c % 8) + 128],
                       htm[:, 128 * c:128 * (c + 1)], identb[:],
                       inc=("PE" if c % 8 == 7 else None))
                    if c % 8 == 7:
                        p.wait("scalar", "PE")
                        c0 = c - 7
                        dst = hbuf[:].rearrange("i (a t) -> i a t", t=QW)[
                            :, c0:c0 + 8, 128 * Ti:128 * (Ti + 1)]
                        ACT(dst, PT[:].rearrange("i (a t) -> i a t", t=128),
                            AF.Copy, inc="SE")
                        pt_guard[g] = p.cnt["SE"]
            if q == 0:
                dump("int2_0", hbuf[:])
                dump("osc2", osc2c[:])
            f3v_guard = 0
            f2_done = p.cnt["VE"]
            for m in range(16):
                sl = m % 2
                semw = "W2A" if sl == 0 else "W2B"
                p.wait("sync", "PE", f3_guard[sl])
                if m < 2:
                    p.wait("sync", "VE", f2_done)
                for part in range(4):
                    p.dma("sync",
                          big[sl][:, 2048 * part:2048 * (part + 1)].rearrange(
                              "p (a j) -> p a j", j=128),
                          w2q[m][2048 * part:2048 * (part + 1), :].rearrange(
                              "(a p) j -> p a j", p=128), inc=semw)
                p.wait("tensor", semw)
                p.wait("tensor", "VE", p.cnt["VE"] if m < 2 else f3v_guard)
                for kc in range(64):
                    MM(PB[:, 512 * sl:512 * (sl + 1)],
                       big[sl][:, 128 * kc:128 * (kc + 1)],
                       hbuf[:].rearrange("i (a t) -> i a t", t=QW)[:, kc, :],
                       kc == 0, kc == 63, inc=("PE" if kc == 63 else None))
                f3_guard[sl] = p.cnt["PE"]
                p.wait("vector", "PE")
                CP("vector", out2[:, QW * m:QW * (m + 1)], PB[:, 512 * sl:512 * (sl + 1)],
                   inc="VE")
                f3v_guard = p.cnt["VE"]
            p.wait("sync", "X2S", x2_done)
            for Ti in range(QW // 128):
                trow = t0 + 128 * Ti
                p.wait("sync", "STO", p.cnt.get("STO", 0))
                p.dma("sync", x3t[:], x2s[trow:trow + 128, :], inc="XM")
                p.wait("vector", "XM")
                TT("vector", x3t[:], x3t[:], b2bc[:], ALU.add, inc="VE")
                for mg in range(2):
                    PT = PTa if mg == 0 else PTb
                    p.wait("tensor", "VE", p.cnt["VE"])
                    for mm_ in range(8):
                        m = 8 * mg + mm_
                        TR(PT[:, 128 * mm_:128 * (mm_ + 1)],
                           out2[:, QW * m + 128 * Ti:QW * m + 128 * (Ti + 1)],
                           identb[:], inc=("PE" if mm_ == 7 else None))
                    p.wait("vector", "PE")
                    for mm_ in range(8):
                        m = 8 * mg + mm_
                        STT("vector", x3t[:, 128 * m:128 * (m + 1)],
                            PT[:, 128 * mm_:128 * (mm_ + 1)],
                            osc2c[:, 4 * q + Ti:4 * q + Ti + 1],
                            x3t[:, 128 * m:128 * (m + 1)], ALU.mult, ALU.add,
                            inc=("VE" if mm_ == 7 else None))
                p.wait("scalar", "VE")
                p.dma("scalar", out[trow:trow + 128, :], x3t[:], inc="STO")
                merge_done[0] = p.cnt["VE"]

    return _finish(p, nc, dbg_out, out, early=False)


def _finish(p, nc, dbg_out, out, early):
    if "STO" in p.cnt:
        p.wait("gpsimd", "STO")
    if "ST" in p.cnt:
        p.wait("gpsimd", "ST")
    blk = p.ctx.enter_context(nc.Block())
    p.finalize(blk)
    for stk in reversed(p.open_stacks):
        stk.close()
    p.ctx.close()
    return nc, dbg_out


def make_inmap(inputs, core):
    hs = np.ascontiguousarray(np.asarray(inputs["L0_attn_hidden_states"])[core])
    w1t = np.ascontiguousarray(np.asarray(inputs["L0_ffn_w1"]).T)
    w2t = np.ascontiguousarray(np.asarray(inputs["L0_ffn_w2"]).T)
    m = dict(
        x=hs.astype(np.float32),
        wqa=np.ascontiguousarray(np.asarray(inputs["L0_attn_Wqa"], np.float32)),
        wqb=np.ascontiguousarray(np.asarray(inputs["L0_attn_Wqb"], np.float32)),
        wkva=np.ascontiguousarray(np.asarray(inputs["L0_attn_Wkva"], np.float32)),
        wkvb=np.ascontiguousarray(np.asarray(inputs["L0_attn_Wkvb"], np.float32)),
        wo=np.ascontiguousarray(np.asarray(inputs["L0_attn_Wo"], np.float32)),
        qlnw=np.ascontiguousarray(np.asarray(inputs["L0_attn_q_a_ln_w"],
                                             np.float32).reshape(12, 128)),
        kvlnw=np.ascontiguousarray(np.asarray(inputs["L0_attn_kv_a_ln_w"],
                                              np.float32).reshape(4, 128)),
        w1t=w1t.astype(np.float32), w2t=w2t.astype(np.float32),
        b1=np.ascontiguousarray(np.asarray(inputs["L0_ffn_b1"],
                                           np.float32).reshape(64, 128)),
        b2=np.ascontiguousarray(np.asarray(inputs["L0_ffn_b2"],
                                           np.float32).reshape(16, 128)),
        b2row=np.ascontiguousarray(np.asarray(inputs["L0_ffn_b2"],
                                              np.float32).reshape(1, 2048)),
        gln=np.ascontiguousarray(np.asarray(inputs["L0_ffn_ln_g"],
                                            np.float32).reshape(1, INNER)),
        bln=np.ascontiguousarray(np.asarray(inputs["L0_ffn_ln_b"],
                                            np.float32).reshape(1, INNER)),
    )
    m.update(host_consts())
    return m


def kernel(**inputs):
    nc, _ = build_program(dbg=(), upto="F")
    in_maps = [make_inmap(inputs, c) for c in range(8)]
    res = run_bass_kernel_spmd(nc, in_maps, core_ids=list(range(8)))
    outs = [np.asarray(r["out"], dtype=np.float32) for r in res.results]
    return np.stack(outs, axis=0)


# version 03
# speedup vs baseline: 1.9407x; 1.9407x over previous
import math
from contextlib import ExitStack

import numpy as np
import ml_dtypes

import concourse.bass as bass
import concourse.mybir as mybir
from concourse.bass_utils import run_bass_kernel_spmd

F32 = mybir.dt.float32
BF16 = mybir.dt.bfloat16
I32 = mybir.dt.int32
AF = mybir.ActivationFunctionType
ALU = mybir.AluOpType
AX = mybir.AxisListType

S = 2048
D = 2048
NT = 16
NH = 16
QLORA = 1536
SC192 = 1.0 / math.sqrt(192.0)
INNER = 8192
QW = 512
NQ = S // QW

ENGS = ("sync", "tensor", "vector", "scalar", "gpsimd")


class Prog:
    def __init__(self, nc):
        self.nc = nc
        self.recs = {e: [] for e in ENGS}
        self.sem_handles = {}
        self.cnt = {}
        self.ctx = ExitStack()

    def sem(self, name):
        if name not in self.sem_handles:
            self.sem_handles[name] = self.ctx.enter_context(self.nc.semaphore(name))
            self.cnt[name] = 0
        return self.sem_handles[name]

    def wait(self, eng, semname, target=None):
        h = self.sem(semname)
        if target is None:
            target = self.cnt[semname]
        assert target <= self.cnt[semname], (semname, target, self.cnt[semname])
        if target > 0:
            self.recs[eng].append(lambda e, hh=h, t=target: e.wait_ge(hh, t))

    def emit(self, eng, build, inc=None, n=1):
        if inc is not None:
            h = self.sem(inc)
            self.cnt[inc] += n
            self.recs[eng].append(lambda e, b=build, hh=h, nn=n: b(e).then_inc(hh, nn))
        else:
            self.recs[eng].append(build)

    def dma(self, eng, out, in_, inc, **kw):
        self.emit(eng, lambda e, o=out, i=in_, k=dict(kw): e.dma_start(out=o, in_=i, **k),
                  inc=inc, n=16)

    def finalize(self, block):
        for eng in ENGS:
            recs = self.recs[eng]
            getattr(block, eng)(lambda e, r=recs: [f(e) for f in r])


def host_consts():
    ident = np.eye(128, dtype=np.float32)
    mask = np.where(np.arange(128)[None, :] > np.arange(128)[:, None],
                    np.float32(-1e30), np.float32(0.0)).astype(np.float32)
    P = np.zeros((64, 64), dtype=np.float32)
    P[:32, 32:] = -np.eye(32, dtype=np.float32)
    P[32:, :32] = np.eye(32, dtype=np.float32)
    cols = np.zeros((128, 8), dtype=np.float32)
    cols[:, 0] = 1e-5
    cols[:, 1] = 1e-6
    cols[:, 2] = 0.5
    cols[:, 3] = -0.5
    cols[:, 4] = -math.log(2.0 * math.pi)
    cols[:, 5] = 0.25
    iota = np.arange(2048, dtype=np.float32)[None, :]
    return dict(
        c_ident=ident,
        c_identb=ident.astype(ml_dtypes.bfloat16),
        c_mask=mask,
        c_rot=P.T.astype(ml_dtypes.bfloat16),
        c_cols=cols,
        c_iota=iota,
        c_onescol=np.ones((128, 1), dtype=np.float32),
        c_onescolb=np.ones((128, 1), dtype=ml_dtypes.bfloat16),
        c_onesrow=np.ones((1, 128), dtype=np.float32),
    )


def build_program(dbg=(), upto="F"):
    PHASES = ["A", "B", "C", "D", "E", "F"]
    stop_at = PHASES.index(upto)

    nc = bass.Bass("TRN2", target_bir_lowering=False, detect_race_conditions=False)
    p = Prog(nc)
    p.open_stacks = []
    EP = p.ctx.enter_context

    ap = {}
    def din(name, shape, dt=F32):
        ap[name] = nc.dram_tensor(name, shape, dt, kind="ExternalInput").ap()
    din("x", [S, D])
    din("wqa", [D, QLORA]); din("wqb", [QLORA, NH * 192]); din("wkva", [D, 576])
    din("wkvb", [512, NH * 256]); din("wo", [D, D])
    din("qlnw", [12, 128]); din("kvlnw", [4, 128])
    din("w1t", [D, INNER]); din("w2t", [INNER, D])
    din("b1", [64, 128]); din("b2", [16, 128]); din("b2row", [1, D])
    din("gln", [1, INNER]); din("bln", [1, INNER])
    din("c_ident", [128, 128]); din("c_identb", [128, 128], BF16)
    din("c_mask", [128, 128]); din("c_rot", [64, 64], BF16)
    din("c_cols", [128, 8]); din("c_iota", [1, 2048])
    din("c_onescol", [128, 1]); din("c_onescolb", [128, 1], BF16)
    din("c_onesrow", [1, 128])

    out = nc.dram_tensor("out", [S, D], F32, kind="ExternalOutput").ap()

    dbg_out = {}
    def dbg_decl(name, shape, dt=F32):
        if name in dbg:
            dbg_out[name] = nc.dram_tensor("dbg_" + name, shape, dt,
                                           kind="ExternalOutput").ap()
    for nm, sh, dt in [
        ("lnx", [128, 16 * 2048], BF16), ("q_a", [128, 12 * 2048], BF16),
        ("ckv_n", [128, 4 * 2048], BF16), ("kpe_r", [64, 2048], BF16),
        ("cs_pack", [128, 2048], BF16), ("escale", [128, 16], F32),
        ("attnout", [128, 16 * 2048], BF16), ("x2", [S, D], F32),
        ("int1", [D, S], BF16), ("osc1", [128, 16], F32),
        ("tern1", [128, 2048], BF16), ("wm", [128, 2], F32),
        ("h0", [128, 64 * QW], BF16), ("int2_0", [128, 64 * QW], BF16),
        ("osc2", [128, 16], F32),
    ]:
        dbg_decl(nm, sh, dt)
    def DBG(name):
        return name in dbg_out

    x2s = nc.dram_tensor("x2s", [S, D], F32, kind="Internal").ap()
    int1s = nc.dram_tensor("int1s", [D, S], BF16, kind="Internal").ap()
    w1q = nc.dram_tensor("w1q", [64, D, 128], BF16, kind="Internal").ap()
    w2q = nc.dram_tensor("w2q", [16, INNER, 128], BF16, kind="Internal").ap()
    rowscr = nc.dram_tensor("rowscr", [16, 128], F32, kind="Internal").ap()

    ident = EP(nc.sbuf_tensor("ident", [128, 128], F32))
    identb = EP(nc.sbuf_tensor("identb", [128, 128], BF16))
    maskt = EP(nc.sbuf_tensor("maskt", [128, 128], F32))
    rotm = EP(nc.sbuf_tensor("rotm", [64, 64], BF16))
    ccols = EP(nc.sbuf_tensor("ccols", [128, 8], F32))
    onescol = EP(nc.sbuf_tensor("onescol", [128, 1], F32))
    onescolb = EP(nc.sbuf_tensor("onescolb", [128, 1], BF16))
    onesrow = EP(nc.sbuf_tensor("onesrow", [1, 128], F32))
    qlnwc = EP(nc.sbuf_tensor("qlnwc", [128, 12], F32))
    kvlnwc = EP(nc.sbuf_tensor("kvlnwc", [128, 4], F32))
    b1c = EP(nc.sbuf_tensor("b1c", [128, 64], F32))
    b2c = EP(nc.sbuf_tensor("b2c", [128, 16], F32))
    stats = EP(nc.sbuf_tensor("stats", [128, 48], F32))
    _statmap = {}
    def scol(name, n=1):
        if name not in _statmap:
            base = sum(v[1] for v in _statmap.values())
            assert base + n <= 48, (name, base)
            _statmap[name] = (base, n)
        b, nn = _statmap[name]
        return stats[:, b:b + nn]
    wder = EP(nc.sbuf_tensor("wder", [2, 4], F32))
    wch = [EP(nc.sbuf_tensor(f"wch{i}", [128, 64], F32)) for i in range(2)]
    def scol2(name):
        if name not in _statmap:
            base = sum(v[1] for v in _statmap.values())
            assert base + 1 <= 48
            _statmap[name] = (base, 1)
        b, _n = _statmap[name]
        return stats[0:2, b:b + 1]
    for _nm in ("wtmp", "w1s", "w2s"):
        scol(_nm)
    cs_pack = EP(nc.sbuf_tensor("cs_pack", [128, 2048], BF16))
    kpe_r = EP(nc.sbuf_tensor("kpe_r", [64, 2048], BF16))
    escale = EP(nc.sbuf_tensor("escale", [128, 16], F32))
    osc1c = EP(nc.sbuf_tensor("osc1c", [128, 16], F32))
    osc2c = EP(nc.sbuf_tensor("osc2c", [128, 16], F32))
    mstage = EP(nc.sbuf_tensor("mstage", [16, 128], F32))

    col_eps5 = ccols[:, 0:1]
    col_eps6 = ccols[:, 1:2]
    col_h = ccols[:, 2:3]
    col_nh = ccols[:, 3:4]
    col_l2pi = ccols[:, 4:5]
    col_q = ccols[:, 5:6]

    def MM(dst, lhsT, rhs, start, stop, inc=None):
        p.emit("tensor", lambda e, d=dst, l=lhsT, r=rhs, s1=start, s2=stop:
               e.matmul(out=d, lhsT=l, rhs=r, start=s1, stop=s2), inc=inc)

    def TR(dst, src, idn, inc=None):
        p.emit("tensor", lambda e, d=dst, s=src, i=idn:
               e.transpose(out=d, in_=s, identity=i), inc=inc)

    def ACT(dst, src, func, bias=0.0, scale=1.0, accum=None, inc=None):
        p.emit("scalar", lambda e, d=dst, s=src, f=func, b=bias, sc=scale, a=accum:
               e.activation(out=d, in_=s, func=f, bias=b, scale=sc, accum_out=a),
               inc=inc)

    def TT(eng, dst, a, b, op, inc=None):
        p.emit(eng, lambda e, d=dst, x=a, y=b, o=op:
               e.tensor_tensor(out=d, in0=x, in1=y, op=o), inc=inc)

    def TS(eng, dst, a, s1, op1, s2=None, op2=None, inc=None, accum=None):
        if s2 is None:
            p.emit(eng, lambda e, d=dst, x=a, u=s1, o1=op1, ac=accum:
                   e.tensor_scalar(out=d, in0=x, scalar1=u, scalar2=None, op0=o1,
                                   accum_out=ac), inc=inc)
        else:
            p.emit(eng, lambda e, d=dst, x=a, u=s1, o1=op1, v=s2, o2=op2, ac=accum:
                   e.tensor_scalar(out=d, in0=x, scalar1=u, scalar2=v, op0=o1,
                                   op1=o2, accum_out=ac), inc=inc)

    def STT(eng, dst, in0, scalar, in1, op0, op1, inc=None):
        p.emit(eng, lambda e, d=dst, a=in0, s=scalar, b=in1, o0=op0, o1=op1:
               e.scalar_tensor_tensor(out=d, in0=a, scalar=s, in1=b, op0=o0, op1=o1),
               inc=inc)

    def CP(eng, dst, src, inc=None):
        p.emit(eng, lambda e, d=dst, s=src: e.tensor_copy(out=d, in_=s), inc=inc)

    def RED(eng, dst, src, op, absval=False, neg=False, inc=None):
        p.emit(eng, lambda e, d=dst, s=src, o=op, a=absval, n=neg:
               e.tensor_reduce(out=d, in_=s, axis=AX.X, op=o,
                               apply_absolute_value=a, negate=n), inc=inc)

    def RECIP(dst, src, inc=None):
        p.emit("vector", lambda e, d=dst, s=src: e.reciprocal(out=d, in_=s), inc=inc)

    def NOP(eng, inc):
        p.emit(eng, lambda e: e.nop(), inc=inc)

    def dump(name, src):
        if not DBG(name):
            return
        p.wait("scalar", "VE"); p.wait("scalar", "SE"); p.wait("scalar", "PO")
        p.wait("scalar", "PE")
        p.dma("scalar", dbg_out[name], src, inc="ST")
        for eng in ("vector", "gpsimd", "tensor", "scalar", "sync"):
            p.wait(eng, "ST")

    def SP(n=2):
        for _ in range(n):
            p.emit("vector", lambda e, d=scol("junkV"), s=ccols[:, 7:8]:
                   e.tensor_copy(out=d, in_=s))

    def VEM():
        SP(1)
        p.emit("vector", lambda e, d=scol("junkV"), s=ccols[:, 7:8]:
               e.tensor_copy(out=d, in_=s), inc="VE")

    def phase_barrier():
        p.emit("vector", lambda e, d=scol("junkV"), s=ccols[:, 7:8]:
               e.tensor_copy(out=d, in_=s), inc="BAR")
        p.emit("gpsimd", lambda e, d=scol("junkP"), s=ccols[:, 6:7]:
               e.tensor_copy(out=d, in_=s), inc="BAR")
        p.emit("scalar", lambda e, d=scol("junkS"), s=ccols[:, 7:8]:
               e.activation(out=d, in_=s, func=AF.Copy), inc="BAR")
        p.emit("tensor", lambda e: e.nop(), inc="BAR")
        for _e in ENGS:
            p.wait(_e, "BAR")

    def ACTACC(dump_dst, src, func, acc_final, bias=0.0, scale=1.0):
        ACT(dump_dst, src, func, bias=bias, scale=scale, accum=scol("accraw"))
        ACT(acc_final, scol("accraw"), AF.Copy, inc="SE")

    def lnstats(xt, scra, width=D):
        RED("vector", scol("sx"), xt, ALU.add, inc="VE")
        p.wait("scalar", "VE")
        ACTACC(scra, xt, AF.Square, scol("sq"))
        p.wait("vector", "SE")
        TS("vector", scol("mu"), scol("sx"), 1.0 / width, ALU.mult)
        TS("vector", scol("m2"), scol("sq"), 1.0 / width, ALU.mult)
        SP()
        STT("vector", scol("v2"), scol("mu"), scol("mu"), scol("m2"),
            ALU.mult, ALU.subtract)
        VEM()
        p.wait("scalar", "VE")
        ACT(scol("v1"), scol("v2"), AF.Sqrt, bias=col_eps5, scale=-1.0, inc="SE")
        p.wait("vector", "SE")
        RECIP(scol("rv"), scol("v1"))
        SP()

    wtA = [("1", kc, sc_) for kc in range(16) for sc_ in range(4)] + \
          [("2", kc, 0) for kc in range(64)]
    wtB = list(wtA)
    wstate = {"phase": "A", "i": 0, "po_guard": 0, "ws_guard": [0, 0]}
    wbufs = {}

    def wsrc(t):
        nm, kc, sc_ = t
        if nm == "1":
            return ap["w1t"][128 * kc:128 * (kc + 1), 2048 * sc_:2048 * (sc_ + 1)]
        return ap["w2t"][128 * kc:128 * (kc + 1), :]

    def wprep_step(n=1):
        wsub, wg = wbufs["wsub"], wbufs["wg"]
        for _ in range(n):
            i = wstate["i"]
            if wstate["phase"] == "A":
                if i >= len(wtA):
                    return
                t = wtA[i]
                nm = t[0]
                p.wait("sync", "WVE", wstate["po_guard"])
                p.dma("sync", wsub[:], wsrc(t), inc="WLA")
                p.wait("vector", "WLA", 16 * (i + 1))
                RED("vector", osc1c[:], wsub[:].rearrange("p (a b) -> p a b", b=128),
                    ALU.add, absval=True)
                SP()
                wi = 0 if nm == "1" else 1
                ci = i if nm == "1" else t[1]
                RED("vector", wch[wi][:, ci:ci + 1], osc1c[:], ALU.add)
                p.emit("vector", lambda e, d=scol("junkV"), s=ccols[:, 7:8]:
                       e.tensor_copy(out=d, in_=s), inc="WVE")
                wstate["po_guard"] = p.cnt["WVE"]
                wstate["i"] = i + 1
            else:
                if i >= len(wtB):
                    return
                t = wtB[i]
                nm, kc, sc_ = t
                sl = i % 2
                p.wait("sync", "PO", wstate["po_guard"])
                p.dma("sync", wsub[:], wsrc(t), inc="WLB")
                p.wait("vector", "WLB", 16 * (i + 1))
                if i == 0:
                    p.wait("vector", "WF4", wstate["wf4"])
                p.wait("vector", "WSS", wstate["ws_guard"][sl])
                thr = scol("thr1") if nm == "1" else scol("thr2")
                nthr = scol("nthr1") if nm == "1" else scol("nthr2")
                TS("vector", wg[sl][:], wsub[:], thr, ALU.is_gt)
                TS("vector", wbufs["wg2"][:], wsub[:], nthr, ALU.is_lt)
                TT("vector", wg[sl][:], wg[sl][:], wbufs["wg2"][:], ALU.subtract,
                   inc="PO")
                wstate["po_guard"] = p.cnt["PO"]
                p.wait("scalar", "PO")
                if nm == "1":
                    dst = w1q[16 * sc_:16 * (sc_ + 1), 128 * kc:128 * (kc + 1), :]
                else:
                    dst = w2q[:, 128 * kc:128 * (kc + 1), :]
                p.dma("scalar", dst.rearrange("a i j -> i a j"),
                      wg[sl][:].rearrange("i (a j) -> i a j", j=128), inc="WSS")
                wstate["ws_guard"][sl] = p.cnt["WSS"]
                if DBG("tern1") and nm == "1" and kc == 0 and sc_ == 0:
                    p.dma("scalar", dbg_out["tern1"], wg[sl][:], inc="ST")
                wstate["i"] = i + 1

    def wprep_finalize():
        assert wstate["phase"] == "A" and wstate["i"] >= len(wtA)
        b1_ = _statmap["w1s"][0]
        assert _statmap["w2s"][0] == b1_ + 1
        p.wait("vector", "WVE")
        for wi in range(2):
            RED("vector", osc2c[:, 0:8], wch[wi][:].rearrange("p (a b) -> p a b", b=8),
                ALU.add)
            SP()
            RED("vector", stats[:, b1_ + wi:b1_ + wi + 1], osc2c[:, 0:8], ALU.add)
            SP()
        p.emit("vector", lambda e, d=scol("junkV"), s=ccols[:, 7:8]:
               e.tensor_copy(out=d, in_=s), inc="WVE")
        p.wait("scalar", "WVE")
        p.emit("scalar", lambda e, o=rowscr[8:10, :].rearrange("c i -> i c"),
               i_=stats[:, b1_:b1_ + 2]:
               e.dma_start(out=o, in_=i_, allow_slow_non_contiguous=True),
               inc="WF", n=16)
        p.wait("sync", "WF")
        p.dma("sync", mstage[0:2, :], rowscr[8:10, :], inc="WF2")
        p.wait("vector", "WF2")
        RED("vector", osc2c[0:2, 0:8], mstage[0:2, :].rearrange("p (a b) -> p a b", b=16),
            ALU.add)
        SP()
        RED("vector", scol2("wsum2"), osc2c[0:2, 0:8], ALU.add)
        SP()
        TS("vector", wder[0:2, 0:1], scol2("wsum2"), 1.0 / (2048.0 * 8192.0),
           ALU.mult, 1e-5, ALU.max)
        SP()
        TS("vector", wder[0:2, 1:2], wder[0:2, 0:1], 0.5, ALU.mult)
        TS("vector", wder[0:2, 2:3], wder[0:2, 0:1], 1.0 / 127.0, ALU.mult)
        SP()
        TS("vector", wder[0:2, 3:4], wder[0:2, 1:2], -1.0, ALU.mult)
        p.emit("vector", lambda e, d=scol("junkV"), s=ccols[:, 7:8]:
               e.tensor_copy(out=d, in_=s), inc="VE")
        p.wait("scalar", "VE")
        p.emit("scalar", lambda e, o=rowscr[10:12, 0:4], i_=wder[:]:
               e.dma_start(out=o, in_=i_), inc="WF3", n=16)
        p.wait("sync", "WF3")
        for col, (r_, c_) in (("thr1", (10, 1)), ("thr2", (11, 1)),
                              ("m127_1", (10, 2)), ("m127_2", (11, 2)),
                              ("nthr1", (10, 3)), ("nthr2", (11, 3))):
            p.dma("sync", scol(col),
                  rowscr[r_:r_ + 1, c_:c_ + 1].broadcast_to([128, 1]), inc="WF4")
        wstate["wf4"] = p.cnt["WF4"]
        wstate["po_guard"] = p.cnt.get("PO", 0)
        wstate["phase"] = "B"
        wstate["i"] = 0

    with ExitStack() as st0:
        IP = st0.enter_context(nc.psum_tensor("IP", [128, 2048], F32))
        ist = st0.enter_context(nc.sbuf_tensor("ist", [128, 512], F32))
        irow = st0.enter_context(nc.sbuf_tensor("irow", [1, 2048], F32))
        iw64 = st0.enter_context(nc.sbuf_tensor("iw64", [1, 64], F32))
        iu = st0.enter_context(nc.sbuf_tensor("iu", [64, 2048], F32))
        iup = st0.enter_context(nc.sbuf_tensor("iup", [64, 2048], F32))
        isA = st0.enter_context(nc.sbuf_tensor("isA", [64, 2048], F32))
        isB = st0.enter_context(nc.sbuf_tensor("isB", [64, 2048], F32))
        isI = st0.enter_context(nc.sbuf_tensor("isI", [64, 2048], I32))
        for nm, dst in [("c_ident", ident[:]), ("c_identb", identb[:]),
                        ("c_mask", maskt[:]), ("c_rot", rotm[:]),
                        ("c_cols", ccols[:]), ("c_onescol", onescol[:]),
                        ("c_onescolb", onescolb[:]), ("c_onesrow", onesrow[:]),
                        ("c_iota", irow[:])]:
            p.dma("sync", dst, ap[nm], inc="LD")
        p.dma("sync", ist[0:12, 0:128], ap["qlnw"], inc="LD")
        p.dma("sync", ist[0:4, 128:256], ap["kvlnw"], inc="LD")
        p.dma("sync", ist[0:64, 256:384], ap["b1"], inc="LD")
        p.dma("sync", ist[0:16, 384:512], ap["b2"], inc="LD")
        p.wait("tensor", "LD")
        TR(IP[:, 0:12], ist[0:12, 0:128], ident[0:12, 0:12])
        TR(IP[:, 12:16], ist[0:4, 128:256], ident[0:4, 0:4])
        TR(IP[:, 16:80], ist[0:64, 256:384], ident[0:64, 0:64])
        TR(IP[:, 80:96], ist[0:16, 384:512], ident[0:16, 0:16], inc="PE")
        p.wait("scalar", "PE")
        ACT(qlnwc[:], IP[:, 0:12], AF.Copy)
        ACT(kvlnwc[:], IP[:, 12:16], AF.Copy)
        ACT(b1c[:], IP[:, 16:80], AF.Copy)
        ACT(b2c[:], IP[:, 80:96], AF.Copy, inc="SE")
        ACT(iw64[0:1, 0:32], irow[0:1, 0:32], AF.Exp,
            bias=col_l2pi[0:1, :], scale=-math.log(10000.0) / 32.0)
        ACT(iw64[0:1, 32:64], irow[0:1, 0:32], AF.Exp,
            bias=col_l2pi[0:1, :], scale=-math.log(10000.0) / 32.0, inc="SE")
        p.wait("tensor", "SE")
        for sl in range(4):
            MM(IP[0:64, 512 * sl:512 * (sl + 1)], iw64[:],
               irow[0:1, 512 * sl:512 * (sl + 1)], True, True,
               inc=("PE" if sl == 3 else None))
        p.wait("vector", "PE")
        CP("vector", iu[:], IP[0:64, :])
        TS("vector", iup[:], IP[0:64, :], col_q[0:64, :], ALU.add, inc="VE")
        def rope_half(u_ap, dstrow):
            TS("vector", isI[:], u_ap, 1.0, ALU.mult)
            TS("vector", isB[:], isI[:], 0.0, ALU.add)
            TT("vector", isA[:], u_ap, isB[:], ALU.subtract)
            TS("vector", isB[:], isA[:], col_h[0:64, :], ALU.is_gt)
            TT("vector", isA[:], isA[:], isB[:], ALU.subtract)
            TS("vector", isB[:], isA[:], col_nh[0:64, :], ALU.is_lt)
            TT("vector", isA[:], isA[:], isB[:], ALU.add, inc="VE")
            p.wait("scalar", "VE")
            ACT(dstrow, isA[:], AF.Sin, scale=2.0 * math.pi, inc="SE")
        rope_half(iu[:], cs_pack[64:128, :])
        rope_half(iup[:], cs_pack[0:64, :])
        p.wait("sync", "SE")
        p.wait("gpsimd", "SE")
    dump("cs_pack", cs_pack[:])

    stk_w = ExitStack(); p.open_stacks.append(stk_w)
    wbufs["wsub"] = stk_w.enter_context(nc.sbuf_tensor("wsub", [128, 2048], F32, side="left"))
    wbufs["wg"] = [stk_w.enter_context(nc.sbuf_tensor("wgA", [128, 2048], BF16, side="left")),
                   stk_w.enter_context(nc.sbuf_tensor("wgB", [128, 2048], BF16, side="left"))]
    wbufs["wg2"] = stk_w.enter_context(nc.sbuf_tensor("wg2", [128, 2048], BF16, side="left"))
    stk_qa = ExitStack(); p.open_stacks.append(stk_qa)
    q_a = stk_qa.enter_context(nc.sbuf_tensor("q_a", [128, 12 * 2048], BF16, side="left"))
    ckv_n = stk_qa.enter_context(nc.sbuf_tensor("ckv_n", [128, 4 * 2048], BF16, side="left"))
    stk_lnx = ExitStack(); p.open_stacks.append(stk_lnx)
    lnx = stk_lnx.enter_context(nc.sbuf_tensor("lnx", [128, 16 * 2048], BF16, side="right"))

    with ExitStack() as stA:
        PTa = stA.enter_context(nc.psum_tensor("PTaA", [128, 1024], BF16))
        PTb = stA.enter_context(nc.psum_tensor("PTbA", [128, 1024], BF16))
        xstg = [stA.enter_context(nc.sbuf_tensor(f"xstgA{i}", [128, 2048], F32))
                for i in range(2)]
        lnts = stA.enter_context(nc.sbuf_tensor("lnts", [128, 2048], BF16))
        scra = stA.enter_context(nc.sbuf_tensor("scraA", [128, 2048], F32))

        xa_guard = [0, 0]
        pt_guard = [0, 0]
        tr_done = 0
        for T in range(NT):
            sl = T % 2
            semx = "XLA" if sl == 0 else "XLB"
            p.wait("sync", "VE", xa_guard[sl])
            p.dma("sync", xstg[sl][:], ap["x"][128 * T:128 * (T + 1), :], inc=semx)
            p.wait("vector", semx)
            xt = xstg[sl][:]
            lnstats(xt, scra[:])
            p.wait("vector", "PE", tr_done)
            TS("vector", lnts[:], xt, scol("mu"), ALU.subtract, scol("rv"), ALU.mult,
               inc="VE")
            xa_guard[sl] = p.cnt["VE"]
            p.wait("tensor", "VE")
            for g, PT in ((0, PTa), (1, PTb)):
                p.wait("tensor", "SE", pt_guard[g])
                for c in range(8):
                    cc = 8 * g + c
                    TR(PT[:, 128 * c:128 * (c + 1)], lnts[:, 128 * cc:128 * (cc + 1)],
                       identb[:], inc=("PE" if c == 7 else None))
                p.wait("scalar", "PE")
                dst = lnx[:].rearrange("i (a t) -> i a t", t=2048)[
                    :, 8 * g:8 * (g + 1), 128 * T:128 * (T + 1)]
                ACT(dst, PT[:].rearrange("i (a t) -> i a t", t=128), AF.Copy, inc="SE")
                pt_guard[g] = p.cnt["SE"]
            tr_done = p.cnt["PE"]
            wprep_step(4)
    dump("lnx", lnx[:])
    if stop_at <= 0:
        return _finish(p, nc, dbg_out, out, early=True)

    phase_barrier()
    with ExitStack() as stB:
        PBm = stB.enter_context(nc.psum_tensor("PBmB", [128, 2048], F32))
        Q2 = stB.enter_context(nc.psum_tensor("Q2B", [1, 2048], F32))
        wst = stB.enter_context(nc.sbuf_tensor("wstB", [128, 2048], F32))
        wbf = [stB.enter_context(nc.sbuf_tensor(f"wbfB{i}", [128, 2048], BF16))
               for i in range(2)]
        sqb = wbufs["wg2"]
        rrow = stB.enter_context(nc.sbuf_tensor("rrowB", [1, 2048], F32))

        p.wait("tensor", "VE")
        p.wait("tensor", "SE")
        cast_guard = [0, 0]
        mm_done = [0, 0]
        evac_guard = 0
        for m in range(12):
            sl = m % 2
            p.wait("sync", "POB", cast_guard[1 - sl])
            p.dma("sync", wst[:].rearrange("p (a j) -> p a j", j=128),
                  ap["wqa"][:, 128 * m:128 * (m + 1)].rearrange("(a p) j -> p a j", p=128),
                  inc="WLQ")
            p.wait("scalar", "WLQ", 16 * (m + 1))
            p.wait("scalar", "PE", mm_done[sl])
            ACT(wbf[sl][:], wst[:], AF.Copy, inc="POB")
            cast_guard[sl] = p.cnt["POB"]
            p.wait("tensor", "POB", cast_guard[sl])
            p.wait("tensor", "SE", evac_guard)
            for kc in range(16):
                for s4 in range(4):
                    MM(PBm[:, 512 * s4:512 * (s4 + 1)],
                       wbf[sl][:, 128 * kc:128 * (kc + 1)],
                       lnx[:, 2048 * kc + 512 * s4:2048 * kc + 512 * (s4 + 1)],
                       kc == 0, kc == 15,
                       inc=("PE" if (kc == 15 and s4 == 3) else None))
            mm_done[sl] = p.cnt["PE"]
            p.wait("scalar", "PE")
            ACT(q_a[:, 2048 * m:2048 * (m + 1)], PBm[:], AF.Copy)
            ACT(sqb[:], PBm[:], AF.Square, inc="SE")
            evac_guard = p.cnt["SE"]
            p.wait("tensor", "SE")
            for s4 in range(4):
                MM(Q2[0:1, 512 * s4:512 * (s4 + 1)], onescolb[:],
                   sqb[:, 512 * s4:512 * (s4 + 1)], m == 0, m == 11,
                   inc=("PE" if (m == 11 and s4 == 3) else None))
            wprep_step(3)
        p.wait("vector", "PE")
        TS("vector", rrow[:], Q2[0:1, :], 1.0 / QLORA, ALU.mult)
        VEM()
        p.wait("scalar", "VE")
        ACT(rrow[:], rrow[:], AF.Sqrt, bias=col_eps6[0:1, :], inc="SE")
        p.wait("vector", "SE")
        RECIP(rrow[:], rrow[:])
        VEM()
        p.wait("tensor", "VE")
        for qi in range(16):
            TR(PBm[:, 1024 + qi:1024 + qi + 1], rrow[0:1, 128 * qi:128 * (qi + 1)],
               ident[0:1, 0:1], inc=("PE" if qi == 15 else None))
        p.wait("vector", "PE")
        TS("vector", escale[:], PBm[:, 1024:1040], SC192, ALU.mult)
        VEM()
    dump("q_a", q_a[:])
    dump("escale", escale[:])
    if stop_at <= 1:
        return _finish(p, nc, dbg_out, out, early=True)

    phase_barrier()
    with ExitStack() as stC:
        PBm = stC.enter_context(nc.psum_tensor("PBmC", [128, 2048], F32))
        Q2 = stC.enter_context(nc.psum_tensor("Q2C", [1, 2048], F32))
        wst = stC.enter_context(nc.sbuf_tensor("wstC", [128, 2048], F32))
        wbf = [stC.enter_context(nc.sbuf_tensor(f"wbfC{i}", [128, 2048], BF16))
               for i in range(2)]
        sqb = wbufs["wg2"]
        rrow = stC.enter_context(nc.sbuf_tensor("rrowC", [1, 2048], F32))
        rbc = stC.enter_context(nc.sbuf_tensor("rbcC", [128, 2048], F32))
        kpebf = stC.enter_context(nc.sbuf_tensor("kpebf", [64, 2048], BF16))
        scrc = stC.enter_context(nc.sbuf_tensor("scrC", [64, 2048], F32))

        p.wait("tensor", "VE")
        p.wait("tensor", "SE")
        cast_guard = [0, 0]
        mm_doneC = [0, 0]
        evac_guard = 0
        for m in range(5):
            ncols = 128 if m < 4 else 64
            sl = m % 2
            p.wait("sync", "POC", cast_guard[1 - sl])
            p.dma("sync", wst[:, 0:16 * ncols].rearrange("p (a j) -> p a j", a=16),
                  ap["wkva"][:, 128 * m:128 * m + ncols].rearrange(
                      "(a p) j -> p a j", p=128), inc="WLC")
            p.wait("scalar", "WLC", 16 * (m + 1))
            p.wait("scalar", "PE", mm_doneC[sl])
            ACT(wbf[sl][:, 0:16 * ncols], wst[:, 0:16 * ncols], AF.Copy, inc="POC")
            cast_guard[sl] = p.cnt["POC"]
            p.wait("tensor", "POC", cast_guard[sl])
            p.wait("tensor", "SE", evac_guard)
            for kc in range(16):
                for s4 in range(4):
                    MM(PBm[0:ncols, 512 * s4:512 * (s4 + 1)],
                       wbf[sl][:, ncols * kc:ncols * (kc + 1)],
                       lnx[:, 2048 * kc + 512 * s4:2048 * kc + 512 * (s4 + 1)],
                       kc == 0, kc == 15,
                       inc=("PE" if (kc == 15 and s4 == 3) else None))
            mm_doneC[sl] = p.cnt["PE"]
            p.wait("scalar", "PE")
            if m < 4:
                ACT(ckv_n[:, 2048 * m:2048 * (m + 1)], PBm[:], AF.Copy)
                ACT(sqb[:], PBm[:], AF.Square, inc="SE")
                evac_guard = p.cnt["SE"]
                p.wait("tensor", "SE")
                for s4 in range(4):
                    MM(Q2[0:1, 512 * s4:512 * (s4 + 1)], onescolb[:],
                       sqb[:, 512 * s4:512 * (s4 + 1)], m == 0, m == 3,
                       inc=("PE" if (m == 3 and s4 == 3) else None))
            else:
                ACT(kpebf[:], PBm[0:64, :], AF.Copy, inc="SE")
                evac_guard = p.cnt["SE"]
            wprep_step(3)
        p.wait("vector", "PE")
        TS("vector", rrow[:], Q2[0:1, :], 1.0 / 512.0, ALU.mult)
        VEM()
        p.wait("scalar", "VE")
        ACT(rrow[:], rrow[:], AF.Sqrt, bias=col_eps6[0:1, :], inc="SE")
        p.wait("vector", "SE")
        RECIP(rrow[:], rrow[:])
        VEM()
        p.wait("tensor", "VE")
        p.wait("tensor", "SE")
        for s4 in range(4):
            MM(PBm[:, 512 * s4:512 * (s4 + 1)], onesrow[:],
               rrow[0:1, 512 * s4:512 * (s4 + 1)], True, True,
               inc=("PE" if s4 == 3 else None))
        p.wait("vector", "PE")
        CP("vector", rbc[:], PBm[:], inc="VE")
        for c in range(4):
            TT("vector", ckv_n[:, 2048 * c:2048 * (c + 1)],
               ckv_n[:, 2048 * c:2048 * (c + 1)], rbc[:], ALU.mult,
               inc=("VE" if c == 3 else None))
        p.wait("tensor", "SE")
        p.wait("tensor", "VE")
        for s4 in range(4):
            MM(PBm[0:64, 512 * s4:512 * (s4 + 1)], rotm[:],
               kpebf[:, 512 * s4:512 * (s4 + 1)], True, True,
               inc=("PE" if s4 == 3 else None))
        p.wait("vector", "PE")
        TT("vector", scrc[:], PBm[0:64, :], cs_pack[64:128, :], ALU.mult)
        TT("vector", kpe_r[:], kpebf[:], cs_pack[0:64, :], ALU.mult)
        TT("vector", kpe_r[:], kpe_r[:], scrc[:], ALU.add, inc="VE")
        p.wait("gpsimd", "VE")
        p.wait("sync", "VE")
    dump("ckv_n", ckv_n[:])
    dump("kpe_r", kpe_r[:])

    while wstate["phase"] == "A" and wstate["i"] < len(wtA):
        wprep_step(4)
    wprep_finalize()
    if stop_at <= 2:
        return _finish(p, nc, dbg_out, out, early=True)

    phase_barrier()
    stk_lnx.close(); p.open_stacks.remove(stk_lnx)
    stk_ao = ExitStack(); p.open_stacks.append(stk_ao)
    attnout = stk_ao.enter_context(
        nc.sbuf_tensor("attnout", [128, 16 * 2048], BF16, side="right"))

    with ExitStack() as stD:
        PB = stD.enter_context(nc.psum_tensor("PBD", [128, 2048], F32))
        PTa = stD.enter_context(nc.psum_tensor("PTaD", [128, 1024], BF16))
        PTb = stD.enter_context(nc.psum_tensor("PTbD", [128, 1024], BF16))
        PO1 = stD.enter_context(nc.psum_tensor("PO1D", [128, 512], F32))
        PO2 = stD.enter_context(nc.psum_tensor("PO2D", [128, 512], F32))
        qnope = stD.enter_context(nc.sbuf_tensor("qnope", [128, 2048], BF16))
        qper = stD.enter_context(nc.sbuf_tensor("qper", [64, 2048], BF16))
        knope = stD.enter_context(nc.sbuf_tensor("knope", [128, 2048], BF16))
        vtm = stD.enter_context(nc.sbuf_tensor("vtm", [128, 2048], BF16))
        psb = stD.enter_context(nc.sbuf_tensor("psb", [128, 2048], BF16))
        ptsb = stD.enter_context(nc.sbuf_tensor("ptsb", [128, 2048], BF16))
        scrd = stD.enter_context(nc.sbuf_tensor("scrD", [64, 2048], F32))
        wqbst = stD.enter_context(nc.sbuf_tensor("wqbst", [128, 256], F32))
        wqbbf = [stD.enter_context(nc.sbuf_tensor(f"wqbbf{i}", [128, 12 * 192], BF16))
                 for i in range(2)]
        wkvbbf = [stD.enter_context(nc.sbuf_tensor(f"wkvbbf{i}", [128, 1024], BF16))
                  for i in range(2)]

        head_ready = {}
        hw_state = {"POq": 0, "mmq": [0, 0]}

        def load_head_weights(h):
            sl = h % 2
            for kc in range(12):
                p.wait("sync", "POq", hw_state["POq"])
                p.dma("sync", wqbst[:, 0:192],
                      ap["wqb"][128 * kc:128 * (kc + 1), 192 * h:192 * (h + 1)],
                      inc="WQ")
                p.wait("scalar", "WQ")
                if kc == 0:
                    p.wait("scalar", "PE", p.cnt["PE"])
                ACT(wqbbf[sl][:, 192 * kc:192 * (kc + 1)],
                    wqbst[:, 0:192], AF.Copy, scale=qlnwc[:, kc:kc + 1], inc="POq")
                hw_state["POq"] = p.cnt["POq"]
            for kc in range(4):
                p.wait("sync", "POq", hw_state["POq"])
                p.dma("sync", wqbst[:, 0:256],
                      ap["wkvb"][128 * kc:128 * (kc + 1), 256 * h:256 * (h + 1)],
                      inc="WQ")
                p.wait("scalar", "WQ")
                ACT(wkvbbf[sl][:, 256 * kc:256 * (kc + 1)],
                    wqbst[:, 0:256], AF.Copy, scale=kvlnwc[:, kc:kc + 1], inc="POq")
                hw_state["POq"] = p.cnt["POq"]
            head_ready[h] = p.cnt["POq"]

        p.wait("tensor", "VE")
        p.wait("tensor", "SE")
        load_head_weights(0)
        pb_se = 0
        pt_se = 0
        out_ve = [0, 0]
        vev = [0, 0]

        for h in range(NH):
            sl = h % 2
            wbf = wqbbf[sl]
            kvbf = wkvbbf[sl]
            p.wait("tensor", "POq", head_ready[h])

            p.wait("tensor", "SE", pb_se)
            for kc in range(12):
                for s4 in range(4):
                    MM(PB[:, 512 * s4:512 * (s4 + 1)],
                       wbf[:, 192 * kc:192 * kc + 128],
                       q_a[:, 2048 * kc + 512 * s4:2048 * kc + 512 * (s4 + 1)],
                       kc == 0, kc == 11,
                       inc=("PE" if (kc == 11 and s4 == 3) else None))
            p.wait("scalar", "PE")
            ACT(qnope[:], PB[:], AF.Copy, inc="SE")
            pb_se = p.cnt["SE"]
            p.wait("tensor", "SE", pb_se)
            for kc in range(12):
                for s4 in range(4):
                    MM(PB[0:64, 512 * s4:512 * (s4 + 1)],
                       wbf[:, 192 * kc + 128:192 * (kc + 1)],
                       q_a[:, 2048 * kc + 512 * s4:2048 * kc + 512 * (s4 + 1)],
                       kc == 0, kc == 11,
                       inc=("PE" if (kc == 11 and s4 == 3) else None))
            p.wait("scalar", "PE")
            ACT(qper[:], PB[0:64, :], AF.Copy, inc="SE")
            pb_se = p.cnt["SE"]
            p.wait("tensor", "SE", pb_se)
            for s4 in range(4):
                MM(PB[0:64, 512 * s4:512 * (s4 + 1)], rotm[:],
                   qper[:, 512 * s4:512 * (s4 + 1)], True, True,
                   inc=("PE" if s4 == 3 else None))
            p.wait("vector", "PE")
            TT("vector", scrd[:], PB[0:64, :], cs_pack[64:128, :], ALU.mult)
            TT("vector", qper[:], qper[:], cs_pack[0:64, :], ALU.mult)
            TT("vector", qper[:], qper[:], scrd[:], ALU.add, inc="VE")
            pb_ve = p.cnt["VE"]
            p.wait("tensor", "VE", pb_ve)
            for kc in range(4):
                for s4 in range(4):
                    MM(PB[:, 512 * s4:512 * (s4 + 1)],
                       kvbf[:, 256 * kc:256 * kc + 128],
                       ckv_n[:, 2048 * kc + 512 * s4:2048 * kc + 512 * (s4 + 1)],
                       kc == 0, kc == 3,
                       inc=("PE" if (kc == 3 and s4 == 3) else None))
            p.wait("scalar", "PE")
            ACT(knope[:], PB[:], AF.Copy, inc="SE")
            pb_se = p.cnt["SE"]
            for T in range(NT):
                POx = PO1 if T % 2 == 0 else PO2
                p.wait("tensor", "SE", vev[T % 2])
                p.wait("tensor", "VE", out_ve[T % 2])
                for kc in range(4):
                    MM(POx[:, 0:128],
                       ckv_n[:, 2048 * kc + 128 * T:2048 * kc + 128 * (T + 1)],
                       kvbf[:, 256 * kc + 128:256 * (kc + 1)],
                       kc == 0, kc == 3, inc=("PE" if kc == 3 else None))
                p.wait("scalar", "PE")
                ACT(vtm[:, 128 * T:128 * (T + 1)], POx[:, 0:128], AF.Copy, inc="SE")
                vev[T % 2] = p.cnt["SE"]
            if h + 1 < NH:
                load_head_weights(h + 1)
            for qi in range(NT):
                Eext = 128 * (qi + 1)
                nsl = (Eext + 511) // 512
                p.wait("tensor", "SE", pb_se)
                for s4 in range(nsl):
                    w = min(512, Eext - 512 * s4)
                    MM(PB[:, 512 * s4:512 * s4 + w], qnope[:, 128 * qi:128 * (qi + 1)],
                       knope[:, 512 * s4:512 * s4 + w], True, False)
                for s4 in range(nsl):
                    w = min(512, Eext - 512 * s4)
                    MM(PB[:, 512 * s4:512 * s4 + w], qper[:, 128 * qi:128 * (qi + 1)],
                       kpe_r[:, 512 * s4:512 * s4 + w], False, True,
                       inc=("PE" if s4 == nsl - 1 else None))
                p.wait("vector", "PE")
                TT("vector", PB[:, 128 * qi:128 * (qi + 1)],
                   PB[:, 128 * qi:128 * (qi + 1)], maskt[:], ALU.add)
                RED("vector", scol("ngm"), PB[:, 0:Eext], ALU.max, neg=True)
                SP()
                TT("vector", scol("ngs"), scol("ngm"), escale[:, qi:qi + 1],
                   ALU.mult)
                VEM()
                p.wait("scalar", "VE")
                ACT(psb[:, 0:Eext], PB[:, 0:Eext], AF.Exp,
                    bias=scol("ngs"), scale=escale[:, qi:qi + 1],
                    accum=scol("accraw"))
                ACT(scol("ell"), scol("accraw"), AF.Copy, inc="SE")
                pb_se = p.cnt["SE"]
                p.wait("vector", "SE")
                RECIP(scol("linv"), scol("ell"))
                SP()
                TS("vector", psb[:, 0:Eext], psb[:, 0:Eext], scol("linv"),
                   ALU.mult, inc="VE")
                p.wait("tensor", "VE")
                p.wait("tensor", "SE", pt_se)
                for j in range(qi + 1):
                    PT = PTa if j < 8 else PTb
                    TR(PT[:, 128 * (j % 8):128 * (j % 8) + 128],
                       psb[:, 128 * j:128 * (j + 1)], identb[:],
                       inc=("PE" if j == qi else None))
                p.wait("scalar", "PE")
                ACT(ptsb[:, 0:min(Eext, 1024)], PTa[:, 0:min(Eext, 1024)], AF.Copy,
                    inc=("SE" if qi < 8 else None))
                if qi >= 8:
                    ACT(ptsb[:, 1024:Eext], PTb[:, 0:Eext - 1024], AF.Copy, inc="SE")
                pt_se = p.cnt["SE"]
                POx = PO1 if qi % 2 == 0 else PO2
                p.wait("tensor", "SE", pt_se)
                p.wait("tensor", "VE", out_ve[qi % 2])
                for j in range(qi + 1):
                    MM(POx[:, 0:128], vtm[:, 128 * j:128 * (j + 1)],
                       ptsb[:, 128 * j:128 * (j + 1)], j == 0, j == qi,
                       inc=("PE" if j == qi else None))
                p.wait("vector", "PE")
                CP("vector",
                   attnout[:, 2048 * h + 128 * qi:2048 * h + 128 * (qi + 1)],
                   POx[:, 0:128], inc="VE")
                out_ve[qi % 2] = p.cnt["VE"]
                if qi % 4 == 3:
                    wprep_step(1)
    dump("attnout", attnout[:])
    if stop_at <= 3:
        return _finish(p, nc, dbg_out, out, early=True)

    phase_barrier()
    stk_qa.close(); p.open_stacks.remove(stk_qa)

    with ExitStack() as stE:
        PB = stE.enter_context(nc.psum_tensor("PBE", [128, 2048], F32))
        PTa = stE.enter_context(nc.psum_tensor("PTaE", [128, 1024], BF16))
        PTb = stE.enter_context(nc.psum_tensor("PTbE", [128, 1024], BF16))
        wo_bf = stE.enter_context(nc.sbuf_tensor("wo_bf", [128, 16 * 2048], BF16))
        xstg1 = stE.enter_context(nc.sbuf_tensor("xstgE", [128, 2048], F32))
        xstg = [xstg1, xstg1]
        x2t = stE.enter_context(nc.sbuf_tensor("x2t", [128, 2048], F32))
        segA = stE.enter_context(nc.sbuf_tensor("segAE", [128, 2048], F32))
        segB = stE.enter_context(nc.sbuf_tensor("segBE", [128, 2048], F32))
        segI = stE.enter_context(nc.sbuf_tensor("segIE", [128, 2048], I32))
        i1tm = stE.enter_context(nc.sbuf_tensor("i1tm", [128, 2048], BF16))
        i1st = stE.enter_context(nc.sbuf_tensor("i1st", [128, 2048], BF16))

        p.wait("tensor", "VE")
        p.wait("tensor", "SE")
        p.wait("vector", "WF4", p.cnt.get("WF4", 0))
        wost = segB
        wo_cast = []
        for c in range(16):
            p.wait("sync", "POw", p.cnt.get("POw", 0))
            p.dma("sync", wost[:], ap["wo"][128 * c:128 * (c + 1), :], inc="WOL")
            p.wait("scalar", "WOL", 16 * (c + 1))
            ACT(wo_bf[:, 2048 * c:2048 * (c + 1)], wost[:], AF.Copy, inc="POw")
            wo_cast.append(p.cnt["POw"])
            wprep_step(2)

        xe_guard = [0, 0]
        pb_free = 0
        pt_guard = [0, 0]
        for T in range(NT):
            sl = 0
            semx = "XEA"
            p.wait("sync", "VE", xe_guard[sl])
            p.dma("sync", xstg[sl][:], ap["x"][128 * T:128 * (T + 1), :], inc=semx)
            p.wait("tensor", "VE", pb_free)
            if T == 0:
                p.wait("tensor", "POw", wo_cast[15])
            for c in range(16):
                for s4 in range(4):
                    MM(PB[:, 512 * s4:512 * (s4 + 1)],
                       attnout[:, 2048 * c + 128 * T:2048 * c + 128 * (T + 1)],
                       wo_bf[:, 2048 * c + 512 * s4:2048 * c + 512 * (s4 + 1)],
                       c == 0, c == 15,
                       inc=("PE" if (c == 15 and s4 == 3) else None))
            p.wait("vector", semx)
            p.wait("vector", "PE")
            p.wait("vector", "X2S", p.cnt.get("X2S", 0))
            TT("vector", x2t[:], PB[:], xstg[sl][:], ALU.add, inc="VE")
            pb_free = p.cnt["VE"]
            xe_guard[sl] = p.cnt["VE"]
            p.wait("scalar", "VE")
            p.dma("scalar", x2s[128 * T:128 * (T + 1), :], x2t[:], inc="X2S")
            if DBG("x2"):
                p.dma("scalar", dbg_out["x2"][128 * T:128 * (T + 1), :], x2t[:],
                      inc="ST")
            lnstats(x2t[:], segA[:])
            TS("vector", segA[:], x2t[:], scol("mu"), ALU.subtract, scol("rv"),
               ALU.mult, inc="VE")
            p.wait("scalar", "VE")
            ACTACC(segB[:], segA[:], AF.Square, scol("q2"))
            p.wait("vector", "SE")
            TS("vector", scol("m2l"), scol("q2"), 1.0 / D, ALU.mult)
            VEM()
            p.wait("scalar", "VE")
            ACT(scol("sdl"), scol("m2l"), AF.Sqrt, bias=col_eps5, inc="SE")
            p.wait("vector", "SE")
            RECIP(scol("r2"), scol("sdl"))
            RED("vector", scol("mx"), segA[:], ALU.max, absval=True)
            SP()
            TT("vector", scol("rm"), scol("r2"), scol("mx"), ALU.mult)
            SP()
            TS("vector", scol("rmc"), scol("rm"), 1e-5, ALU.max)
            SP()
            RECIP(scol("irmc"), scol("rmc"))
            SP()
            TS("vector", osc1c[:, T:T + 1], scol("rmc"), scol("m127_1"), ALU.mult)
            STT("vector", scol("c1"), scol("irmc"), 127.0, scol("r2"),
                ALU.mult, ALU.mult)
            SP()
            TS("vector", segI[:], segA[:], scol("c1"), ALU.mult)
            CP("vector", i1tm[:], segI[:])
            STT("vector", segB[:], segA[:], scol("c1"), i1tm[:], ALU.mult,
                ALU.subtract)
            TS("vector", segA[:], segB[:], col_h, ALU.is_gt)
            TS("vector", segB[:], segB[:], col_nh, ALU.is_lt)
            TT("vector", segA[:], segA[:], segB[:], ALU.subtract)
            TT("vector", i1tm[:], i1tm[:], segA[:], ALU.add, inc="VE")
            p.wait("tensor", "VE")
            for g, PT in ((0, PTa), (1, PTb)):
                p.wait("tensor", "SE", pt_guard[g])
                for c in range(8):
                    cc = 8 * g + c
                    TR(PT[:, 128 * c:128 * (c + 1)], i1tm[:, 128 * cc:128 * (cc + 1)],
                       identb[:], inc=("PE" if c == 7 else None))
                p.wait("scalar", "PE")
                p.wait("scalar", "I1S", p.cnt.get("I1S", 0))
                ACT(i1st[:, 1024 * g:1024 * (g + 1)], PT[:], AF.Copy, inc="SE")
                pt_guard[g] = p.cnt["SE"]
            p.wait("scalar", "SE")
            p.dma("scalar",
                  int1s.rearrange("(c i) t -> i c t", i=128)[:, :, 128 * T:128 * (T + 1)],
                  i1st[:].rearrange("i (c t) -> i c t", t=128), inc="I1S")
            wprep_step(3)
    dump("osc1", osc1c[:])
    if DBG("int1"):
        pass
    if stop_at <= 4:
        return _finish(p, nc, dbg_out, out, early=True)

    while wstate["i"] < len(wtB):
        wprep_step(4)
    phase_barrier()
    stk_w.close(); p.open_stacks.remove(stk_w)
    stk_ao.close(); p.open_stacks.remove(stk_ao)

    with ExitStack() as stF:
        PB = stF.enter_context(nc.psum_tensor("PBF", [128, 2048], F32))
        PTa = stF.enter_context(nc.psum_tensor("PTaF", [128, 1024], BF16))
        PTb = stF.enter_context(nc.psum_tensor("PTbF", [128, 1024], BF16))
        PH = stF.enter_context(nc.psum_tensor("PHF", [128, 1024], F32))
        hbuf = stF.enter_context(nc.sbuf_tensor("hbuf", [128, 64 * QW], BF16))
        int1h = stF.enter_context(nc.sbuf_tensor("int1h", [128, 16 * QW], BF16))
        big = [stF.enter_context(nc.sbuf_tensor(f"bigF{i}", [128, INNER], BF16))
               for i in range(2)]
        htm = stF.enter_context(nc.sbuf_tensor("htmF", [128, INNER], BF16))
        segA = stF.enter_context(nc.sbuf_tensor("segAF", [128, 2048], F32))
        segB = stF.enter_context(nc.sbuf_tensor("segBF", [128, 2048], F32))
        segI = stF.enter_context(nc.sbuf_tensor("segIF", [128, 2048], I32))
        w1sb = [stF.enter_context(nc.sbuf_tensor(f"w1sb{i}", [128, 2048], BF16))
                for i in range(2)]
        s1b = stF.enter_context(nc.sbuf_tensor("s1b", [128, QW], F32))
        out2 = stF.enter_context(nc.sbuf_tensor("out2F", [128, 16 * QW], BF16))
        x3t = stF.enter_context(nc.sbuf_tensor("x3t", [128, 2048], F32))
        b2bc = stF.enter_context(nc.sbuf_tensor("b2bc", [128, 2048], F32))
        rrow = stF.enter_context(nc.sbuf_tensor("rrowF", [1, QW], F32))

        p.dma("sync", segA[0:1, 0:2048], ap["b2row"], inc="GL")
        w1q_done = p.cnt["WSS"]
        i1_done = p.cnt["I1S"]
        x2_done = p.cnt["X2S"]
        p.wait("tensor", "GL")
        p.wait("tensor", "VE")
        p.wait("tensor", "SE")
        for s4 in range(4):
            MM(PB[:, 512 * s4:512 * (s4 + 1)], onesrow[:],
               segA[0:1, 512 * s4:512 * (s4 + 1)], True, True,
               inc=("PE" if s4 == 3 else None))
        p.wait("vector", "PE")
        CP("vector", b2bc[:], PB[:], inc="VE")
        p.wait("tensor", "VE")
        TR(PB[0:16, 0:128], osc1c[:], ident[:], inc="PE")
        p.wait("vector", "PE")
        CP("vector", mstage[:], PB[0:16, 0:128], inc="VE")
        p.wait("scalar", "VE")
        p.dma("scalar", rowscr, mstage[:], inc="RS")

        def build_bcast(dst_bf, src_dram):
            for blk in range(4):
                p.wait("sync", "SE", p.cnt["SE"])
                p.wait("sync", "VE", p.cnt["VE"])
                p.dma("sync", segA[0:1, 0:2048],
                      src_dram[0:1, 2048 * blk:2048 * (blk + 1)], inc="GBL")
                p.wait("tensor", "GBL")
                p.wait("tensor", "SE", p.cnt["SE"])
                p.wait("tensor", "VE", p.cnt["VE"])
                for s4 in range(4):
                    MM(PB[:, 512 * s4:512 * (s4 + 1)], onesrow[:],
                       segA[0:1, 512 * s4:512 * (s4 + 1)], True, True,
                       inc=("PE" if s4 == 3 else None))
                p.wait("scalar", "PE")
                ACT(dst_bf[:, 2048 * blk:2048 * (blk + 1)], PB[:], AF.Copy, inc="SE")
            p.wait("tensor", "SE")

        mm1_guard = [0, 0]
        silu_guard = 0
        pt_guard = [0, 0]
        f3_guard = [0, 0]
        merge_done = [0]
        for q in range(NQ):
            t0 = QW * q
            p.wait("sync", "RS")
            p.dma("sync", rrow[:],
                  rowscr.rearrange("a b -> (a b)").rearrange("(o n) -> o n", o=1)[
                      0:1, t0:t0 + QW], inc="RR")
            p.wait("tensor", "RR")
            p.wait("tensor", "VE", p.cnt["VE"])
            MM(PB[:, 0:512], onesrow[:], rrow[0:1, 0:512], True, True, inc="PE")
            p.wait("vector", "PE")
            CP("vector", s1b[:], PB[:, 0:512], inc="VE")
            if q == 0:
                p.wait("sync", "I1S", i1_done)
                p.wait("sync", "WSS", w1q_done)
            p.wait("sync", "VE", merge_done[0])
            for kc in range(16):
                p.dma("sync", int1h[:, QW * kc:QW * (kc + 1)],
                      int1s[128 * kc:128 * (kc + 1), t0:t0 + QW], inc="I1R")
            for oc in range(64):
                sl = oc % 2
                p.wait("sync", "PE", mm1_guard[sl])
                p.dma("sync", w1sb[sl][:].rearrange("p (a j) -> p a j", j=128),
                      w1q[oc].rearrange("(a p) j -> p a j", p=128),
                      inc=("W1A" if sl == 0 else "W1B"))
                p.wait("tensor", "W1A" if sl == 0 else "W1B")
                if oc == 0:
                    p.wait("tensor", "I1R", 16 * 16 * (q + 1))
                    p.wait("tensor", "VE", p.cnt["VE"])
                p.wait("tensor", "SE", silu_guard)
                ph = PH[:, 512 * sl:512 * (sl + 1)]
                for kc in range(16):
                    MM(ph, w1sb[sl][:, 128 * kc:128 * (kc + 1)],
                       int1h[:, QW * kc:QW * (kc + 1)], kc == 0, kc == 15,
                       inc=("PE" if kc == 15 else None))
                mm1_guard[sl] = p.cnt["PE"]
                p.wait("vector", "PE")
                TT("vector", segA[:, 0:512], ph, s1b[:], ALU.mult, inc="VE")
                p.wait("scalar", "VE")
                ACT(segB[:, 0:512], segA[:, 0:512], AF.Silu, bias=b1c[:, oc:oc + 1])
                ACT(hbuf[:, QW * oc:QW * (oc + 1)], segB[:, 0:512], AF.Gelu,
                    inc="SE")
                silu_guard = p.cnt["SE"]
            if q == 0:
                dump("h0", hbuf[:])
            p.wait("tensor", "W2A", p.cnt.get("W2A", 0))
            p.wait("tensor", "W2B", p.cnt.get("W2B", 0))
            p.wait("tensor", "PO", p.cnt.get("PO", 0))
            build_bcast(big[0], ap["gln"])
            build_bcast(big[1], ap["bln"])
            for Ti in range(QW // 128):
                for c in range(64):
                    g = (c // 8) % 2
                    PT = PTa if g == 0 else PTb
                    if c % 8 == 0:
                        p.wait("tensor", "SE", pt_guard[g])
                    TR(PT[:, 128 * (c % 8):128 * (c % 8) + 128],
                       hbuf[:, QW * c + 128 * Ti:QW * c + 128 * (Ti + 1)],
                       identb[:], inc=("PE" if c % 8 == 7 else None))
                    if c % 8 == 7:
                        p.wait("scalar", "PE")
                        ACT(htm[:, 1024 * (c // 8):1024 * (c // 8 + 1)], PT[:],
                            AF.Copy, inc="SE")
                        pt_guard[g] = p.cnt["SE"]
                p.wait("vector", "SE")
                for sgi in range(4):
                    sg = htm[:, 2048 * sgi:2048 * (sgi + 1)]
                    RED("vector", scol("p1"), sg, ALU.add, inc="VE")
                    p.wait("scalar", "VE")
                    ACTACC(segA[:], sg, AF.Square, scol("p2"))
                    p.wait("vector", "SE")
                    SP()
                    if sgi == 0:
                        CP("vector", scol("sx"), scol("p1"))
                        CP("vector", scol("sq"), scol("p2"))
                    else:
                        TT("vector", scol("sx"), scol("sx"), scol("p1"), ALU.add)
                        TT("vector", scol("sq"), scol("sq"), scol("p2"), ALU.add)
                    SP()
                TS("vector", scol("mu"), scol("sx"), 1.0 / INNER, ALU.mult)
                TS("vector", scol("m2"), scol("sq"), 1.0 / INNER, ALU.mult)
                SP()
                STT("vector", scol("v2"), scol("mu"), scol("mu"), scol("m2"),
                    ALU.mult, ALU.subtract)
                VEM()
                p.wait("scalar", "VE")
                ACT(scol("v1"), scol("v2"), AF.Sqrt, bias=col_eps5, scale=-1.0,
                    inc="SE")
                p.wait("vector", "SE")
                RECIP(scol("rv"), scol("v1"))
                VEM()
                for sgi in range(4):
                    sg = htm[:, 2048 * sgi:2048 * (sgi + 1)]
                    TS("vector", segA[:], sg, scol("mu"), ALU.subtract, scol("rv"),
                       ALU.mult)
                    TT("vector", segA[:], segA[:], big[0][:, 2048 * sgi:2048 * (sgi + 1)],
                       ALU.mult)
                    TT("vector", segA[:], segA[:], big[1][:, 2048 * sgi:2048 * (sgi + 1)],
                       ALU.add, inc="VE")
                    RED("vector", scol("p1"), segA[:], ALU.max, absval=True)
                    p.wait("scalar", "VE")
                    ACTACC(segB[:], segA[:], AF.Square, scol("p2"))
                    p.wait("vector", "SE")
                    SP()
                    if sgi == 0:
                        CP("vector", scol("zq"), scol("p2"))
                        CP("vector", scol("zm"), scol("p1"))
                    else:
                        TT("vector", scol("zq"), scol("zq"), scol("p2"), ALU.add)
                        TT("vector", scol("zm"), scol("zm"), scol("p1"), ALU.max)
                    SP()
                TS("vector", scol("m2z"), scol("zq"), 1.0 / INNER, ALU.mult)
                VEM()
                p.wait("scalar", "VE")
                ACT(scol("sdz"), scol("m2z"), AF.Sqrt, bias=col_eps5, inc="SE")
                p.wait("vector", "SE")
                RECIP(scol("rho"), scol("sdz"))
                SP()
                TT("vector", scol("rm"), scol("rho"), scol("zm"), ALU.mult)
                SP()
                TS("vector", scol("rmc"), scol("rm"), 1e-5, ALU.max)
                SP()
                RECIP(scol("irmc"), scol("rmc"))
                SP()
                TS("vector", osc2c[:, 4 * q + Ti:4 * q + Ti + 1], scol("rmc"),
                   scol("m127_2"), ALU.mult)
                STT("vector", scol("c2"), scol("irmc"), 127.0, scol("rho"),
                    ALU.mult, ALU.mult)
                SP()
                for sgi in range(4):
                    sg = htm[:, 2048 * sgi:2048 * (sgi + 1)]
                    TS("vector", segA[:], sg, scol("mu"), ALU.subtract, scol("rv"),
                       ALU.mult)
                    TT("vector", segA[:], segA[:], big[0][:, 2048 * sgi:2048 * (sgi + 1)],
                       ALU.mult)
                    TT("vector", segA[:], segA[:], big[1][:, 2048 * sgi:2048 * (sgi + 1)],
                       ALU.add)
                    TS("vector", segI[:], segA[:], scol("c2"), ALU.mult)
                    CP("vector", sg, segI[:])
                    STT("vector", segB[:], segA[:], scol("c2"), sg, ALU.mult,
                        ALU.subtract)
                    TS("vector", segA[:], segB[:], col_h, ALU.is_gt)
                    TS("vector", segB[:], segB[:], col_nh, ALU.is_lt)
                    TT("vector", segA[:], segA[:], segB[:], ALU.subtract)
                    TT("vector", sg, sg, segA[:], ALU.add, inc="VE")
                p.wait("tensor", "VE")
                for c in range(64):
                    g = (c // 8) % 2
                    PT = PTa if g == 0 else PTb
                    if c % 8 == 0:
                        p.wait("tensor", "SE", pt_guard[g])
                    TR(PT[:, 128 * (c % 8):128 * (c % 8) + 128],
                       htm[:, 128 * c:128 * (c + 1)], identb[:],
                       inc=("PE" if c % 8 == 7 else None))
                    if c % 8 == 7:
                        p.wait("scalar", "PE")
                        c0 = c - 7
                        dst = hbuf[:].rearrange("i (a t) -> i a t", t=QW)[
                            :, c0:c0 + 8, 128 * Ti:128 * (Ti + 1)]
                        ACT(dst, PT[:].rearrange("i (a t) -> i a t", t=128),
                            AF.Copy, inc="SE")
                        pt_guard[g] = p.cnt["SE"]
            if q == 0:
                dump("int2_0", hbuf[:])
                dump("osc2", osc2c[:])
            f3v_guard = 0
            f2_done = p.cnt["VE"]
            for m in range(16):
                sl = m % 2
                semw = "W2A" if sl == 0 else "W2B"
                p.wait("sync", "PE", f3_guard[sl])
                if m < 2:
                    p.wait("sync", "VE", f2_done)
                for part in range(4):
                    p.dma("sync",
                          big[sl][:, 2048 * part:2048 * (part + 1)].rearrange(
                              "p (a j) -> p a j", j=128),
                          w2q[m][2048 * part:2048 * (part + 1), :].rearrange(
                              "(a p) j -> p a j", p=128), inc=semw)
                p.wait("tensor", semw)
                p.wait("tensor", "VE", p.cnt["VE"] if m < 2 else f3v_guard)
                for kc in range(64):
                    MM(PB[:, 512 * sl:512 * (sl + 1)],
                       big[sl][:, 128 * kc:128 * (kc + 1)],
                       hbuf[:].rearrange("i (a t) -> i a t", t=QW)[:, kc, :],
                       kc == 0, kc == 63, inc=("PE" if kc == 63 else None))
                f3_guard[sl] = p.cnt["PE"]
                p.wait("vector", "PE")
                CP("vector", out2[:, QW * m:QW * (m + 1)], PB[:, 512 * sl:512 * (sl + 1)],
                   inc="VE")
                f3v_guard = p.cnt["VE"]
            p.wait("sync", "X2S", x2_done)
            for Ti in range(QW // 128):
                trow = t0 + 128 * Ti
                p.wait("sync", "STO", p.cnt.get("STO", 0))
                p.dma("sync", x3t[:], x2s[trow:trow + 128, :], inc="XM")
                p.wait("vector", "XM")
                TT("vector", x3t[:], x3t[:], b2bc[:], ALU.add, inc="VE")
                for mg in range(2):
                    PT = PTa if mg == 0 else PTb
                    p.wait("tensor", "VE", p.cnt["VE"])
                    for mm_ in range(8):
                        m = 8 * mg + mm_
                        TR(PT[:, 128 * mm_:128 * (mm_ + 1)],
                           out2[:, QW * m + 128 * Ti:QW * m + 128 * (Ti + 1)],
                           identb[:], inc=("PE" if mm_ == 7 else None))
                    p.wait("vector", "PE")
                    for mm_ in range(8):
                        m = 8 * mg + mm_
                        STT("vector", x3t[:, 128 * m:128 * (m + 1)],
                            PT[:, 128 * mm_:128 * (mm_ + 1)],
                            osc2c[:, 4 * q + Ti:4 * q + Ti + 1],
                            x3t[:, 128 * m:128 * (m + 1)], ALU.mult, ALU.add,
                            inc=("VE" if mm_ == 7 else None))
                p.wait("scalar", "VE")
                p.dma("scalar", out[trow:trow + 128, :], x3t[:], inc="STO")
                merge_done[0] = p.cnt["VE"]

    return _finish(p, nc, dbg_out, out, early=False)


def _finish(p, nc, dbg_out, out, early):
    if "STO" in p.cnt:
        p.wait("gpsimd", "STO")
    if "ST" in p.cnt:
        p.wait("gpsimd", "ST")
    blk = p.ctx.enter_context(nc.Block())
    p.finalize(blk)
    for stk in reversed(p.open_stacks):
        stk.close()
    p.ctx.close()
    return nc, dbg_out


def make_inmap(inputs, core):
    hs = np.ascontiguousarray(np.asarray(inputs["L0_attn_hidden_states"])[core])
    w1t = np.ascontiguousarray(np.asarray(inputs["L0_ffn_w1"]).T)
    w2t = np.ascontiguousarray(np.asarray(inputs["L0_ffn_w2"]).T)
    m = dict(
        x=hs.astype(np.float32),
        wqa=np.ascontiguousarray(np.asarray(inputs["L0_attn_Wqa"], np.float32)),
        wqb=np.ascontiguousarray(np.asarray(inputs["L0_attn_Wqb"], np.float32)),
        wkva=np.ascontiguousarray(np.asarray(inputs["L0_attn_Wkva"], np.float32)),
        wkvb=np.ascontiguousarray(np.asarray(inputs["L0_attn_Wkvb"], np.float32)),
        wo=np.ascontiguousarray(np.asarray(inputs["L0_attn_Wo"], np.float32)),
        qlnw=np.ascontiguousarray(np.asarray(inputs["L0_attn_q_a_ln_w"],
                                             np.float32).reshape(12, 128)),
        kvlnw=np.ascontiguousarray(np.asarray(inputs["L0_attn_kv_a_ln_w"],
                                              np.float32).reshape(4, 128)),
        w1t=w1t.astype(np.float32), w2t=w2t.astype(np.float32),
        b1=np.ascontiguousarray(np.asarray(inputs["L0_ffn_b1"],
                                           np.float32).reshape(64, 128)),
        b2=np.ascontiguousarray(np.asarray(inputs["L0_ffn_b2"],
                                           np.float32).reshape(16, 128)),
        b2row=np.ascontiguousarray(np.asarray(inputs["L0_ffn_b2"],
                                              np.float32).reshape(1, 2048)),
        gln=np.ascontiguousarray(np.asarray(inputs["L0_ffn_ln_g"],
                                            np.float32).reshape(1, INNER)),
        bln=np.ascontiguousarray(np.asarray(inputs["L0_ffn_ln_b"],
                                            np.float32).reshape(1, INNER)),
    )
    m.update(host_consts())
    return m


def kernel(**inputs):
    nc, _ = build_program(dbg=(), upto="F")
    in_maps = [make_inmap(inputs, c) for c in range(8)]
    res = run_bass_kernel_spmd(nc, in_maps, core_ids=list(range(8)))
    outs = [np.asarray(r["out"], dtype=np.float32) for r in res.results]
    return np.stack(outs, axis=0)


# version 04
# speedup vs baseline: 2.1296x; 1.0973x over previous
import math
from contextlib import ExitStack

import numpy as np
import ml_dtypes

import concourse.bass as bass
import concourse.mybir as mybir
from concourse.bass_utils import run_bass_kernel_spmd

F32 = mybir.dt.float32
BF16 = mybir.dt.bfloat16
I32 = mybir.dt.int32
AF = mybir.ActivationFunctionType
ALU = mybir.AluOpType
AX = mybir.AxisListType

S = 2048
D = 2048
NT = 16
NH = 16
QLORA = 1536
SC192 = 1.0 / math.sqrt(192.0)
INNER = 8192
QW = 512
NQ = S // QW

ENGS = ("sync", "tensor", "vector", "scalar", "gpsimd")


class Prog:
    def __init__(self, nc):
        self.nc = nc
        self.recs = {e: [] for e in ENGS}
        self.sem_handles = {}
        self.cnt = {}
        self.ctx = ExitStack()

    def sem(self, name):
        if name not in self.sem_handles:
            self.sem_handles[name] = self.ctx.enter_context(self.nc.semaphore(name))
            self.cnt[name] = 0
        return self.sem_handles[name]

    def wait(self, eng, semname, target=None):
        h = self.sem(semname)
        if target is None:
            target = self.cnt[semname]
        assert target <= self.cnt[semname], (semname, target, self.cnt[semname])
        if target > 0:
            self.recs[eng].append(lambda e, hh=h, t=target: e.wait_ge(hh, t))

    def emit(self, eng, build, inc=None, n=1):
        if inc is not None:
            h = self.sem(inc)
            self.cnt[inc] += n
            self.recs[eng].append(lambda e, b=build, hh=h, nn=n: b(e).then_inc(hh, nn))
        else:
            self.recs[eng].append(build)

    def dma(self, eng, out, in_, inc, **kw):
        self.emit(eng, lambda e, o=out, i=in_, k=dict(kw): e.dma_start(out=o, in_=i, **k),
                  inc=inc, n=16)

    def finalize(self, block):
        for eng in ENGS:
            recs = self.recs[eng]
            getattr(block, eng)(lambda e, r=recs: [f(e) for f in r])


def host_consts():
    ident = np.eye(128, dtype=np.float32)
    mask = np.where(np.arange(128)[None, :] > np.arange(128)[:, None],
                    np.float32(-1e30), np.float32(0.0)).astype(np.float32)
    P = np.zeros((64, 64), dtype=np.float32)
    P[:32, 32:] = -np.eye(32, dtype=np.float32)
    P[32:, :32] = np.eye(32, dtype=np.float32)
    cols = np.zeros((128, 8), dtype=np.float32)
    cols[:, 0] = 1e-5
    cols[:, 1] = 1e-6
    cols[:, 2] = 0.5
    cols[:, 3] = -0.5
    cols[:, 4] = -math.log(2.0 * math.pi)
    cols[:, 5] = 0.25
    iota = np.arange(2048, dtype=np.float32)[None, :]
    return dict(
        c_ident=ident,
        c_identb=ident.astype(ml_dtypes.bfloat16),
        c_mask=mask,
        c_rot=P.T.astype(ml_dtypes.bfloat16),
        c_cols=cols,
        c_iota=iota,
        c_onescol=np.ones((128, 1), dtype=np.float32),
        c_onescolb=np.ones((128, 1), dtype=ml_dtypes.bfloat16),
        c_onesrow=np.ones((1, 128), dtype=np.float32),
    )


def build_program(dbg=(), upto="F"):
    PHASES = ["A", "B", "C", "D", "E", "F"]
    stop_at = PHASES.index(upto)

    nc = bass.Bass("TRN2", target_bir_lowering=False, detect_race_conditions=False)
    p = Prog(nc)
    p.open_stacks = []
    EP = p.ctx.enter_context

    ap = {}
    def din(name, shape, dt=F32):
        ap[name] = nc.dram_tensor(name, shape, dt, kind="ExternalInput").ap()
    din("x", [S, D])
    din("wqa", [D, QLORA]); din("wqb", [QLORA, NH * 192]); din("wkva", [D, 576])
    din("wkvb", [512, NH * 256]); din("wo", [D, D])
    din("qlnw", [12, 128]); din("kvlnw", [4, 128])
    din("w1t", [D, INNER]); din("w2t", [INNER, D])
    din("b1", [64, 128]); din("b2", [16, 128]); din("b2row", [1, D])
    din("gln", [1, INNER]); din("bln", [1, INNER])
    din("c_ident", [128, 128]); din("c_identb", [128, 128], BF16)
    din("c_mask", [128, 128]); din("c_rot", [64, 64], BF16)
    din("c_cols", [128, 8]); din("c_iota", [1, 2048])
    din("c_onescol", [128, 1]); din("c_onescolb", [128, 1], BF16)
    din("c_onesrow", [1, 128])

    out = nc.dram_tensor("out", [S, D], F32, kind="ExternalOutput").ap()

    dbg_out = {}
    def dbg_decl(name, shape, dt=F32):
        if name in dbg:
            dbg_out[name] = nc.dram_tensor("dbg_" + name, shape, dt,
                                           kind="ExternalOutput").ap()
    for nm, sh, dt in [
        ("lnx", [128, 16 * 2048], BF16), ("q_a", [128, 12 * 2048], BF16),
        ("ckv_n", [128, 4 * 2048], BF16), ("kpe_r", [64, 2048], BF16),
        ("cs_pack", [128, 2048], BF16), ("escale", [128, 16], F32),
        ("attnout", [128, 16 * 2048], BF16), ("x2", [S, D], F32),
        ("int1", [D, S], BF16), ("osc1", [128, 16], F32),
        ("tern1", [128, 2048], BF16), ("wm", [128, 2], F32),
        ("h0", [128, 64 * QW], BF16), ("int2_0", [128, 64 * QW], BF16),
        ("osc2", [128, 16], F32),
    ]:
        dbg_decl(nm, sh, dt)
    def DBG(name):
        return name in dbg_out

    x2s = nc.dram_tensor("x2s", [S, D], F32, kind="Internal").ap()
    int1s = nc.dram_tensor("int1s", [D, S], BF16, kind="Internal").ap()
    w1q = nc.dram_tensor("w1q", [64, D, 128], BF16, kind="Internal").ap()
    w2q = nc.dram_tensor("w2q", [16, INNER, 128], BF16, kind="Internal").ap()
    rowscr = nc.dram_tensor("rowscr", [16, 128], F32, kind="Internal").ap()

    ident = EP(nc.sbuf_tensor("ident", [128, 128], F32))
    identb = EP(nc.sbuf_tensor("identb", [128, 128], BF16))
    maskt = EP(nc.sbuf_tensor("maskt", [128, 128], F32))
    rotm = EP(nc.sbuf_tensor("rotm", [64, 64], BF16))
    ccols = EP(nc.sbuf_tensor("ccols", [128, 8], F32))
    onescol = EP(nc.sbuf_tensor("onescol", [128, 1], F32))
    onescolb = EP(nc.sbuf_tensor("onescolb", [128, 1], BF16))
    onesrow = EP(nc.sbuf_tensor("onesrow", [1, 128], F32))
    qlnwc = EP(nc.sbuf_tensor("qlnwc", [128, 12], F32))
    kvlnwc = EP(nc.sbuf_tensor("kvlnwc", [128, 4], F32))
    b1c = EP(nc.sbuf_tensor("b1c", [128, 64], F32))
    b2c = EP(nc.sbuf_tensor("b2c", [128, 16], F32))
    stats = EP(nc.sbuf_tensor("stats", [128, 48], F32))
    _statmap = {}
    def scol(name, n=1):
        if name not in _statmap:
            base = sum(v[1] for v in _statmap.values())
            assert base + n <= 48, (name, base)
            _statmap[name] = (base, n)
        b, nn = _statmap[name]
        return stats[:, b:b + nn]
    wder = EP(nc.sbuf_tensor("wder", [2, 4], F32))
    wch = [EP(nc.sbuf_tensor(f"wch{i}", [128, 64], F32)) for i in range(2)]
    def scol2(name):
        if name not in _statmap:
            base = sum(v[1] for v in _statmap.values())
            assert base + 1 <= 48
            _statmap[name] = (base, 1)
        b, _n = _statmap[name]
        return stats[0:2, b:b + 1]
    for _nm in ("wtmp", "w1s", "w2s"):
        scol(_nm)
    stk_cs = ExitStack()
    cs_pack = stk_cs.enter_context(nc.sbuf_tensor("cs_pack", [128, 2048], BF16,
                                                  side="right"))
    kpe_r = stk_cs.enter_context(nc.sbuf_tensor("kpe_r", [64, 2048], BF16,
                                                side="right"))
    escale = EP(nc.sbuf_tensor("escale", [128, 16], F32))
    osc1c = EP(nc.sbuf_tensor("osc1c", [128, 16], F32))
    osc2c = EP(nc.sbuf_tensor("osc2c", [128, 16], F32))
    mstage = EP(nc.sbuf_tensor("mstage", [16, 128], F32))

    col_eps5 = ccols[:, 0:1]
    col_eps6 = ccols[:, 1:2]
    col_h = ccols[:, 2:3]
    col_nh = ccols[:, 3:4]
    col_l2pi = ccols[:, 4:5]
    col_q = ccols[:, 5:6]

    def MM(dst, lhsT, rhs, start, stop, inc=None):
        p.emit("tensor", lambda e, d=dst, l=lhsT, r=rhs, s1=start, s2=stop:
               e.matmul(out=d, lhsT=l, rhs=r, start=s1, stop=s2), inc=inc)

    def TR(dst, src, idn, inc=None):
        p.emit("tensor", lambda e, d=dst, s=src, i=idn:
               e.transpose(out=d, in_=s, identity=i), inc=inc)

    def ACT(dst, src, func, bias=0.0, scale=1.0, accum=None, inc=None):
        p.emit("scalar", lambda e, d=dst, s=src, f=func, b=bias, sc=scale, a=accum:
               e.activation(out=d, in_=s, func=f, bias=b, scale=sc, accum_out=a),
               inc=inc)

    def TT(eng, dst, a, b, op, inc=None):
        p.emit(eng, lambda e, d=dst, x=a, y=b, o=op:
               e.tensor_tensor(out=d, in0=x, in1=y, op=o), inc=inc)

    def TS(eng, dst, a, s1, op1, s2=None, op2=None, inc=None, accum=None):
        if s2 is None:
            p.emit(eng, lambda e, d=dst, x=a, u=s1, o1=op1, ac=accum:
                   e.tensor_scalar(out=d, in0=x, scalar1=u, scalar2=None, op0=o1,
                                   accum_out=ac), inc=inc)
        else:
            p.emit(eng, lambda e, d=dst, x=a, u=s1, o1=op1, v=s2, o2=op2, ac=accum:
                   e.tensor_scalar(out=d, in0=x, scalar1=u, scalar2=v, op0=o1,
                                   op1=o2, accum_out=ac), inc=inc)

    def STT(eng, dst, in0, scalar, in1, op0, op1, inc=None):
        p.emit(eng, lambda e, d=dst, a=in0, s=scalar, b=in1, o0=op0, o1=op1:
               e.scalar_tensor_tensor(out=d, in0=a, scalar=s, in1=b, op0=o0, op1=o1),
               inc=inc)

    def CP(eng, dst, src, inc=None):
        p.emit(eng, lambda e, d=dst, s=src: e.tensor_copy(out=d, in_=s), inc=inc)

    def RED(eng, dst, src, op, absval=False, neg=False, inc=None):
        p.emit(eng, lambda e, d=dst, s=src, o=op, a=absval, n=neg:
               e.tensor_reduce(out=d, in_=s, axis=AX.X, op=o,
                               apply_absolute_value=a, negate=n), inc=inc)

    def RECIP(dst, src, inc=None):
        p.emit("vector", lambda e, d=dst, s=src: e.reciprocal(out=d, in_=s), inc=inc)

    def NOP(eng, inc):
        p.emit(eng, lambda e: e.nop(), inc=inc)

    def dump(name, src):
        if not DBG(name):
            return
        p.wait("scalar", "VE"); p.wait("scalar", "SE"); p.wait("scalar", "PO")
        p.wait("scalar", "PE")
        p.dma("scalar", dbg_out[name], src, inc="ST")
        for eng in ("vector", "gpsimd", "tensor", "scalar", "sync"):
            p.wait(eng, "ST")

    def SP(n=2):
        for _ in range(n):
            p.emit("vector", lambda e, d=scol("junkV"), s=ccols[:, 7:8]:
                   e.tensor_copy(out=d, in_=s))

    def VEM():
        SP(1)
        p.emit("vector", lambda e, d=scol("junkV"), s=ccols[:, 7:8]:
               e.tensor_copy(out=d, in_=s), inc="VE")

    def phase_barrier():
        p.emit("vector", lambda e, d=scol("junkV"), s=ccols[:, 7:8]:
               e.tensor_copy(out=d, in_=s), inc="BAR")
        p.emit("gpsimd", lambda e, d=scol("junkP"), s=ccols[:, 6:7]:
               e.tensor_copy(out=d, in_=s), inc="BAR")
        p.emit("scalar", lambda e, d=scol("junkS"), s=ccols[:, 7:8]:
               e.activation(out=d, in_=s, func=AF.Copy), inc="BAR")
        p.emit("tensor", lambda e: e.nop(), inc="BAR")
        for _e in ENGS:
            p.wait(_e, "BAR")

    def ACTACC(dump_dst, src, func, acc_final, bias=0.0, scale=1.0):
        ACT(dump_dst, src, func, bias=bias, scale=scale, accum=scol("accraw"))
        ACT(acc_final, scol("accraw"), AF.Copy, inc="SE")

    def lnstats(xt, scra, width=D):
        RED("vector", scol("sx"), xt, ALU.add, inc="VE")
        p.wait("scalar", "VE")
        ACTACC(scra, xt, AF.Square, scol("sq"))
        p.wait("vector", "SE")
        TS("vector", scol("mu"), scol("sx"), 1.0 / width, ALU.mult)
        TS("vector", scol("m2"), scol("sq"), 1.0 / width, ALU.mult)
        SP()
        STT("vector", scol("v2"), scol("mu"), scol("mu"), scol("m2"),
            ALU.mult, ALU.subtract)
        VEM()
        p.wait("scalar", "VE")
        ACT(scol("v1"), scol("v2"), AF.Sqrt, bias=col_eps5, scale=-1.0, inc="SE")
        p.wait("vector", "SE")
        RECIP(scol("rv"), scol("v1"))
        SP()

    wtA = [("1", kc, sc_) for kc in range(16) for sc_ in range(4)] + \
          [("2", kc, 0) for kc in range(64)]
    wtB = list(wtA)
    wstate = {"phase": "A", "i": 0, "po_guard": 0, "ws_guard": [0, 0]}
    wbufs = {}

    def wsrc(t):
        nm, kc, sc_ = t
        if nm == "1":
            return ap["w1t"][128 * kc:128 * (kc + 1), 2048 * sc_:2048 * (sc_ + 1)]
        return ap["w2t"][128 * kc:128 * (kc + 1), :]

    def wprep_step(n=1):
        wsub, wg = wbufs["wsub"], wbufs["wg"]
        for _ in range(n):
            i = wstate["i"]
            if wstate["phase"] == "A":
                if i >= len(wtA):
                    return
                t = wtA[i]
                nm = t[0]
                p.wait("sync", "WVE", wstate["po_guard"])
                p.dma("sync", wsub[:], wsrc(t), inc="WLA")
                p.wait("vector", "WLA", 16 * (i + 1))
                RED("vector", osc1c[:], wsub[:].rearrange("p (a b) -> p a b", b=128),
                    ALU.add, absval=True)
                SP()
                wi = 0 if nm == "1" else 1
                ci = i if nm == "1" else t[1]
                RED("vector", wch[wi][:, ci:ci + 1], osc1c[:], ALU.add)
                p.emit("vector", lambda e, d=scol("junkV"), s=ccols[:, 7:8]:
                       e.tensor_copy(out=d, in_=s), inc="WVE")
                wstate["po_guard"] = p.cnt["WVE"]
                wstate["i"] = i + 1
            else:
                if i >= len(wtB):
                    return
                t = wtB[i]
                nm, kc, sc_ = t
                sl = i % 2
                p.wait("sync", "PO", wstate["po_guard"])
                p.dma("sync", wsub[:], wsrc(t), inc="WLB")
                p.wait("vector", "WLB", 16 * (i + 1))
                if i == 0:
                    p.wait("vector", "WF4", wstate["wf4"])
                p.wait("vector", "WSS", wstate["ws_guard"][sl])
                thr = scol("thr1") if nm == "1" else scol("thr2")
                nthr = scol("nthr1") if nm == "1" else scol("nthr2")
                TS("vector", wg[sl][:], wsub[:], thr, ALU.is_gt)
                TS("vector", wbufs["wg2"][:], wsub[:], nthr, ALU.is_lt)
                TT("vector", wg[sl][:], wg[sl][:], wbufs["wg2"][:], ALU.subtract,
                   inc="PO")
                wstate["po_guard"] = p.cnt["PO"]
                p.wait("scalar", "PO")
                if nm == "1":
                    dst = w1q[16 * sc_:16 * (sc_ + 1), 128 * kc:128 * (kc + 1), :]
                else:
                    dst = w2q[:, 128 * kc:128 * (kc + 1), :]
                p.dma("scalar", dst.rearrange("a i j -> i a j"),
                      wg[sl][:].rearrange("i (a j) -> i a j", j=128), inc="WSS")
                wstate["ws_guard"][sl] = p.cnt["WSS"]
                if DBG("tern1") and nm == "1" and kc == 0 and sc_ == 0:
                    p.dma("scalar", dbg_out["tern1"], wg[sl][:], inc="ST")
                wstate["i"] = i + 1

    def wprep_finalize():
        assert wstate["phase"] == "A" and wstate["i"] >= len(wtA)
        b1_ = _statmap["w1s"][0]
        assert _statmap["w2s"][0] == b1_ + 1
        p.wait("vector", "WVE")
        for wi in range(2):
            RED("vector", osc2c[:, 0:8], wch[wi][:].rearrange("p (a b) -> p a b", b=8),
                ALU.add)
            SP()
            RED("vector", stats[:, b1_ + wi:b1_ + wi + 1], osc2c[:, 0:8], ALU.add)
            SP()
        p.emit("vector", lambda e, d=scol("junkV"), s=ccols[:, 7:8]:
               e.tensor_copy(out=d, in_=s), inc="WVE")
        p.wait("scalar", "WVE")
        p.emit("scalar", lambda e, o=rowscr[8:10, :].rearrange("c i -> i c"),
               i_=stats[:, b1_:b1_ + 2]:
               e.dma_start(out=o, in_=i_, allow_slow_non_contiguous=True),
               inc="WF", n=16)
        p.wait("sync", "WF")
        p.dma("sync", mstage[0:2, :], rowscr[8:10, :], inc="WF2")
        p.wait("vector", "WF2")
        RED("vector", osc2c[0:2, 0:8], mstage[0:2, :].rearrange("p (a b) -> p a b", b=16),
            ALU.add)
        SP()
        RED("vector", scol2("wsum2"), osc2c[0:2, 0:8], ALU.add)
        SP()
        TS("vector", wder[0:2, 0:1], scol2("wsum2"), 1.0 / (2048.0 * 8192.0),
           ALU.mult, 1e-5, ALU.max)
        SP()
        TS("vector", wder[0:2, 1:2], wder[0:2, 0:1], 0.5, ALU.mult)
        TS("vector", wder[0:2, 2:3], wder[0:2, 0:1], 1.0 / 127.0, ALU.mult)
        SP()
        TS("vector", wder[0:2, 3:4], wder[0:2, 1:2], -1.0, ALU.mult)
        p.emit("vector", lambda e, d=scol("junkV"), s=ccols[:, 7:8]:
               e.tensor_copy(out=d, in_=s), inc="VE")
        p.wait("scalar", "VE")
        p.emit("scalar", lambda e, o=rowscr[10:12, 0:4], i_=wder[:]:
               e.dma_start(out=o, in_=i_), inc="WF3", n=16)
        p.wait("sync", "WF3")
        for col, (r_, c_) in (("thr1", (10, 1)), ("thr2", (11, 1)),
                              ("m127_1", (10, 2)), ("m127_2", (11, 2)),
                              ("nthr1", (10, 3)), ("nthr2", (11, 3))):
            p.dma("sync", scol(col),
                  rowscr[r_:r_ + 1, c_:c_ + 1].broadcast_to([128, 1]), inc="WF4")
        wstate["wf4"] = p.cnt["WF4"]
        wstate["po_guard"] = p.cnt.get("PO", 0)
        wstate["phase"] = "B"
        wstate["i"] = 0

    with ExitStack() as st0:
        IP = st0.enter_context(nc.psum_tensor("IP", [128, 2048], F32))
        ist = st0.enter_context(nc.sbuf_tensor("ist", [128, 512], F32))
        irow = st0.enter_context(nc.sbuf_tensor("irow", [1, 2048], F32))
        iw64 = st0.enter_context(nc.sbuf_tensor("iw64", [1, 64], F32))
        iu = st0.enter_context(nc.sbuf_tensor("iu", [64, 2048], F32))
        iup = st0.enter_context(nc.sbuf_tensor("iup", [64, 2048], F32))
        isA = st0.enter_context(nc.sbuf_tensor("isA", [64, 2048], F32))
        isB = st0.enter_context(nc.sbuf_tensor("isB", [64, 2048], F32))
        isI = st0.enter_context(nc.sbuf_tensor("isI", [64, 2048], I32))
        for nm, dst in [("c_ident", ident[:]), ("c_identb", identb[:]),
                        ("c_mask", maskt[:]), ("c_rot", rotm[:]),
                        ("c_cols", ccols[:]), ("c_onescol", onescol[:]),
                        ("c_onescolb", onescolb[:]), ("c_onesrow", onesrow[:]),
                        ("c_iota", irow[:])]:
            p.dma("sync", dst, ap[nm], inc="LD")
        p.dma("sync", ist[0:12, 0:128], ap["qlnw"], inc="LD")
        p.dma("sync", ist[0:4, 128:256], ap["kvlnw"], inc="LD")
        p.dma("sync", ist[0:64, 256:384], ap["b1"], inc="LD")
        p.dma("sync", ist[0:16, 384:512], ap["b2"], inc="LD")
        p.wait("tensor", "LD")
        TR(IP[:, 0:12], ist[0:12, 0:128], ident[0:12, 0:12])
        TR(IP[:, 12:16], ist[0:4, 128:256], ident[0:4, 0:4])
        TR(IP[:, 16:80], ist[0:64, 256:384], ident[0:64, 0:64])
        TR(IP[:, 80:96], ist[0:16, 384:512], ident[0:16, 0:16], inc="PE")
        p.wait("scalar", "PE")
        ACT(qlnwc[:], IP[:, 0:12], AF.Copy)
        ACT(kvlnwc[:], IP[:, 12:16], AF.Copy)
        ACT(b1c[:], IP[:, 16:80], AF.Copy)
        ACT(b2c[:], IP[:, 80:96], AF.Copy, inc="SE")
        ACT(iw64[0:1, 0:32], irow[0:1, 0:32], AF.Exp,
            bias=col_l2pi[0:1, :], scale=-math.log(10000.0) / 32.0)
        ACT(iw64[0:1, 32:64], irow[0:1, 0:32], AF.Exp,
            bias=col_l2pi[0:1, :], scale=-math.log(10000.0) / 32.0, inc="SE")
        p.wait("tensor", "SE")
        for sl in range(4):
            MM(IP[0:64, 512 * sl:512 * (sl + 1)], iw64[:],
               irow[0:1, 512 * sl:512 * (sl + 1)], True, True,
               inc=("PE" if sl == 3 else None))
        p.wait("vector", "PE")
        CP("vector", iu[:], IP[0:64, :])
        TS("vector", iup[:], IP[0:64, :], col_q[0:64, :], ALU.add, inc="VE")
        def rope_half(u_ap, dstrow):
            TS("vector", isI[:], u_ap, 1.0, ALU.mult)
            TS("vector", isB[:], isI[:], 0.0, ALU.add)
            TT("vector", isA[:], u_ap, isB[:], ALU.subtract)
            TS("vector", isB[:], isA[:], col_h[0:64, :], ALU.is_gt)
            TT("vector", isA[:], isA[:], isB[:], ALU.subtract)
            TS("vector", isB[:], isA[:], col_nh[0:64, :], ALU.is_lt)
            TT("vector", isA[:], isA[:], isB[:], ALU.add, inc="VE")
            p.wait("scalar", "VE")
            ACT(dstrow, isA[:], AF.Sin, scale=2.0 * math.pi, inc="SE")
        rope_half(iu[:], cs_pack[64:128, :])
        rope_half(iup[:], cs_pack[0:64, :])
        p.wait("sync", "SE")
        p.wait("gpsimd", "SE")
    dump("cs_pack", cs_pack[:])

    stk_w = ExitStack(); p.open_stacks.append(stk_w)
    wbufs["wsub"] = stk_w.enter_context(nc.sbuf_tensor("wsub", [128, 2048], F32, side="left"))
    wbufs["wg"] = [stk_w.enter_context(nc.sbuf_tensor("wgA", [128, 2048], BF16, side="left")),
                   stk_w.enter_context(nc.sbuf_tensor("wgB", [128, 2048], BF16, side="left"))]
    wbufs["wg2"] = stk_w.enter_context(nc.sbuf_tensor("wg2", [128, 2048], BF16, side="left"))
    stk_qa = ExitStack(); p.open_stacks.append(stk_qa)
    q_a = stk_qa.enter_context(nc.sbuf_tensor("q_a", [128, 12 * 2048], BF16, side="left"))
    ckv_n = stk_qa.enter_context(nc.sbuf_tensor("ckv_n", [128, 4 * 2048], BF16, side="left"))
    stk_lnx = ExitStack(); p.open_stacks.append(stk_lnx)
    lnx = stk_lnx.enter_context(nc.sbuf_tensor("lnx", [128, 16 * 2048], BF16, side="right"))

    with ExitStack() as stA:
        PTa = stA.enter_context(nc.psum_tensor("PTaA", [128, 1024], BF16))
        PTb = stA.enter_context(nc.psum_tensor("PTbA", [128, 1024], BF16))
        xstg = [stA.enter_context(nc.sbuf_tensor(f"xstgA{i}", [128, 2048], F32))
                for i in range(2)]
        lnts = stA.enter_context(nc.sbuf_tensor("lnts", [128, 2048], BF16))
        scra = stA.enter_context(nc.sbuf_tensor("scraA", [128, 2048], F32))

        xa_guard = [0, 0]
        pt_guard = [0, 0]
        tr_done = 0
        for T in range(NT):
            sl = T % 2
            semx = "XLA" if sl == 0 else "XLB"
            p.wait("sync", "VE", xa_guard[sl])
            p.dma("sync", xstg[sl][:], ap["x"][128 * T:128 * (T + 1), :], inc=semx)
            p.wait("vector", semx)
            xt = xstg[sl][:]
            lnstats(xt, scra[:])
            p.wait("vector", "PE", tr_done)
            TS("vector", lnts[:], xt, scol("mu"), ALU.subtract, scol("rv"), ALU.mult,
               inc="VE")
            xa_guard[sl] = p.cnt["VE"]
            p.wait("tensor", "VE")
            for g, PT in ((0, PTa), (1, PTb)):
                p.wait("tensor", "SE", pt_guard[g])
                for c in range(8):
                    cc = 8 * g + c
                    TR(PT[:, 128 * c:128 * (c + 1)], lnts[:, 128 * cc:128 * (cc + 1)],
                       identb[:], inc=("PE" if c == 7 else None))
                p.wait("scalar", "PE")
                dst = lnx[:].rearrange("i (a t) -> i a t", t=2048)[
                    :, 8 * g:8 * (g + 1), 128 * T:128 * (T + 1)]
                ACT(dst, PT[:].rearrange("i (a t) -> i a t", t=128), AF.Copy, inc="SE")
                pt_guard[g] = p.cnt["SE"]
            tr_done = p.cnt["PE"]
            wprep_step(4)
    dump("lnx", lnx[:])
    if stop_at <= 0:
        return _finish(p, nc, dbg_out, out, early=True)

    phase_barrier()
    with ExitStack() as stB:
        PBm = stB.enter_context(nc.psum_tensor("PBmB", [128, 2048], F32))
        Q2 = stB.enter_context(nc.psum_tensor("Q2B", [1, 2048], F32))
        wst = stB.enter_context(nc.sbuf_tensor("wstB", [128, 2048], F32))
        wbf = [stB.enter_context(nc.sbuf_tensor(f"wbfB{i}", [128, 2048], BF16))
               for i in range(2)]
        sqb = wbufs["wg2"]
        rrow = stB.enter_context(nc.sbuf_tensor("rrowB", [1, 2048], F32))

        p.wait("tensor", "VE")
        p.wait("tensor", "SE")
        cast_guard = [0, 0]
        mm_done = [0, 0]
        evac_guard = 0
        for m in range(12):
            sl = m % 2
            p.wait("sync", "POB", cast_guard[1 - sl])
            p.dma("sync", wst[:].rearrange("p (a j) -> p a j", j=128),
                  ap["wqa"][:, 128 * m:128 * (m + 1)].rearrange("(a p) j -> p a j", p=128),
                  inc="WLQ")
            p.wait("scalar", "WLQ", 16 * (m + 1))
            p.wait("scalar", "PE", mm_done[sl])
            ACT(wbf[sl][:], wst[:], AF.Copy, inc="POB")
            cast_guard[sl] = p.cnt["POB"]
            p.wait("tensor", "POB", cast_guard[sl])
            p.wait("tensor", "SE", evac_guard)
            for kc in range(16):
                for s4 in range(4):
                    MM(PBm[:, 512 * s4:512 * (s4 + 1)],
                       wbf[sl][:, 128 * kc:128 * (kc + 1)],
                       lnx[:, 2048 * kc + 512 * s4:2048 * kc + 512 * (s4 + 1)],
                       kc == 0, kc == 15,
                       inc=("PE" if (kc == 15 and s4 == 3) else None))
            mm_done[sl] = p.cnt["PE"]
            p.wait("scalar", "PE")
            ACT(q_a[:, 2048 * m:2048 * (m + 1)], PBm[:], AF.Copy)
            ACT(sqb[:], PBm[:], AF.Square, inc="SE")
            evac_guard = p.cnt["SE"]
            p.wait("tensor", "SE")
            for s4 in range(4):
                MM(Q2[0:1, 512 * s4:512 * (s4 + 1)], onescolb[:],
                   sqb[:, 512 * s4:512 * (s4 + 1)], m == 0, m == 11,
                   inc=("PE" if (m == 11 and s4 == 3) else None))
            wprep_step(3)
        p.wait("vector", "PE")
        TS("vector", rrow[:], Q2[0:1, :], 1.0 / QLORA, ALU.mult)
        VEM()
        p.wait("scalar", "VE")
        ACT(rrow[:], rrow[:], AF.Sqrt, bias=col_eps6[0:1, :], inc="SE")
        p.wait("vector", "SE")
        RECIP(rrow[:], rrow[:])
        VEM()
        p.wait("tensor", "VE")
        for qi in range(16):
            TR(PBm[:, 1024 + qi:1024 + qi + 1], rrow[0:1, 128 * qi:128 * (qi + 1)],
               ident[0:1, 0:1], inc=("PE" if qi == 15 else None))
        p.wait("vector", "PE")
        TS("vector", escale[:], PBm[:, 1024:1040], SC192, ALU.mult)
        VEM()
    dump("q_a", q_a[:])
    dump("escale", escale[:])
    if stop_at <= 1:
        return _finish(p, nc, dbg_out, out, early=True)

    phase_barrier()
    with ExitStack() as stC:
        PBm = stC.enter_context(nc.psum_tensor("PBmC", [128, 2048], F32))
        Q2 = stC.enter_context(nc.psum_tensor("Q2C", [1, 2048], F32))
        wst = stC.enter_context(nc.sbuf_tensor("wstC", [128, 2048], F32))
        wbf = [stC.enter_context(nc.sbuf_tensor(f"wbfC{i}", [128, 2048], BF16))
               for i in range(2)]
        sqb = wbufs["wg2"]
        rrow = stC.enter_context(nc.sbuf_tensor("rrowC", [1, 2048], F32))
        rbc = stC.enter_context(nc.sbuf_tensor("rbcC", [128, 2048], F32))
        kpebf = stC.enter_context(nc.sbuf_tensor("kpebf", [64, 2048], BF16))
        scrc = stC.enter_context(nc.sbuf_tensor("scrC", [64, 2048], F32))

        p.wait("tensor", "VE")
        p.wait("tensor", "SE")
        cast_guard = [0, 0]
        mm_doneC = [0, 0]
        evac_guard = 0
        for m in range(5):
            ncols = 128 if m < 4 else 64
            sl = m % 2
            p.wait("sync", "POC", cast_guard[1 - sl])
            p.dma("sync", wst[:, 0:16 * ncols].rearrange("p (a j) -> p a j", a=16),
                  ap["wkva"][:, 128 * m:128 * m + ncols].rearrange(
                      "(a p) j -> p a j", p=128), inc="WLC")
            p.wait("scalar", "WLC", 16 * (m + 1))
            p.wait("scalar", "PE", mm_doneC[sl])
            ACT(wbf[sl][:, 0:16 * ncols], wst[:, 0:16 * ncols], AF.Copy, inc="POC")
            cast_guard[sl] = p.cnt["POC"]
            p.wait("tensor", "POC", cast_guard[sl])
            p.wait("tensor", "SE", evac_guard)
            for kc in range(16):
                for s4 in range(4):
                    MM(PBm[0:ncols, 512 * s4:512 * (s4 + 1)],
                       wbf[sl][:, ncols * kc:ncols * (kc + 1)],
                       lnx[:, 2048 * kc + 512 * s4:2048 * kc + 512 * (s4 + 1)],
                       kc == 0, kc == 15,
                       inc=("PE" if (kc == 15 and s4 == 3) else None))
            mm_doneC[sl] = p.cnt["PE"]
            p.wait("scalar", "PE")
            if m < 4:
                ACT(ckv_n[:, 2048 * m:2048 * (m + 1)], PBm[:], AF.Copy)
                ACT(sqb[:], PBm[:], AF.Square, inc="SE")
                evac_guard = p.cnt["SE"]
                p.wait("tensor", "SE")
                for s4 in range(4):
                    MM(Q2[0:1, 512 * s4:512 * (s4 + 1)], onescolb[:],
                       sqb[:, 512 * s4:512 * (s4 + 1)], m == 0, m == 3,
                       inc=("PE" if (m == 3 and s4 == 3) else None))
            else:
                ACT(kpebf[:], PBm[0:64, :], AF.Copy, inc="SE")
                evac_guard = p.cnt["SE"]
            wprep_step(3)
        p.wait("vector", "PE")
        TS("vector", rrow[:], Q2[0:1, :], 1.0 / 512.0, ALU.mult)
        VEM()
        p.wait("scalar", "VE")
        ACT(rrow[:], rrow[:], AF.Sqrt, bias=col_eps6[0:1, :], inc="SE")
        p.wait("vector", "SE")
        RECIP(rrow[:], rrow[:])
        VEM()
        p.wait("tensor", "VE")
        p.wait("tensor", "SE")
        for s4 in range(4):
            MM(PBm[:, 512 * s4:512 * (s4 + 1)], onesrow[:],
               rrow[0:1, 512 * s4:512 * (s4 + 1)], True, True,
               inc=("PE" if s4 == 3 else None))
        p.wait("vector", "PE")
        CP("vector", rbc[:], PBm[:], inc="VE")
        for c in range(4):
            TT("vector", ckv_n[:, 2048 * c:2048 * (c + 1)],
               ckv_n[:, 2048 * c:2048 * (c + 1)], rbc[:], ALU.mult,
               inc=("VE" if c == 3 else None))
        p.wait("tensor", "SE")
        p.wait("tensor", "VE")
        for s4 in range(4):
            MM(PBm[0:64, 512 * s4:512 * (s4 + 1)], rotm[:],
               kpebf[:, 512 * s4:512 * (s4 + 1)], True, True,
               inc=("PE" if s4 == 3 else None))
        p.wait("vector", "PE")
        TT("vector", scrc[:], PBm[0:64, :], cs_pack[64:128, :], ALU.mult)
        TT("vector", kpe_r[:], kpebf[:], cs_pack[0:64, :], ALU.mult)
        TT("vector", kpe_r[:], kpe_r[:], scrc[:], ALU.add, inc="VE")
        p.wait("gpsimd", "VE")
        p.wait("sync", "VE")
    dump("ckv_n", ckv_n[:])
    dump("kpe_r", kpe_r[:])

    while wstate["phase"] == "A" and wstate["i"] < len(wtA):
        wprep_step(4)
    wprep_finalize()
    if stop_at <= 2:
        return _finish(p, nc, dbg_out, out, early=True)

    phase_barrier()
    stk_lnx.close(); p.open_stacks.remove(stk_lnx)
    stk_ao = ExitStack(); p.open_stacks.append(stk_ao)
    attnout = stk_ao.enter_context(
        nc.sbuf_tensor("attnout", [128, 16 * 2048], BF16, side="right"))

    with ExitStack() as stD:
        PB = stD.enter_context(nc.psum_tensor("PBD", [128, 2048], F32))
        PTa = stD.enter_context(nc.psum_tensor("PTaD", [128, 1024], BF16))
        PTb = stD.enter_context(nc.psum_tensor("PTbD", [128, 1024], BF16))
        PO1 = stD.enter_context(nc.psum_tensor("PO1D", [128, 512], F32))
        PO2 = stD.enter_context(nc.psum_tensor("PO2D", [128, 512], F32))
        qnope = stD.enter_context(nc.sbuf_tensor("qnope", [128, 2048], BF16))
        qper = stD.enter_context(nc.sbuf_tensor("qper", [64, 2048], BF16))
        knope = stD.enter_context(nc.sbuf_tensor("knope", [128, 2048], BF16))
        vtm = stD.enter_context(nc.sbuf_tensor("vtm", [128, 2048], BF16))
        psb = stD.enter_context(nc.sbuf_tensor("psb", [128, 2048], BF16))
        ptsb = stD.enter_context(nc.sbuf_tensor("ptsb", [128, 2048], BF16))
        scrd = stD.enter_context(nc.sbuf_tensor("scrD", [64, 2048], F32))
        wqbst = stD.enter_context(nc.sbuf_tensor("wqbst", [128, 256], F32))
        wqbbf = [stD.enter_context(nc.sbuf_tensor(f"wqbbf{i}", [128, 12 * 192], BF16))
                 for i in range(2)]
        wkvbbf = [stD.enter_context(nc.sbuf_tensor(f"wkvbbf{i}", [128, 1024], BF16))
                  for i in range(2)]

        head_ready = {}
        hw_state = {"POq": 0}

        def head_weight_step(h):
            sl = h % 2
            kc = hw_state.setdefault(("kc", h), 0)
            if kc >= 16:
                return
            p.wait("sync", "POq", hw_state["POq"])
            if kc < 12:
                p.dma("sync", wqbst[:, 0:192],
                      ap["wqb"][128 * kc:128 * (kc + 1), 192 * h:192 * (h + 1)],
                      inc="WQ")
                p.wait("scalar", "WQ")
                if kc == 0:
                    p.wait("scalar", "PE", p.cnt["PE"])
                ACT(wqbbf[sl][:, 192 * kc:192 * (kc + 1)],
                    wqbst[:, 0:192], AF.Copy, scale=qlnwc[:, kc:kc + 1], inc="POq")
            else:
                k4 = kc - 12
                p.dma("sync", wqbst[:, 0:256],
                      ap["wkvb"][128 * k4:128 * (k4 + 1), 256 * h:256 * (h + 1)],
                      inc="WQ")
                p.wait("scalar", "WQ")
                ACT(wkvbbf[sl][:, 256 * k4:256 * (k4 + 1)],
                    wqbst[:, 0:256], AF.Copy, scale=kvlnwc[:, k4:k4 + 1], inc="POq")
            hw_state["POq"] = p.cnt["POq"]
            hw_state[("kc", h)] = kc + 1
            head_ready[h] = p.cnt["POq"]

        def load_head_weights(h):
            for _ in range(16):
                head_weight_step(h)

        p.wait("tensor", "VE")
        p.wait("tensor", "SE")
        load_head_weights(0)
        pb_se = 0
        pt_se = 0
        out_ve = [0, 0]
        vev = [0, 0]

        for h in range(NH):
            sl = h % 2
            wbf = wqbbf[sl]
            kvbf = wkvbbf[sl]
            p.wait("tensor", "POq", head_ready[h])

            p.wait("tensor", "SE", pb_se)
            for kc in range(12):
                for s4 in range(4):
                    MM(PB[:, 512 * s4:512 * (s4 + 1)],
                       wbf[:, 192 * kc:192 * kc + 128],
                       q_a[:, 2048 * kc + 512 * s4:2048 * kc + 512 * (s4 + 1)],
                       kc == 0, kc == 11,
                       inc=("PE" if (kc == 11 and s4 == 3) else None))
            p.wait("scalar", "PE")
            ACT(qnope[:], PB[:], AF.Copy, inc="SE")
            pb_se = p.cnt["SE"]
            p.wait("tensor", "SE", pb_se)
            for kc in range(12):
                for s4 in range(4):
                    MM(PB[0:64, 512 * s4:512 * (s4 + 1)],
                       wbf[:, 192 * kc + 128:192 * (kc + 1)],
                       q_a[:, 2048 * kc + 512 * s4:2048 * kc + 512 * (s4 + 1)],
                       kc == 0, kc == 11,
                       inc=("PE" if (kc == 11 and s4 == 3) else None))
            p.wait("scalar", "PE")
            ACT(qper[:], PB[0:64, :], AF.Copy, inc="SE")
            pb_se = p.cnt["SE"]
            p.wait("tensor", "SE", pb_se)
            for s4 in range(4):
                MM(PB[0:64, 512 * s4:512 * (s4 + 1)], rotm[:],
                   qper[:, 512 * s4:512 * (s4 + 1)], True, True,
                   inc=("PE" if s4 == 3 else None))
            p.wait("vector", "PE")
            TT("vector", scrd[:], PB[0:64, :], cs_pack[64:128, :], ALU.mult)
            TT("vector", qper[:], qper[:], cs_pack[0:64, :], ALU.mult)
            TT("vector", qper[:], qper[:], scrd[:], ALU.add, inc="VE")
            pb_ve = p.cnt["VE"]
            p.wait("tensor", "VE", pb_ve)
            for kc in range(4):
                for s4 in range(4):
                    MM(PB[:, 512 * s4:512 * (s4 + 1)],
                       kvbf[:, 256 * kc:256 * kc + 128],
                       ckv_n[:, 2048 * kc + 512 * s4:2048 * kc + 512 * (s4 + 1)],
                       kc == 0, kc == 3,
                       inc=("PE" if (kc == 3 and s4 == 3) else None))
            p.wait("scalar", "PE")
            ACT(knope[:], PB[:], AF.Copy, inc="SE")
            pb_se = p.cnt["SE"]
            for T in range(NT):
                POx = PO1 if T % 2 == 0 else PO2
                p.wait("tensor", "SE", vev[T % 2])
                p.wait("tensor", "VE", out_ve[T % 2])
                for kc in range(4):
                    MM(POx[:, 0:128],
                       ckv_n[:, 2048 * kc + 128 * T:2048 * kc + 128 * (T + 1)],
                       kvbf[:, 256 * kc + 128:256 * (kc + 1)],
                       kc == 0, kc == 3, inc=("PE" if kc == 3 else None))
                p.wait("scalar", "PE")
                ACT(vtm[:, 128 * T:128 * (T + 1)], POx[:, 0:128], AF.Copy, inc="SE")
                vev[T % 2] = p.cnt["SE"]
            for qi in range(NT):
                if h + 1 < NH:
                    head_weight_step(h + 1)
                Eext = 128 * (qi + 1)
                nsl = (Eext + 511) // 512
                p.wait("tensor", "SE", pb_se)
                for s4 in range(nsl):
                    w = min(512, Eext - 512 * s4)
                    MM(PB[:, 512 * s4:512 * s4 + w], qnope[:, 128 * qi:128 * (qi + 1)],
                       knope[:, 512 * s4:512 * s4 + w], True, False)
                for s4 in range(nsl):
                    w = min(512, Eext - 512 * s4)
                    MM(PB[:, 512 * s4:512 * s4 + w], qper[:, 128 * qi:128 * (qi + 1)],
                       kpe_r[:, 512 * s4:512 * s4 + w], False, True,
                       inc=("PE" if s4 == nsl - 1 else None))
                p.wait("vector", "PE")
                TT("vector", PB[:, 128 * qi:128 * (qi + 1)],
                   PB[:, 128 * qi:128 * (qi + 1)], maskt[:], ALU.add)
                RED("vector", scol("ngm"), PB[:, 0:Eext], ALU.max, neg=True)
                SP()
                TT("vector", scol("ngs"), scol("ngm"), escale[:, qi:qi + 1],
                   ALU.mult)
                VEM()
                p.wait("scalar", "VE")
                ACT(psb[:, 0:Eext], PB[:, 0:Eext], AF.Exp,
                    bias=scol("ngs"), scale=escale[:, qi:qi + 1],
                    accum=scol("accraw"))
                ACT(scol("ell"), scol("accraw"), AF.Copy, inc="SE")
                pb_se = p.cnt["SE"]
                p.wait("vector", "SE")
                RECIP(scol("linv"), scol("ell"))
                SP()
                TS("vector", psb[:, 0:Eext], psb[:, 0:Eext], scol("linv"),
                   ALU.mult, inc="VE")
                p.wait("tensor", "VE")
                p.wait("tensor", "SE", pt_se)
                for j in range(qi + 1):
                    PT = PTa if j < 8 else PTb
                    TR(PT[:, 128 * (j % 8):128 * (j % 8) + 128],
                       psb[:, 128 * j:128 * (j + 1)], identb[:],
                       inc=("PE" if j == qi else None))
                p.wait("scalar", "PE")
                ACT(ptsb[:, 0:min(Eext, 1024)], PTa[:, 0:min(Eext, 1024)], AF.Copy,
                    inc=("SE" if qi < 8 else None))
                if qi >= 8:
                    ACT(ptsb[:, 1024:Eext], PTb[:, 0:Eext - 1024], AF.Copy, inc="SE")
                pt_se = p.cnt["SE"]
                POx = PO1 if qi % 2 == 0 else PO2
                p.wait("tensor", "SE", pt_se)
                p.wait("tensor", "VE", out_ve[qi % 2])
                for j in range(qi + 1):
                    MM(POx[:, 0:128], vtm[:, 128 * j:128 * (j + 1)],
                       ptsb[:, 128 * j:128 * (j + 1)], j == 0, j == qi,
                       inc=("PE" if j == qi else None))
                p.wait("vector", "PE")
                CP("vector",
                   attnout[:, 2048 * h + 128 * qi:2048 * h + 128 * (qi + 1)],
                   POx[:, 0:128], inc="VE")
                out_ve[qi % 2] = p.cnt["VE"]
                if qi % 4 == 3:
                    wprep_step(1)
    dump("attnout", attnout[:])
    if stop_at <= 3:
        return _finish(p, nc, dbg_out, out, early=True)

    phase_barrier()
    stk_qa.close(); p.open_stacks.remove(stk_qa)

    with ExitStack() as stE:
        PB = stE.enter_context(nc.psum_tensor("PBE", [128, 2048], F32))
        PTa = stE.enter_context(nc.psum_tensor("PTaE", [128, 1024], BF16))
        PTb = stE.enter_context(nc.psum_tensor("PTbE", [128, 1024], BF16))
        wo_bf = stE.enter_context(nc.sbuf_tensor("wo_bf", [128, 16 * 2048], BF16))
        xstg1 = stE.enter_context(nc.sbuf_tensor("xstgE", [128, 2048], F32))
        xstg = [xstg1, xstg1]
        x2t = stE.enter_context(nc.sbuf_tensor("x2t", [128, 2048], F32))
        segA = stE.enter_context(nc.sbuf_tensor("segAE", [128, 2048], F32))
        segB = stE.enter_context(nc.sbuf_tensor("segBE", [128, 2048], F32))
        segI = stE.enter_context(nc.sbuf_tensor("segIE", [128, 2048], I32))
        i1tm = stE.enter_context(nc.sbuf_tensor("i1tm", [128, 2048], BF16))
        i1st = stE.enter_context(nc.sbuf_tensor("i1st", [128, 2048], BF16))

        p.wait("tensor", "VE")
        p.wait("tensor", "SE")
        p.wait("vector", "WF4", p.cnt.get("WF4", 0))
        wost = segB
        wo_cast = []
        for c in range(16):
            p.wait("sync", "POw", p.cnt.get("POw", 0))
            p.dma("sync", wost[:], ap["wo"][128 * c:128 * (c + 1), :], inc="WOL")
            p.wait("scalar", "WOL", 16 * (c + 1))
            ACT(wo_bf[:, 2048 * c:2048 * (c + 1)], wost[:], AF.Copy, inc="POw")
            wo_cast.append(p.cnt["POw"])
            wprep_step(2)

        xe_guard = [0, 0]
        pb_free = 0
        pt_guard = [0, 0]
        for T in range(NT):
            sl = 0
            semx = "XEA"
            p.wait("sync", "VE", xe_guard[sl])
            p.dma("sync", xstg[sl][:], ap["x"][128 * T:128 * (T + 1), :], inc=semx)
            p.wait("tensor", "VE", pb_free)
            if T == 0:
                p.wait("tensor", "POw", wo_cast[15])
            for c in range(16):
                for s4 in range(4):
                    MM(PB[:, 512 * s4:512 * (s4 + 1)],
                       attnout[:, 2048 * c + 128 * T:2048 * c + 128 * (T + 1)],
                       wo_bf[:, 2048 * c + 512 * s4:2048 * c + 512 * (s4 + 1)],
                       c == 0, c == 15,
                       inc=("PE" if (c == 15 and s4 == 3) else None))
            p.wait("vector", semx)
            p.wait("vector", "PE")
            p.wait("vector", "X2S", p.cnt.get("X2S", 0))
            TT("vector", x2t[:], PB[:], xstg[sl][:], ALU.add, inc="VE")
            pb_free = p.cnt["VE"]
            xe_guard[sl] = p.cnt["VE"]
            p.wait("scalar", "VE")
            p.dma("scalar", x2s[128 * T:128 * (T + 1), :], x2t[:], inc="X2S")
            if DBG("x2"):
                p.dma("scalar", dbg_out["x2"][128 * T:128 * (T + 1), :], x2t[:],
                      inc="ST")
            lnstats(x2t[:], segA[:])
            TS("vector", segA[:], x2t[:], scol("mu"), ALU.subtract, scol("rv"),
               ALU.mult, inc="VE")
            p.wait("scalar", "VE")
            ACTACC(segB[:], segA[:], AF.Square, scol("q2"))
            p.wait("vector", "SE")
            TS("vector", scol("m2l"), scol("q2"), 1.0 / D, ALU.mult)
            VEM()
            p.wait("scalar", "VE")
            ACT(scol("sdl"), scol("m2l"), AF.Sqrt, bias=col_eps5, inc="SE")
            p.wait("vector", "SE")
            RECIP(scol("r2"), scol("sdl"))
            RED("vector", scol("mx"), segA[:], ALU.max, absval=True)
            SP()
            TT("vector", scol("rm"), scol("r2"), scol("mx"), ALU.mult)
            SP()
            TS("vector", scol("rmc"), scol("rm"), 1e-5, ALU.max)
            SP()
            RECIP(scol("irmc"), scol("rmc"))
            SP()
            TS("vector", osc1c[:, T:T + 1], scol("rmc"), scol("m127_1"), ALU.mult)
            STT("vector", scol("c1"), scol("irmc"), 127.0, scol("r2"),
                ALU.mult, ALU.mult)
            SP()
            TS("vector", segI[:], segA[:], scol("c1"), ALU.mult)
            CP("vector", i1tm[:], segI[:])
            STT("vector", segB[:], segA[:], scol("c1"), i1tm[:], ALU.mult,
                ALU.subtract)
            TS("vector", segA[:], segB[:], col_h, ALU.is_gt)
            TS("vector", segB[:], segB[:], col_nh, ALU.is_lt)
            TT("vector", segA[:], segA[:], segB[:], ALU.subtract)
            TT("vector", i1tm[:], i1tm[:], segA[:], ALU.add, inc="VE")
            p.wait("tensor", "VE")
            for g, PT in ((0, PTa), (1, PTb)):
                p.wait("tensor", "SE", pt_guard[g])
                for c in range(8):
                    cc = 8 * g + c
                    TR(PT[:, 128 * c:128 * (c + 1)], i1tm[:, 128 * cc:128 * (cc + 1)],
                       identb[:], inc=("PE" if c == 7 else None))
                p.wait("scalar", "PE")
                p.wait("scalar", "I1S", p.cnt.get("I1S", 0))
                ACT(i1st[:, 1024 * g:1024 * (g + 1)], PT[:], AF.Copy, inc="SE")
                pt_guard[g] = p.cnt["SE"]
            p.wait("scalar", "SE")
            p.dma("scalar",
                  int1s.rearrange("(c i) t -> i c t", i=128)[:, :, 128 * T:128 * (T + 1)],
                  i1st[:].rearrange("i (c t) -> i c t", t=128), inc="I1S")
            wprep_step(3)
    dump("osc1", osc1c[:])
    if DBG("int1"):
        pass
    if stop_at <= 4:
        return _finish(p, nc, dbg_out, out, early=True)

    while wstate["i"] < len(wtB):
        wprep_step(4)
    phase_barrier()
    stk_w.close(); p.open_stacks.remove(stk_w)
    stk_ao.close(); p.open_stacks.remove(stk_ao)
    stk_cs.close()

    with ExitStack() as stF:
        PB = stF.enter_context(nc.psum_tensor("PBF", [128, 2048], F32))
        PTa = stF.enter_context(nc.psum_tensor("PTaF", [128, 1024], BF16))
        PTb = stF.enter_context(nc.psum_tensor("PTbF", [128, 1024], BF16))
        PH = stF.enter_context(nc.psum_tensor("PHF", [128, 1024], F32))
        hbuf = stF.enter_context(nc.sbuf_tensor("hbuf", [128, 64 * QW], BF16))
        int1h = stF.enter_context(nc.sbuf_tensor("int1h", [128, 16 * QW], BF16))
        big = [stF.enter_context(nc.sbuf_tensor(f"bigF{i}", [128, INNER], BF16))
               for i in range(2)]
        htm = stF.enter_context(nc.sbuf_tensor("htmF", [128, INNER], BF16))
        zbuf = stF.enter_context(nc.sbuf_tensor("zbuf", [128, INNER], BF16))
        segA = stF.enter_context(nc.sbuf_tensor("segAF", [128, 2048], F32))
        segB = stF.enter_context(nc.sbuf_tensor("segBF", [128, 2048], F32))
        segI = stF.enter_context(nc.sbuf_tensor("segIF", [128, 2048], I32))
        w1sb = [stF.enter_context(nc.sbuf_tensor(f"w1sb{i}", [128, 2048], BF16))
                for i in range(2)]
        s1b = stF.enter_context(nc.sbuf_tensor("s1b", [128, QW], F32))
        out2 = stF.enter_context(nc.sbuf_tensor("out2F", [128, 16 * QW], BF16))
        x3t = segA
        b2bc = stF.enter_context(nc.sbuf_tensor("b2bc", [128, 2048], BF16))
        rrow = None

        p.dma("sync", segA[0:1, 0:2048], ap["b2row"], inc="GL")
        w1q_done = p.cnt["WSS"]
        i1_done = p.cnt["I1S"]
        x2_done = p.cnt["X2S"]
        p.wait("tensor", "GL")
        p.wait("tensor", "VE")
        p.wait("tensor", "SE")
        for s4 in range(4):
            MM(PB[:, 512 * s4:512 * (s4 + 1)], onesrow[:],
               segA[0:1, 512 * s4:512 * (s4 + 1)], True, True,
               inc=("PE" if s4 == 3 else None))
        p.wait("vector", "PE")
        CP("vector", b2bc[:], PB[:], inc="VE")
        p.wait("tensor", "VE")
        TR(PB[0:16, 0:128], osc1c[:], ident[:], inc="PE")
        p.wait("vector", "PE")
        CP("vector", mstage[:], PB[0:16, 0:128], inc="VE")
        p.wait("scalar", "VE")
        p.dma("scalar", rowscr, mstage[:], inc="RS")

        def build_bcast(dst_bf, src_dram):
            for blk in range(4):
                p.wait("sync", "SE", p.cnt["SE"])
                p.wait("sync", "VE", p.cnt["VE"])
                p.dma("sync", segA[0:1, 0:2048],
                      src_dram[0:1, 2048 * blk:2048 * (blk + 1)], inc="GBL")
                p.wait("tensor", "GBL")
                p.wait("tensor", "SE", p.cnt["SE"])
                p.wait("tensor", "VE", p.cnt["VE"])
                for s4 in range(4):
                    MM(PB[:, 512 * s4:512 * (s4 + 1)], onesrow[:],
                       segA[0:1, 512 * s4:512 * (s4 + 1)], True, True,
                       inc=("PE" if s4 == 3 else None))
                p.wait("scalar", "PE")
                ACT(dst_bf[:, 2048 * blk:2048 * (blk + 1)], PB[:], AF.Copy, inc="SE")
            p.wait("tensor", "SE")

        mm1_guard = [0, 0]
        silu_guard = 0
        pt_guard = [0, 0]
        f3_guard = [0, 0]
        merge_done = [0]
        for q in range(NQ):
            t0 = QW * q
            p.wait("sync", "RS")
            p.wait("sync", "SE", p.cnt["SE"])
            p.dma("sync", segB[0:1, 0:QW],
                  rowscr.rearrange("a b -> (a b)").rearrange("(o n) -> o n", o=1)[
                      0:1, t0:t0 + QW], inc="RR")
            p.wait("tensor", "RR")
            p.wait("tensor", "VE", p.cnt["VE"])
            MM(PB[:, 0:512], onesrow[:], segB[0:1, 0:512], True, True, inc="PE")
            p.wait("vector", "PE")
            CP("vector", s1b[:], PB[:, 0:512], inc="VE")
            if q == 0:
                p.wait("sync", "I1S", i1_done)
                p.wait("sync", "WSS", w1q_done)
            p.wait("sync", "VE", merge_done[0])
            for kc in range(16):
                p.dma("sync", int1h[:, QW * kc:QW * (kc + 1)],
                      int1s[128 * kc:128 * (kc + 1), t0:t0 + QW], inc="I1R")
            for oc in range(64):
                sl = oc % 2
                p.wait("sync", "PE", mm1_guard[sl])
                p.dma("sync", w1sb[sl][:].rearrange("p (a j) -> p a j", j=128),
                      w1q[oc].rearrange("(a p) j -> p a j", p=128),
                      inc=("W1A" if sl == 0 else "W1B"))
                p.wait("tensor", "W1A" if sl == 0 else "W1B")
                if oc == 0:
                    p.wait("tensor", "I1R", 16 * 16 * (q + 1))
                    p.wait("tensor", "VE", p.cnt["VE"])
                p.wait("tensor", "SE", silu_guard)
                ph = PH[:, 512 * sl:512 * (sl + 1)]
                for kc in range(16):
                    MM(ph, w1sb[sl][:, 128 * kc:128 * (kc + 1)],
                       int1h[:, QW * kc:QW * (kc + 1)], kc == 0, kc == 15,
                       inc=("PE" if kc == 15 else None))
                mm1_guard[sl] = p.cnt["PE"]
                p.wait("vector", "PE")
                TT("vector", segA[:, 0:512], ph, s1b[:], ALU.mult, inc="VE")
                p.wait("scalar", "VE")
                ACT(segB[:, 0:512], segA[:, 0:512], AF.Silu, bias=b1c[:, oc:oc + 1])
                ACT(hbuf[:, QW * oc:QW * (oc + 1)], segB[:, 0:512], AF.Gelu,
                    inc="SE")
                silu_guard = p.cnt["SE"]
            if q == 0:
                dump("h0", hbuf[:])
            p.wait("tensor", "W2A", p.cnt.get("W2A", 0))
            p.wait("tensor", "W2B", p.cnt.get("W2B", 0))
            p.wait("tensor", "PO", p.cnt.get("PO", 0))
            build_bcast(big[0], ap["gln"])
            build_bcast(big[1], ap["bln"])
            for Ti in range(QW // 128):
                for c in range(64):
                    g = (c // 8) % 2
                    PT = PTa if g == 0 else PTb
                    if c % 8 == 0:
                        p.wait("tensor", "SE", pt_guard[g])
                    TR(PT[:, 128 * (c % 8):128 * (c % 8) + 128],
                       hbuf[:, QW * c + 128 * Ti:QW * c + 128 * (Ti + 1)],
                       identb[:], inc=("PE" if c % 8 == 7 else None))
                    if c % 8 == 7:
                        p.wait("scalar", "PE")
                        ACT(htm[:, 1024 * (c // 8):1024 * (c // 8 + 1)], PT[:],
                            AF.Copy, inc="SE")
                        pt_guard[g] = p.cnt["SE"]
                p.wait("vector", "SE")
                for sgi in range(4):
                    sg = htm[:, 2048 * sgi:2048 * (sgi + 1)]
                    RED("vector", scol("p1"), sg, ALU.add, inc="VE")
                    p.wait("scalar", "VE")
                    ACTACC(segA[:], sg, AF.Square, scol("p2"))
                    p.wait("vector", "SE")
                    SP()
                    if sgi == 0:
                        CP("vector", scol("sx"), scol("p1"))
                        CP("vector", scol("sq"), scol("p2"))
                    else:
                        TT("vector", scol("sx"), scol("sx"), scol("p1"), ALU.add)
                        TT("vector", scol("sq"), scol("sq"), scol("p2"), ALU.add)
                    SP()
                TS("vector", scol("mu"), scol("sx"), 1.0 / INNER, ALU.mult)
                TS("vector", scol("m2"), scol("sq"), 1.0 / INNER, ALU.mult)
                SP()
                STT("vector", scol("v2"), scol("mu"), scol("mu"), scol("m2"),
                    ALU.mult, ALU.subtract)
                VEM()
                p.wait("scalar", "VE")
                ACT(scol("v1"), scol("v2"), AF.Sqrt, bias=col_eps5, scale=-1.0,
                    inc="SE")
                p.wait("vector", "SE")
                RECIP(scol("rv"), scol("v1"))
                VEM()
                for sgi in range(4):
                    sg = htm[:, 2048 * sgi:2048 * (sgi + 1)]
                    zs = zbuf[:, 2048 * sgi:2048 * (sgi + 1)]
                    TS("vector", segA[:], sg, scol("mu"), ALU.subtract, scol("rv"),
                       ALU.mult)
                    TT("vector", segA[:], segA[:], big[0][:, 2048 * sgi:2048 * (sgi + 1)],
                       ALU.mult)
                    TT("vector", zs, segA[:], big[1][:, 2048 * sgi:2048 * (sgi + 1)],
                       ALU.add, inc="VE")
                    RED("vector", scol("p1"), zs, ALU.max, absval=True)
                    p.wait("scalar", "VE")
                    ACTACC(segB[:], zs, AF.Square, scol("p2"))
                    p.wait("vector", "SE")
                    SP()
                    if sgi == 0:
                        CP("vector", scol("zq"), scol("p2"))
                        CP("vector", scol("zm"), scol("p1"))
                    else:
                        TT("vector", scol("zq"), scol("zq"), scol("p2"), ALU.add)
                        TT("vector", scol("zm"), scol("zm"), scol("p1"), ALU.max)
                    SP()
                TS("vector", scol("m2z"), scol("zq"), 1.0 / INNER, ALU.mult)
                VEM()
                p.wait("scalar", "VE")
                ACT(scol("sdz"), scol("m2z"), AF.Sqrt, bias=col_eps5, inc="SE")
                p.wait("vector", "SE")
                RECIP(scol("rho"), scol("sdz"))
                SP()
                TT("vector", scol("rm"), scol("rho"), scol("zm"), ALU.mult)
                SP()
                TS("vector", scol("rmc"), scol("rm"), 1e-5, ALU.max)
                SP()
                RECIP(scol("irmc"), scol("rmc"))
                SP()
                TS("vector", osc2c[:, 4 * q + Ti:4 * q + Ti + 1], scol("rmc"),
                   scol("m127_2"), ALU.mult)
                STT("vector", scol("c2"), scol("irmc"), 127.0, scol("rho"),
                    ALU.mult, ALU.mult)
                SP()
                for sgi in range(4):
                    sg = htm[:, 2048 * sgi:2048 * (sgi + 1)]
                    zs = zbuf[:, 2048 * sgi:2048 * (sgi + 1)]
                    TS("vector", segI[:], zs, scol("c2"), ALU.mult)
                    CP("vector", sg, segI[:])
                    STT("vector", segB[:], zs, scol("c2"), sg, ALU.mult,
                        ALU.subtract)
                    TS("vector", segA[:], segB[:], col_h, ALU.is_gt)
                    TS("vector", segB[:], segB[:], col_nh, ALU.is_lt)
                    TT("vector", segA[:], segA[:], segB[:], ALU.subtract)
                    TT("vector", sg, sg, segA[:], ALU.add, inc="VE")
                p.wait("tensor", "VE")
                for c in range(64):
                    g = (c // 8) % 2
                    PT = PTa if g == 0 else PTb
                    if c % 8 == 0:
                        p.wait("tensor", "SE", pt_guard[g])
                    TR(PT[:, 128 * (c % 8):128 * (c % 8) + 128],
                       htm[:, 128 * c:128 * (c + 1)], identb[:],
                       inc=("PE" if c % 8 == 7 else None))
                    if c % 8 == 7:
                        p.wait("scalar", "PE")
                        c0 = c - 7
                        dst = hbuf[:].rearrange("i (a t) -> i a t", t=QW)[
                            :, c0:c0 + 8, 128 * Ti:128 * (Ti + 1)]
                        ACT(dst, PT[:].rearrange("i (a t) -> i a t", t=128),
                            AF.Copy, inc="SE")
                        pt_guard[g] = p.cnt["SE"]
            if q == 0:
                dump("int2_0", hbuf[:])
                dump("osc2", osc2c[:])
            f3v_guard = 0
            f2_done = p.cnt["VE"]
            for m in range(16):
                sl = m % 2
                semw = "W2A" if sl == 0 else "W2B"
                p.wait("sync", "PE", f3_guard[sl])
                if m < 2:
                    p.wait("sync", "VE", f2_done)
                for part in range(4):
                    p.dma("sync",
                          big[sl][:, 2048 * part:2048 * (part + 1)].rearrange(
                              "p (a j) -> p a j", j=128),
                          w2q[m][2048 * part:2048 * (part + 1), :].rearrange(
                              "(a p) j -> p a j", p=128), inc=semw)
                p.wait("tensor", semw)
                p.wait("tensor", "VE", p.cnt["VE"] if m < 2 else f3v_guard)
                for kc in range(64):
                    MM(PB[:, 512 * sl:512 * (sl + 1)],
                       big[sl][:, 128 * kc:128 * (kc + 1)],
                       hbuf[:].rearrange("i (a t) -> i a t", t=QW)[:, kc, :],
                       kc == 0, kc == 63, inc=("PE" if kc == 63 else None))
                f3_guard[sl] = p.cnt["PE"]
                p.wait("vector", "PE")
                CP("vector", out2[:, QW * m:QW * (m + 1)], PB[:, 512 * sl:512 * (sl + 1)],
                   inc="VE")
                f3v_guard = p.cnt["VE"]
            p.wait("sync", "X2S", x2_done)
            for Ti in range(QW // 128):
                trow = t0 + 128 * Ti
                p.wait("sync", "STO", p.cnt.get("STO", 0))
                p.dma("sync", x3t[:], x2s[trow:trow + 128, :], inc="XM")
                p.wait("vector", "XM")
                TT("vector", x3t[:], x3t[:], b2bc[:], ALU.add, inc="VE")
                for mg in range(2):
                    PT = PTa if mg == 0 else PTb
                    p.wait("tensor", "VE", p.cnt["VE"])
                    for mm_ in range(8):
                        m = 8 * mg + mm_
                        TR(PT[:, 128 * mm_:128 * (mm_ + 1)],
                           out2[:, QW * m + 128 * Ti:QW * m + 128 * (Ti + 1)],
                           identb[:], inc=("PE" if mm_ == 7 else None))
                    p.wait("vector", "PE")
                    for mm_ in range(8):
                        m = 8 * mg + mm_
                        STT("vector", x3t[:, 128 * m:128 * (m + 1)],
                            PT[:, 128 * mm_:128 * (mm_ + 1)],
                            osc2c[:, 4 * q + Ti:4 * q + Ti + 1],
                            x3t[:, 128 * m:128 * (m + 1)], ALU.mult, ALU.add,
                            inc=("VE" if mm_ == 7 else None))
                p.wait("scalar", "VE")
                p.dma("scalar", out[trow:trow + 128, :], x3t[:], inc="STO")
                merge_done[0] = p.cnt["VE"]

    return _finish(p, nc, dbg_out, out, early=False)


def _finish(p, nc, dbg_out, out, early):
    if "STO" in p.cnt:
        p.wait("gpsimd", "STO")
    if "ST" in p.cnt:
        p.wait("gpsimd", "ST")
    blk = p.ctx.enter_context(nc.Block())
    p.finalize(blk)
    for stk in reversed(p.open_stacks):
        stk.close()
    p.ctx.close()
    return nc, dbg_out


def make_inmap(inputs, core):
    hs = np.ascontiguousarray(np.asarray(inputs["L0_attn_hidden_states"])[core])
    w1t = np.ascontiguousarray(np.asarray(inputs["L0_ffn_w1"]).T)
    w2t = np.ascontiguousarray(np.asarray(inputs["L0_ffn_w2"]).T)
    m = dict(
        x=hs.astype(np.float32),
        wqa=np.ascontiguousarray(np.asarray(inputs["L0_attn_Wqa"], np.float32)),
        wqb=np.ascontiguousarray(np.asarray(inputs["L0_attn_Wqb"], np.float32)),
        wkva=np.ascontiguousarray(np.asarray(inputs["L0_attn_Wkva"], np.float32)),
        wkvb=np.ascontiguousarray(np.asarray(inputs["L0_attn_Wkvb"], np.float32)),
        wo=np.ascontiguousarray(np.asarray(inputs["L0_attn_Wo"], np.float32)),
        qlnw=np.ascontiguousarray(np.asarray(inputs["L0_attn_q_a_ln_w"],
                                             np.float32).reshape(12, 128)),
        kvlnw=np.ascontiguousarray(np.asarray(inputs["L0_attn_kv_a_ln_w"],
                                              np.float32).reshape(4, 128)),
        w1t=w1t.astype(np.float32), w2t=w2t.astype(np.float32),
        b1=np.ascontiguousarray(np.asarray(inputs["L0_ffn_b1"],
                                           np.float32).reshape(64, 128)),
        b2=np.ascontiguousarray(np.asarray(inputs["L0_ffn_b2"],
                                           np.float32).reshape(16, 128)),
        b2row=np.ascontiguousarray(np.asarray(inputs["L0_ffn_b2"],
                                              np.float32).reshape(1, 2048)),
        gln=np.ascontiguousarray(np.asarray(inputs["L0_ffn_ln_g"],
                                            np.float32).reshape(1, INNER)),
        bln=np.ascontiguousarray(np.asarray(inputs["L0_ffn_ln_b"],
                                            np.float32).reshape(1, INNER)),
    )
    m.update(host_consts())
    return m


def kernel(**inputs):
    nc, _ = build_program(dbg=(), upto="F")
    in_maps = [make_inmap(inputs, c) for c in range(8)]
    res = run_bass_kernel_spmd(nc, in_maps, core_ids=list(range(8)))
    outs = [np.asarray(r["out"], dtype=np.float32) for r in res.results]
    return np.stack(outs, axis=0)


# version 05
# speedup vs baseline: 2.1361x; 1.0030x over previous
import math
from contextlib import ExitStack

import numpy as np
import ml_dtypes

import concourse.bass as bass
import concourse.mybir as mybir
from concourse.bass_utils import run_bass_kernel_spmd

F32 = mybir.dt.float32
BF16 = mybir.dt.bfloat16
I32 = mybir.dt.int32
AF = mybir.ActivationFunctionType
ALU = mybir.AluOpType
AX = mybir.AxisListType

S = 2048
D = 2048
NT = 16
NH = 16
QLORA = 1536
SC192 = 1.0 / math.sqrt(192.0)
INNER = 8192
QW = 512
NQ = S // QW

ENGS = ("sync", "tensor", "vector", "scalar", "gpsimd")


class Prog:
    def __init__(self, nc):
        self.nc = nc
        self.recs = {e: [] for e in ENGS}
        self.sem_handles = {}
        self.cnt = {}
        self.ctx = ExitStack()

    def sem(self, name):
        if name not in self.sem_handles:
            self.sem_handles[name] = self.ctx.enter_context(self.nc.semaphore(name))
            self.cnt[name] = 0
        return self.sem_handles[name]

    def wait(self, eng, semname, target=None):
        h = self.sem(semname)
        if target is None:
            target = self.cnt[semname]
        assert target <= self.cnt[semname], (semname, target, self.cnt[semname])
        if target > 0:
            self.recs[eng].append(lambda e, hh=h, t=target: e.wait_ge(hh, t))

    def emit(self, eng, build, inc=None, n=1):
        if inc is not None:
            h = self.sem(inc)
            self.cnt[inc] += n
            self.recs[eng].append(lambda e, b=build, hh=h, nn=n: b(e).then_inc(hh, nn))
        else:
            self.recs[eng].append(build)

    def dma(self, eng, out, in_, inc, **kw):
        self.emit(eng, lambda e, o=out, i=in_, k=dict(kw): e.dma_start(out=o, in_=i, **k),
                  inc=inc, n=16)

    def finalize(self, block):
        for eng in ENGS:
            recs = self.recs[eng]
            getattr(block, eng)(lambda e, r=recs: [f(e) for f in r])


def host_consts():
    ident = np.eye(128, dtype=np.float32)
    mask = np.where(np.arange(128)[None, :] > np.arange(128)[:, None],
                    np.float32(-1e30), np.float32(0.0)).astype(np.float32)
    P = np.zeros((64, 64), dtype=np.float32)
    P[:32, 32:] = -np.eye(32, dtype=np.float32)
    P[32:, :32] = np.eye(32, dtype=np.float32)
    cols = np.zeros((128, 8), dtype=np.float32)
    cols[:, 0] = 1e-5
    cols[:, 1] = 1e-6
    cols[:, 2] = 0.5
    cols[:, 3] = -0.5
    cols[:, 4] = -math.log(2.0 * math.pi)
    cols[:, 5] = 0.25
    iota = np.arange(2048, dtype=np.float32)[None, :]
    return dict(
        c_ident=ident,
        c_identb=ident.astype(ml_dtypes.bfloat16),
        c_mask=mask,
        c_rot=P.T.astype(ml_dtypes.bfloat16),
        c_cols=cols,
        c_iota=iota,
        c_onescol=np.ones((128, 1), dtype=np.float32),
        c_onescolb=np.ones((128, 1), dtype=ml_dtypes.bfloat16),
        c_onesrow=np.ones((1, 128), dtype=np.float32),
    )


def build_program(dbg=(), upto="F"):
    PHASES = ["A", "B", "C", "D", "E", "F"]
    stop_at = PHASES.index(upto)

    nc = bass.Bass("TRN2", target_bir_lowering=False, detect_race_conditions=False)
    p = Prog(nc)
    p.open_stacks = []
    EP = p.ctx.enter_context

    ap = {}
    def din(name, shape, dt=F32):
        ap[name] = nc.dram_tensor(name, shape, dt, kind="ExternalInput").ap()
    din("x", [S, D])
    din("wqa", [D, QLORA]); din("wqb", [QLORA, NH * 192]); din("wkva", [D, 576])
    din("wkvb", [512, NH * 256]); din("wo", [D, D])
    din("qlnw", [12, 128]); din("kvlnw", [4, 128])
    din("w1t", [D, INNER]); din("w2t", [INNER, D])
    din("b1", [64, 128]); din("b2", [16, 128]); din("b2row", [1, D])
    din("gln", [1, INNER]); din("bln", [1, INNER])
    din("c_ident", [128, 128]); din("c_identb", [128, 128], BF16)
    din("c_mask", [128, 128]); din("c_rot", [64, 64], BF16)
    din("c_cols", [128, 8]); din("c_iota", [1, 2048])
    din("c_onescol", [128, 1]); din("c_onescolb", [128, 1], BF16)
    din("c_onesrow", [1, 128])

    out = nc.dram_tensor("out", [S, D], F32, kind="ExternalOutput").ap()

    dbg_out = {}
    def dbg_decl(name, shape, dt=F32):
        if name in dbg:
            dbg_out[name] = nc.dram_tensor("dbg_" + name, shape, dt,
                                           kind="ExternalOutput").ap()
    for nm, sh, dt in [
        ("lnx", [128, 16 * 2048], BF16), ("q_a", [128, 12 * 2048], BF16),
        ("ckv_n", [128, 4 * 2048], BF16), ("kpe_r", [64, 2048], BF16),
        ("cs_pack", [128, 2048], BF16), ("escale", [128, 16], F32),
        ("attnout", [128, 16 * 2048], BF16), ("x2", [S, D], F32),
        ("int1", [D, S], BF16), ("osc1", [128, 16], F32),
        ("tern1", [128, 2048], BF16), ("wm", [128, 2], F32),
        ("h0", [128, 64 * QW], BF16), ("int2_0", [128, 64 * QW], BF16),
        ("osc2", [128, 16], F32),
    ]:
        dbg_decl(nm, sh, dt)
    def DBG(name):
        return name in dbg_out

    x2s = nc.dram_tensor("x2s", [S, D], F32, kind="Internal").ap()
    int1s = nc.dram_tensor("int1s", [D, S], BF16, kind="Internal").ap()
    w1q = nc.dram_tensor("w1q", [64, 128, D], BF16, kind="Internal").ap()
    w2q = nc.dram_tensor("w2q", [16, 128, INNER], BF16, kind="Internal").ap()
    rowscr = nc.dram_tensor("rowscr", [16, 128], F32, kind="Internal").ap()

    ident = EP(nc.sbuf_tensor("ident", [128, 128], F32))
    identb = EP(nc.sbuf_tensor("identb", [128, 128], BF16))
    maskt = EP(nc.sbuf_tensor("maskt", [128, 128], F32))
    rotm = EP(nc.sbuf_tensor("rotm", [64, 64], BF16))
    ccols = EP(nc.sbuf_tensor("ccols", [128, 8], F32))
    onescol = EP(nc.sbuf_tensor("onescol", [128, 1], F32))
    onescolb = EP(nc.sbuf_tensor("onescolb", [128, 1], BF16))
    onesrow = EP(nc.sbuf_tensor("onesrow", [1, 128], F32))
    qlnwc = EP(nc.sbuf_tensor("qlnwc", [128, 12], F32))
    kvlnwc = EP(nc.sbuf_tensor("kvlnwc", [128, 4], F32))
    b1c = EP(nc.sbuf_tensor("b1c", [128, 64], F32))
    b2c = EP(nc.sbuf_tensor("b2c", [128, 16], F32))
    stats = EP(nc.sbuf_tensor("stats", [128, 48], F32))
    _statmap = {}
    def scol(name, n=1):
        if name not in _statmap:
            base = sum(v[1] for v in _statmap.values())
            assert base + n <= 48, (name, base)
            _statmap[name] = (base, n)
        b, nn = _statmap[name]
        return stats[:, b:b + nn]
    wder = EP(nc.sbuf_tensor("wder", [2, 4], F32))
    wch = [EP(nc.sbuf_tensor(f"wch{i}", [128, 64], F32)) for i in range(2)]
    def scol2(name):
        if name not in _statmap:
            base = sum(v[1] for v in _statmap.values())
            assert base + 1 <= 48
            _statmap[name] = (base, 1)
        b, _n = _statmap[name]
        return stats[0:2, b:b + 1]
    for _nm in ("wtmp", "w1s", "w2s"):
        scol(_nm)
    stk_cs = ExitStack()
    cs_pack = stk_cs.enter_context(nc.sbuf_tensor("cs_pack", [128, 2048], BF16,
                                                  side="right"))
    kpe_r = stk_cs.enter_context(nc.sbuf_tensor("kpe_r", [64, 2048], BF16,
                                                side="right"))
    escale = EP(nc.sbuf_tensor("escale", [128, 16], F32))
    osc1c = EP(nc.sbuf_tensor("osc1c", [128, 16], F32))
    osc2c = EP(nc.sbuf_tensor("osc2c", [128, 16], F32))
    mstage = EP(nc.sbuf_tensor("mstage", [16, 128], F32))

    col_eps5 = ccols[:, 0:1]
    col_eps6 = ccols[:, 1:2]
    col_h = ccols[:, 2:3]
    col_nh = ccols[:, 3:4]
    col_l2pi = ccols[:, 4:5]
    col_q = ccols[:, 5:6]

    def MM(dst, lhsT, rhs, start, stop, inc=None):
        p.emit("tensor", lambda e, d=dst, l=lhsT, r=rhs, s1=start, s2=stop:
               e.matmul(out=d, lhsT=l, rhs=r, start=s1, stop=s2), inc=inc)

    def TR(dst, src, idn, inc=None):
        p.emit("tensor", lambda e, d=dst, s=src, i=idn:
               e.transpose(out=d, in_=s, identity=i), inc=inc)

    def ACT(dst, src, func, bias=0.0, scale=1.0, accum=None, inc=None):
        p.emit("scalar", lambda e, d=dst, s=src, f=func, b=bias, sc=scale, a=accum:
               e.activation(out=d, in_=s, func=f, bias=b, scale=sc, accum_out=a),
               inc=inc)

    def TT(eng, dst, a, b, op, inc=None):
        p.emit(eng, lambda e, d=dst, x=a, y=b, o=op:
               e.tensor_tensor(out=d, in0=x, in1=y, op=o), inc=inc)

    def TS(eng, dst, a, s1, op1, s2=None, op2=None, inc=None, accum=None):
        if s2 is None:
            p.emit(eng, lambda e, d=dst, x=a, u=s1, o1=op1, ac=accum:
                   e.tensor_scalar(out=d, in0=x, scalar1=u, scalar2=None, op0=o1,
                                   accum_out=ac), inc=inc)
        else:
            p.emit(eng, lambda e, d=dst, x=a, u=s1, o1=op1, v=s2, o2=op2, ac=accum:
                   e.tensor_scalar(out=d, in0=x, scalar1=u, scalar2=v, op0=o1,
                                   op1=o2, accum_out=ac), inc=inc)

    def STT(eng, dst, in0, scalar, in1, op0, op1, inc=None):
        p.emit(eng, lambda e, d=dst, a=in0, s=scalar, b=in1, o0=op0, o1=op1:
               e.scalar_tensor_tensor(out=d, in0=a, scalar=s, in1=b, op0=o0, op1=o1),
               inc=inc)

    def CP(eng, dst, src, inc=None):
        p.emit(eng, lambda e, d=dst, s=src: e.tensor_copy(out=d, in_=s), inc=inc)

    def RED(eng, dst, src, op, absval=False, neg=False, inc=None):
        p.emit(eng, lambda e, d=dst, s=src, o=op, a=absval, n=neg:
               e.tensor_reduce(out=d, in_=s, axis=AX.X, op=o,
                               apply_absolute_value=a, negate=n), inc=inc)

    def RECIP(dst, src, inc=None):
        p.emit("vector", lambda e, d=dst, s=src: e.reciprocal(out=d, in_=s), inc=inc)

    def NOP(eng, inc):
        p.emit(eng, lambda e: e.nop(), inc=inc)

    def dump(name, src):
        if not DBG(name):
            return
        p.wait("scalar", "VE"); p.wait("scalar", "SE"); p.wait("scalar", "PO")
        p.wait("scalar", "PE")
        p.dma("scalar", dbg_out[name], src, inc="ST")
        for eng in ("vector", "gpsimd", "tensor", "scalar", "sync"):
            p.wait(eng, "ST")

    def SP(n=2):
        for _ in range(n):
            p.emit("vector", lambda e, d=scol("junkV"), s=ccols[:, 7:8]:
                   e.tensor_copy(out=d, in_=s))

    def VEM():
        SP(1)
        p.emit("vector", lambda e, d=scol("junkV"), s=ccols[:, 7:8]:
               e.tensor_copy(out=d, in_=s), inc="VE")

    def phase_barrier():
        p.emit("vector", lambda e, d=scol("junkV"), s=ccols[:, 7:8]:
               e.tensor_copy(out=d, in_=s), inc="BAR")
        p.emit("gpsimd", lambda e, d=scol("junkP"), s=ccols[:, 6:7]:
               e.tensor_copy(out=d, in_=s), inc="BAR")
        p.emit("scalar", lambda e, d=scol("junkS"), s=ccols[:, 7:8]:
               e.activation(out=d, in_=s, func=AF.Copy), inc="BAR")
        p.emit("tensor", lambda e: e.nop(), inc="BAR")
        for _e in ENGS:
            p.wait(_e, "BAR")

    def ACTACC(dump_dst, src, func, acc_final, bias=0.0, scale=1.0):
        ACT(dump_dst, src, func, bias=bias, scale=scale, accum=scol("accraw"))
        ACT(acc_final, scol("accraw"), AF.Copy, inc="SE")

    def lnstats(xt, scra, width=D):
        RED("vector", scol("sx"), xt, ALU.add, inc="VE")
        p.wait("scalar", "VE")
        ACTACC(scra, xt, AF.Square, scol("sq"))
        p.wait("vector", "SE")
        TS("vector", scol("mu"), scol("sx"), 1.0 / width, ALU.mult)
        TS("vector", scol("m2"), scol("sq"), 1.0 / width, ALU.mult)
        SP()
        STT("vector", scol("v2"), scol("mu"), scol("mu"), scol("m2"),
            ALU.mult, ALU.subtract)
        VEM()
        p.wait("scalar", "VE")
        ACT(scol("v1"), scol("v2"), AF.Sqrt, bias=col_eps5, scale=-1.0, inc="SE")
        p.wait("vector", "SE")
        RECIP(scol("rv"), scol("v1"))
        SP()

    wtA = [("1", kc, sc_) for kc in range(16) for sc_ in range(4)] + \
          [("2", kc, 0) for kc in range(64)]
    wtB = list(wtA)
    wstate = {"phase": "A", "i": 0, "po_guard": 0, "ws_guard": [0, 0]}
    wbufs = {}

    def wsrc(t):
        nm, kc, sc_ = t
        if nm == "1":
            return ap["w1t"][128 * kc:128 * (kc + 1), 2048 * sc_:2048 * (sc_ + 1)]
        return ap["w2t"][128 * kc:128 * (kc + 1), :]

    def wprep_step(n=1):
        wsub, wg = wbufs["wsub"], wbufs["wg"]
        for _ in range(n):
            i = wstate["i"]
            if wstate["phase"] == "A":
                if i >= len(wtA):
                    return
                t = wtA[i]
                nm = t[0]
                p.wait("sync", "WVE", wstate["po_guard"])
                p.dma("sync", wsub[:], wsrc(t), inc="WLA")
                p.wait("vector", "WLA", 16 * (i + 1))
                RED("vector", osc1c[:], wsub[:].rearrange("p (a b) -> p a b", b=128),
                    ALU.add, absval=True)
                SP()
                wi = 0 if nm == "1" else 1
                ci = i if nm == "1" else t[1]
                RED("vector", wch[wi][:, ci:ci + 1], osc1c[:], ALU.add)
                p.emit("vector", lambda e, d=scol("junkV"), s=ccols[:, 7:8]:
                       e.tensor_copy(out=d, in_=s), inc="WVE")
                wstate["po_guard"] = p.cnt["WVE"]
                wstate["i"] = i + 1
            else:
                if i >= len(wtB):
                    return
                t = wtB[i]
                nm, kc, sc_ = t
                sl = i % 2
                p.wait("sync", "PO", wstate["po_guard"])
                p.dma("sync", wsub[:], wsrc(t), inc="WLB")
                p.wait("vector", "WLB", 16 * (i + 1))
                if i == 0:
                    p.wait("vector", "WF4", wstate["wf4"])
                p.wait("vector", "WSS", wstate["ws_guard"][sl])
                thr = scol("thr1") if nm == "1" else scol("thr2")
                nthr = scol("nthr1") if nm == "1" else scol("nthr2")
                TS("vector", wg[sl][:], wsub[:], thr, ALU.is_gt)
                TS("vector", wbufs["wg2"][:], wsub[:], nthr, ALU.is_lt)
                TT("vector", wg[sl][:], wg[sl][:], wbufs["wg2"][:], ALU.subtract,
                   inc="PO")
                wstate["po_guard"] = p.cnt["PO"]
                p.wait("scalar", "PO")
                if nm == "1":
                    dst = w1q[16 * sc_:16 * (sc_ + 1), :, 128 * kc:128 * (kc + 1)]
                else:
                    dst = w2q[:, :, 128 * kc:128 * (kc + 1)]
                p.dma("scalar", dst.rearrange("a i j -> i a j"),
                      wg[sl][:].rearrange("i (a j) -> i a j", j=128), inc="WSS")
                wstate["ws_guard"][sl] = p.cnt["WSS"]
                if DBG("tern1") and nm == "1" and kc == 0 and sc_ == 0:
                    p.dma("scalar", dbg_out["tern1"], wg[sl][:], inc="ST")
                wstate["i"] = i + 1

    def wprep_finalize():
        assert wstate["phase"] == "A" and wstate["i"] >= len(wtA)
        b1_ = _statmap["w1s"][0]
        assert _statmap["w2s"][0] == b1_ + 1
        p.wait("vector", "WVE")
        for wi in range(2):
            RED("vector", osc2c[:, 0:8], wch[wi][:].rearrange("p (a b) -> p a b", b=8),
                ALU.add)
            SP()
            RED("vector", stats[:, b1_ + wi:b1_ + wi + 1], osc2c[:, 0:8], ALU.add)
            SP()
        p.emit("vector", lambda e, d=scol("junkV"), s=ccols[:, 7:8]:
               e.tensor_copy(out=d, in_=s), inc="WVE")
        p.wait("scalar", "WVE")
        p.emit("scalar", lambda e, o=rowscr[8:10, :].rearrange("c i -> i c"),
               i_=stats[:, b1_:b1_ + 2]:
               e.dma_start(out=o, in_=i_, allow_slow_non_contiguous=True),
               inc="WF", n=16)
        p.wait("sync", "WF")
        p.dma("sync", mstage[0:2, :], rowscr[8:10, :], inc="WF2")
        p.wait("vector", "WF2")
        RED("vector", osc2c[0:2, 0:8], mstage[0:2, :].rearrange("p (a b) -> p a b", b=16),
            ALU.add)
        SP()
        RED("vector", scol2("wsum2"), osc2c[0:2, 0:8], ALU.add)
        SP()
        TS("vector", wder[0:2, 0:1], scol2("wsum2"), 1.0 / (2048.0 * 8192.0),
           ALU.mult, 1e-5, ALU.max)
        SP()
        TS("vector", wder[0:2, 1:2], wder[0:2, 0:1], 0.5, ALU.mult)
        TS("vector", wder[0:2, 2:3], wder[0:2, 0:1], 1.0 / 127.0, ALU.mult)
        SP()
        TS("vector", wder[0:2, 3:4], wder[0:2, 1:2], -1.0, ALU.mult)
        p.emit("vector", lambda e, d=scol("junkV"), s=ccols[:, 7:8]:
               e.tensor_copy(out=d, in_=s), inc="VE")
        p.wait("scalar", "VE")
        p.emit("scalar", lambda e, o=rowscr[10:12, 0:4], i_=wder[:]:
               e.dma_start(out=o, in_=i_), inc="WF3", n=16)
        p.wait("sync", "WF3")
        for col, (r_, c_) in (("thr1", (10, 1)), ("thr2", (11, 1)),
                              ("m127_1", (10, 2)), ("m127_2", (11, 2)),
                              ("nthr1", (10, 3)), ("nthr2", (11, 3))):
            p.dma("sync", scol(col),
                  rowscr[r_:r_ + 1, c_:c_ + 1].broadcast_to([128, 1]), inc="WF4")
        wstate["wf4"] = p.cnt["WF4"]
        wstate["po_guard"] = p.cnt.get("PO", 0)
        wstate["phase"] = "B"
        wstate["i"] = 0

    with ExitStack() as st0:
        IP = st0.enter_context(nc.psum_tensor("IP", [128, 2048], F32))
        ist = st0.enter_context(nc.sbuf_tensor("ist", [128, 512], F32))
        irow = st0.enter_context(nc.sbuf_tensor("irow", [1, 2048], F32))
        iw64 = st0.enter_context(nc.sbuf_tensor("iw64", [1, 64], F32))
        iu = st0.enter_context(nc.sbuf_tensor("iu", [64, 2048], F32))
        iup = st0.enter_context(nc.sbuf_tensor("iup", [64, 2048], F32))
        isA = st0.enter_context(nc.sbuf_tensor("isA", [64, 2048], F32))
        isB = st0.enter_context(nc.sbuf_tensor("isB", [64, 2048], F32))
        isI = st0.enter_context(nc.sbuf_tensor("isI", [64, 2048], I32))
        for nm, dst in [("c_ident", ident[:]), ("c_identb", identb[:]),
                        ("c_mask", maskt[:]), ("c_rot", rotm[:]),
                        ("c_cols", ccols[:]), ("c_onescol", onescol[:]),
                        ("c_onescolb", onescolb[:]), ("c_onesrow", onesrow[:]),
                        ("c_iota", irow[:])]:
            p.dma("sync", dst, ap[nm], inc="LD")
        p.dma("sync", ist[0:12, 0:128], ap["qlnw"], inc="LD")
        p.dma("sync", ist[0:4, 128:256], ap["kvlnw"], inc="LD")
        p.dma("sync", ist[0:64, 256:384], ap["b1"], inc="LD")
        p.dma("sync", ist[0:16, 384:512], ap["b2"], inc="LD")
        p.wait("tensor", "LD")
        TR(IP[:, 0:12], ist[0:12, 0:128], ident[0:12, 0:12])
        TR(IP[:, 12:16], ist[0:4, 128:256], ident[0:4, 0:4])
        TR(IP[:, 16:80], ist[0:64, 256:384], ident[0:64, 0:64])
        TR(IP[:, 80:96], ist[0:16, 384:512], ident[0:16, 0:16], inc="PE")
        p.wait("scalar", "PE")
        ACT(qlnwc[:], IP[:, 0:12], AF.Copy)
        ACT(kvlnwc[:], IP[:, 12:16], AF.Copy)
        ACT(b1c[:], IP[:, 16:80], AF.Copy)
        ACT(b2c[:], IP[:, 80:96], AF.Copy, inc="SE")
        ACT(iw64[0:1, 0:32], irow[0:1, 0:32], AF.Exp,
            bias=col_l2pi[0:1, :], scale=-math.log(10000.0) / 32.0)
        ACT(iw64[0:1, 32:64], irow[0:1, 0:32], AF.Exp,
            bias=col_l2pi[0:1, :], scale=-math.log(10000.0) / 32.0, inc="SE")
        p.wait("tensor", "SE")
        for sl in range(4):
            MM(IP[0:64, 512 * sl:512 * (sl + 1)], iw64[:],
               irow[0:1, 512 * sl:512 * (sl + 1)], True, True,
               inc=("PE" if sl == 3 else None))
        p.wait("vector", "PE")
        CP("vector", iu[:], IP[0:64, :])
        TS("vector", iup[:], IP[0:64, :], col_q[0:64, :], ALU.add, inc="VE")
        def rope_half(u_ap, dstrow):
            TS("vector", isI[:], u_ap, 1.0, ALU.mult)
            TS("vector", isB[:], isI[:], 0.0, ALU.add)
            TT("vector", isA[:], u_ap, isB[:], ALU.subtract)
            TS("vector", isB[:], isA[:], col_h[0:64, :], ALU.is_gt)
            TT("vector", isA[:], isA[:], isB[:], ALU.subtract)
            TS("vector", isB[:], isA[:], col_nh[0:64, :], ALU.is_lt)
            TT("vector", isA[:], isA[:], isB[:], ALU.add, inc="VE")
            p.wait("scalar", "VE")
            ACT(dstrow, isA[:], AF.Sin, scale=2.0 * math.pi, inc="SE")
        rope_half(iu[:], cs_pack[64:128, :])
        rope_half(iup[:], cs_pack[0:64, :])
        p.wait("sync", "SE")
        p.wait("gpsimd", "SE")
    dump("cs_pack", cs_pack[:])

    stk_w = ExitStack(); p.open_stacks.append(stk_w)
    wbufs["wsub"] = stk_w.enter_context(nc.sbuf_tensor("wsub", [128, 2048], F32, side="left"))
    wbufs["wg"] = [stk_w.enter_context(nc.sbuf_tensor("wgA", [128, 2048], BF16, side="left")),
                   stk_w.enter_context(nc.sbuf_tensor("wgB", [128, 2048], BF16, side="left"))]
    wbufs["wg2"] = stk_w.enter_context(nc.sbuf_tensor("wg2", [128, 2048], BF16, side="left"))
    stk_qa = ExitStack(); p.open_stacks.append(stk_qa)
    q_a = stk_qa.enter_context(nc.sbuf_tensor("q_a", [128, 12 * 2048], BF16, side="left"))
    ckv_n = stk_qa.enter_context(nc.sbuf_tensor("ckv_n", [128, 4 * 2048], BF16, side="left"))
    stk_lnx = ExitStack(); p.open_stacks.append(stk_lnx)
    lnx = stk_lnx.enter_context(nc.sbuf_tensor("lnx", [128, 16 * 2048], BF16, side="right"))

    with ExitStack() as stA:
        PTa = stA.enter_context(nc.psum_tensor("PTaA", [128, 1024], BF16))
        PTb = stA.enter_context(nc.psum_tensor("PTbA", [128, 1024], BF16))
        xstg = [stA.enter_context(nc.sbuf_tensor(f"xstgA{i}", [128, 2048], F32))
                for i in range(2)]
        lnts = stA.enter_context(nc.sbuf_tensor("lnts", [128, 2048], BF16))
        scra = stA.enter_context(nc.sbuf_tensor("scraA", [128, 2048], F32))

        xa_guard = [0, 0]
        pt_guard = [0, 0]
        tr_done = 0
        for T in range(NT):
            sl = T % 2
            semx = "XLA" if sl == 0 else "XLB"
            p.wait("sync", "VE", xa_guard[sl])
            p.dma("sync", xstg[sl][:], ap["x"][128 * T:128 * (T + 1), :], inc=semx)
            p.wait("vector", semx)
            xt = xstg[sl][:]
            lnstats(xt, scra[:])
            p.wait("vector", "PE", tr_done)
            TS("vector", lnts[:], xt, scol("mu"), ALU.subtract, scol("rv"), ALU.mult,
               inc="VE")
            xa_guard[sl] = p.cnt["VE"]
            p.wait("tensor", "VE")
            for g, PT in ((0, PTa), (1, PTb)):
                p.wait("tensor", "SE", pt_guard[g])
                for c in range(8):
                    cc = 8 * g + c
                    TR(PT[:, 128 * c:128 * (c + 1)], lnts[:, 128 * cc:128 * (cc + 1)],
                       identb[:], inc=("PE" if c == 7 else None))
                p.wait("scalar", "PE")
                dst = lnx[:].rearrange("i (a t) -> i a t", t=2048)[
                    :, 8 * g:8 * (g + 1), 128 * T:128 * (T + 1)]
                ACT(dst, PT[:].rearrange("i (a t) -> i a t", t=128), AF.Copy, inc="SE")
                pt_guard[g] = p.cnt["SE"]
            tr_done = p.cnt["PE"]
            wprep_step(4)
    dump("lnx", lnx[:])
    if stop_at <= 0:
        return _finish(p, nc, dbg_out, out, early=True)

    phase_barrier()
    with ExitStack() as stB:
        PBm = stB.enter_context(nc.psum_tensor("PBmB", [128, 2048], F32))
        Q2 = stB.enter_context(nc.psum_tensor("Q2B", [1, 2048], F32))
        wst = stB.enter_context(nc.sbuf_tensor("wstB", [128, 2048], F32))
        wbf = [stB.enter_context(nc.sbuf_tensor(f"wbfB{i}", [128, 2048], BF16))
               for i in range(2)]
        sqb = wbufs["wg2"]
        rrow = stB.enter_context(nc.sbuf_tensor("rrowB", [1, 2048], F32))

        p.wait("tensor", "VE")
        p.wait("tensor", "SE")
        cast_guard = [0, 0]
        mm_done = [0, 0]
        evac_guard = 0
        for m in range(12):
            sl = m % 2
            p.wait("sync", "POB", cast_guard[1 - sl])
            p.dma("sync", wst[:].rearrange("p (a j) -> p a j", j=128),
                  ap["wqa"][:, 128 * m:128 * (m + 1)].rearrange("(a p) j -> p a j", p=128),
                  inc="WLQ")
            p.wait("scalar", "WLQ", 16 * (m + 1))
            p.wait("scalar", "PE", mm_done[sl])
            ACT(wbf[sl][:], wst[:], AF.Copy, inc="POB")
            cast_guard[sl] = p.cnt["POB"]
            p.wait("tensor", "POB", cast_guard[sl])
            p.wait("tensor", "SE", evac_guard)
            for kc in range(16):
                for s4 in range(4):
                    MM(PBm[:, 512 * s4:512 * (s4 + 1)],
                       wbf[sl][:, 128 * kc:128 * (kc + 1)],
                       lnx[:, 2048 * kc + 512 * s4:2048 * kc + 512 * (s4 + 1)],
                       kc == 0, kc == 15,
                       inc=("PE" if (kc == 15 and s4 == 3) else None))
            mm_done[sl] = p.cnt["PE"]
            p.wait("scalar", "PE")
            ACT(q_a[:, 2048 * m:2048 * (m + 1)], PBm[:], AF.Copy)
            ACT(sqb[:], PBm[:], AF.Square, inc="SE")
            evac_guard = p.cnt["SE"]
            p.wait("tensor", "SE")
            for s4 in range(4):
                MM(Q2[0:1, 512 * s4:512 * (s4 + 1)], onescolb[:],
                   sqb[:, 512 * s4:512 * (s4 + 1)], m == 0, m == 11,
                   inc=("PE" if (m == 11 and s4 == 3) else None))
            wprep_step(3)
        p.wait("vector", "PE")
        TS("vector", rrow[:], Q2[0:1, :], 1.0 / QLORA, ALU.mult)
        VEM()
        p.wait("scalar", "VE")
        ACT(rrow[:], rrow[:], AF.Sqrt, bias=col_eps6[0:1, :], inc="SE")
        p.wait("vector", "SE")
        RECIP(rrow[:], rrow[:])
        VEM()
        p.wait("tensor", "VE")
        for qi in range(16):
            TR(PBm[:, 1024 + qi:1024 + qi + 1], rrow[0:1, 128 * qi:128 * (qi + 1)],
               ident[0:1, 0:1], inc=("PE" if qi == 15 else None))
        p.wait("vector", "PE")
        TS("vector", escale[:], PBm[:, 1024:1040], SC192, ALU.mult)
        VEM()
    dump("q_a", q_a[:])
    dump("escale", escale[:])
    if stop_at <= 1:
        return _finish(p, nc, dbg_out, out, early=True)

    phase_barrier()
    with ExitStack() as stC:
        PBm = stC.enter_context(nc.psum_tensor("PBmC", [128, 2048], F32))
        Q2 = stC.enter_context(nc.psum_tensor("Q2C", [1, 2048], F32))
        wst = stC.enter_context(nc.sbuf_tensor("wstC", [128, 2048], F32))
        wbf = [stC.enter_context(nc.sbuf_tensor(f"wbfC{i}", [128, 2048], BF16))
               for i in range(2)]
        sqb = wbufs["wg2"]
        rrow = stC.enter_context(nc.sbuf_tensor("rrowC", [1, 2048], F32))
        rbc = stC.enter_context(nc.sbuf_tensor("rbcC", [128, 2048], F32))
        kpebf = stC.enter_context(nc.sbuf_tensor("kpebf", [64, 2048], BF16))
        scrc = stC.enter_context(nc.sbuf_tensor("scrC", [64, 2048], F32))

        p.wait("tensor", "VE")
        p.wait("tensor", "SE")
        cast_guard = [0, 0]
        mm_doneC = [0, 0]
        evac_guard = 0
        for m in range(5):
            ncols = 128 if m < 4 else 64
            sl = m % 2
            p.wait("sync", "POC", cast_guard[1 - sl])
            p.dma("sync", wst[:, 0:16 * ncols].rearrange("p (a j) -> p a j", a=16),
                  ap["wkva"][:, 128 * m:128 * m + ncols].rearrange(
                      "(a p) j -> p a j", p=128), inc="WLC")
            p.wait("scalar", "WLC", 16 * (m + 1))
            p.wait("scalar", "PE", mm_doneC[sl])
            ACT(wbf[sl][:, 0:16 * ncols], wst[:, 0:16 * ncols], AF.Copy, inc="POC")
            cast_guard[sl] = p.cnt["POC"]
            p.wait("tensor", "POC", cast_guard[sl])
            p.wait("tensor", "SE", evac_guard)
            for kc in range(16):
                for s4 in range(4):
                    MM(PBm[0:ncols, 512 * s4:512 * (s4 + 1)],
                       wbf[sl][:, ncols * kc:ncols * (kc + 1)],
                       lnx[:, 2048 * kc + 512 * s4:2048 * kc + 512 * (s4 + 1)],
                       kc == 0, kc == 15,
                       inc=("PE" if (kc == 15 and s4 == 3) else None))
            mm_doneC[sl] = p.cnt["PE"]
            p.wait("scalar", "PE")
            if m < 4:
                ACT(ckv_n[:, 2048 * m:2048 * (m + 1)], PBm[:], AF.Copy)
                ACT(sqb[:], PBm[:], AF.Square, inc="SE")
                evac_guard = p.cnt["SE"]
                p.wait("tensor", "SE")
                for s4 in range(4):
                    MM(Q2[0:1, 512 * s4:512 * (s4 + 1)], onescolb[:],
                       sqb[:, 512 * s4:512 * (s4 + 1)], m == 0, m == 3,
                       inc=("PE" if (m == 3 and s4 == 3) else None))
            else:
                ACT(kpebf[:], PBm[0:64, :], AF.Copy, inc="SE")
                evac_guard = p.cnt["SE"]
            wprep_step(3)
        p.wait("vector", "PE")
        TS("vector", rrow[:], Q2[0:1, :], 1.0 / 512.0, ALU.mult)
        VEM()
        p.wait("scalar", "VE")
        ACT(rrow[:], rrow[:], AF.Sqrt, bias=col_eps6[0:1, :], inc="SE")
        p.wait("vector", "SE")
        RECIP(rrow[:], rrow[:])
        VEM()
        p.wait("tensor", "VE")
        p.wait("tensor", "SE")
        for s4 in range(4):
            MM(PBm[:, 512 * s4:512 * (s4 + 1)], onesrow[:],
               rrow[0:1, 512 * s4:512 * (s4 + 1)], True, True,
               inc=("PE" if s4 == 3 else None))
        p.wait("vector", "PE")
        CP("vector", rbc[:], PBm[:], inc="VE")
        for c in range(4):
            TT("vector", ckv_n[:, 2048 * c:2048 * (c + 1)],
               ckv_n[:, 2048 * c:2048 * (c + 1)], rbc[:], ALU.mult,
               inc=("VE" if c == 3 else None))
        p.wait("tensor", "SE")
        p.wait("tensor", "VE")
        for s4 in range(4):
            MM(PBm[0:64, 512 * s4:512 * (s4 + 1)], rotm[:],
               kpebf[:, 512 * s4:512 * (s4 + 1)], True, True,
               inc=("PE" if s4 == 3 else None))
        p.wait("vector", "PE")
        TT("vector", scrc[:], PBm[0:64, :], cs_pack[64:128, :], ALU.mult)
        TT("vector", kpe_r[:], kpebf[:], cs_pack[0:64, :], ALU.mult)
        TT("vector", kpe_r[:], kpe_r[:], scrc[:], ALU.add, inc="VE")
        p.wait("gpsimd", "VE")
        p.wait("sync", "VE")
    dump("ckv_n", ckv_n[:])
    dump("kpe_r", kpe_r[:])

    while wstate["phase"] == "A" and wstate["i"] < len(wtA):
        wprep_step(4)
    wprep_finalize()
    if stop_at <= 2:
        return _finish(p, nc, dbg_out, out, early=True)

    phase_barrier()
    stk_lnx.close(); p.open_stacks.remove(stk_lnx)
    stk_ao = ExitStack(); p.open_stacks.append(stk_ao)
    attnout = stk_ao.enter_context(
        nc.sbuf_tensor("attnout", [128, 16 * 2048], BF16, side="right"))

    with ExitStack() as stD:
        PB = stD.enter_context(nc.psum_tensor("PBD", [128, 2048], F32))
        PTa = stD.enter_context(nc.psum_tensor("PTaD", [128, 1024], BF16))
        PTb = stD.enter_context(nc.psum_tensor("PTbD", [128, 1024], BF16))
        PO1 = stD.enter_context(nc.psum_tensor("PO1D", [128, 512], F32))
        PO2 = stD.enter_context(nc.psum_tensor("PO2D", [128, 512], F32))
        qnope = stD.enter_context(nc.sbuf_tensor("qnope", [128, 2048], BF16))
        qper = stD.enter_context(nc.sbuf_tensor("qper", [64, 2048], BF16))
        knope = stD.enter_context(nc.sbuf_tensor("knope", [128, 2048], BF16))
        vtm = stD.enter_context(nc.sbuf_tensor("vtm", [128, 2048], BF16))
        psb = stD.enter_context(nc.sbuf_tensor("psb", [128, 2048], BF16))
        ptsb = stD.enter_context(nc.sbuf_tensor("ptsb", [128, 2048], BF16))
        scrd = stD.enter_context(nc.sbuf_tensor("scrD", [64, 2048], F32))
        wqbst = stD.enter_context(nc.sbuf_tensor("wqbst", [128, 256], F32))
        wqbbf = [stD.enter_context(nc.sbuf_tensor(f"wqbbf{i}", [128, 12 * 192], BF16))
                 for i in range(2)]
        wkvbbf = [stD.enter_context(nc.sbuf_tensor(f"wkvbbf{i}", [128, 1024], BF16))
                  for i in range(2)]

        head_ready = {}
        hw_state = {"POq": 0}

        def head_weight_step(h):
            sl = h % 2
            kc = hw_state.setdefault(("kc", h), 0)
            if kc >= 16:
                return
            p.wait("sync", "POq", hw_state["POq"])
            if kc < 12:
                p.dma("sync", wqbst[:, 0:192],
                      ap["wqb"][128 * kc:128 * (kc + 1), 192 * h:192 * (h + 1)],
                      inc="WQ")
                p.wait("scalar", "WQ")
                if kc == 0:
                    p.wait("scalar", "PE", p.cnt["PE"])
                ACT(wqbbf[sl][:, 192 * kc:192 * (kc + 1)],
                    wqbst[:, 0:192], AF.Copy, scale=qlnwc[:, kc:kc + 1], inc="POq")
            else:
                k4 = kc - 12
                p.dma("sync", wqbst[:, 0:256],
                      ap["wkvb"][128 * k4:128 * (k4 + 1), 256 * h:256 * (h + 1)],
                      inc="WQ")
                p.wait("scalar", "WQ")
                ACT(wkvbbf[sl][:, 256 * k4:256 * (k4 + 1)],
                    wqbst[:, 0:256], AF.Copy, scale=kvlnwc[:, k4:k4 + 1], inc="POq")
            hw_state["POq"] = p.cnt["POq"]
            hw_state[("kc", h)] = kc + 1
            head_ready[h] = p.cnt["POq"]

        def load_head_weights(h):
            for _ in range(16):
                head_weight_step(h)

        p.wait("tensor", "VE")
        p.wait("tensor", "SE")
        load_head_weights(0)
        pb_se = 0
        pt_se = 0
        out_ve = [0, 0]
        vev = [0, 0]

        for h in range(NH):
            sl = h % 2
            wbf = wqbbf[sl]
            kvbf = wkvbbf[sl]
            p.wait("tensor", "POq", head_ready[h])

            p.wait("tensor", "SE", pb_se)
            for kc in range(12):
                for s4 in range(4):
                    MM(PB[:, 512 * s4:512 * (s4 + 1)],
                       wbf[:, 192 * kc:192 * kc + 128],
                       q_a[:, 2048 * kc + 512 * s4:2048 * kc + 512 * (s4 + 1)],
                       kc == 0, kc == 11,
                       inc=("PE" if (kc == 11 and s4 == 3) else None))
            p.wait("scalar", "PE")
            ACT(qnope[:], PB[:], AF.Copy, inc="SE")
            pb_se = p.cnt["SE"]
            p.wait("tensor", "SE", pb_se)
            for kc in range(12):
                for s4 in range(4):
                    MM(PB[0:64, 512 * s4:512 * (s4 + 1)],
                       wbf[:, 192 * kc + 128:192 * (kc + 1)],
                       q_a[:, 2048 * kc + 512 * s4:2048 * kc + 512 * (s4 + 1)],
                       kc == 0, kc == 11,
                       inc=("PE" if (kc == 11 and s4 == 3) else None))
            p.wait("scalar", "PE")
            ACT(qper[:], PB[0:64, :], AF.Copy, inc="SE")
            pb_se = p.cnt["SE"]
            p.wait("tensor", "SE", pb_se)
            for s4 in range(4):
                MM(PB[0:64, 512 * s4:512 * (s4 + 1)], rotm[:],
                   qper[:, 512 * s4:512 * (s4 + 1)], True, True,
                   inc=("PE" if s4 == 3 else None))
            p.wait("vector", "PE")
            TT("vector", scrd[:], PB[0:64, :], cs_pack[64:128, :], ALU.mult)
            TT("vector", qper[:], qper[:], cs_pack[0:64, :], ALU.mult)
            TT("vector", qper[:], qper[:], scrd[:], ALU.add, inc="VE")
            pb_ve = p.cnt["VE"]
            p.wait("tensor", "VE", pb_ve)
            for kc in range(4):
                for s4 in range(4):
                    MM(PB[:, 512 * s4:512 * (s4 + 1)],
                       kvbf[:, 256 * kc:256 * kc + 128],
                       ckv_n[:, 2048 * kc + 512 * s4:2048 * kc + 512 * (s4 + 1)],
                       kc == 0, kc == 3,
                       inc=("PE" if (kc == 3 and s4 == 3) else None))
            p.wait("scalar", "PE")
            ACT(knope[:], PB[:], AF.Copy, inc="SE")
            pb_se = p.cnt["SE"]
            for T in range(NT):
                POx = PO1 if T % 2 == 0 else PO2
                p.wait("tensor", "SE", vev[T % 2])
                p.wait("tensor", "VE", out_ve[T % 2])
                for kc in range(4):
                    MM(POx[:, 0:128],
                       ckv_n[:, 2048 * kc + 128 * T:2048 * kc + 128 * (T + 1)],
                       kvbf[:, 256 * kc + 128:256 * (kc + 1)],
                       kc == 0, kc == 3, inc=("PE" if kc == 3 else None))
                p.wait("scalar", "PE")
                ACT(vtm[:, 128 * T:128 * (T + 1)], POx[:, 0:128], AF.Copy, inc="SE")
                vev[T % 2] = p.cnt["SE"]
            for qi in range(NT):
                if h + 1 < NH:
                    head_weight_step(h + 1)
                Eext = 128 * (qi + 1)
                nsl = (Eext + 511) // 512
                p.wait("tensor", "SE", pb_se)
                for s4 in range(nsl):
                    w = min(512, Eext - 512 * s4)
                    MM(PB[:, 512 * s4:512 * s4 + w], qnope[:, 128 * qi:128 * (qi + 1)],
                       knope[:, 512 * s4:512 * s4 + w], True, False)
                for s4 in range(nsl):
                    w = min(512, Eext - 512 * s4)
                    MM(PB[:, 512 * s4:512 * s4 + w], qper[:, 128 * qi:128 * (qi + 1)],
                       kpe_r[:, 512 * s4:512 * s4 + w], False, True,
                       inc=("PE" if s4 == nsl - 1 else None))
                p.wait("vector", "PE")
                TT("vector", PB[:, 128 * qi:128 * (qi + 1)],
                   PB[:, 128 * qi:128 * (qi + 1)], maskt[:], ALU.add)
                RED("vector", scol("ngm"), PB[:, 0:Eext], ALU.max, neg=True)
                SP()
                TT("vector", scol("ngs"), scol("ngm"), escale[:, qi:qi + 1],
                   ALU.mult)
                VEM()
                p.wait("scalar", "VE")
                ACT(psb[:, 0:Eext], PB[:, 0:Eext], AF.Exp,
                    bias=scol("ngs"), scale=escale[:, qi:qi + 1],
                    accum=scol("accraw"))
                ACT(scol("ell"), scol("accraw"), AF.Copy, inc="SE")
                pb_se = p.cnt["SE"]
                p.wait("vector", "SE")
                RECIP(scol("linv"), scol("ell"))
                SP()
                TS("vector", psb[:, 0:Eext], psb[:, 0:Eext], scol("linv"),
                   ALU.mult, inc="VE")
                p.wait("tensor", "VE")
                p.wait("tensor", "SE", pt_se)
                for j in range(qi + 1):
                    PT = PTa if j < 8 else PTb
                    TR(PT[:, 128 * (j % 8):128 * (j % 8) + 128],
                       psb[:, 128 * j:128 * (j + 1)], identb[:],
                       inc=("PE" if j == qi else None))
                p.wait("scalar", "PE")
                ACT(ptsb[:, 0:min(Eext, 1024)], PTa[:, 0:min(Eext, 1024)], AF.Copy,
                    inc=("SE" if qi < 8 else None))
                if qi >= 8:
                    ACT(ptsb[:, 1024:Eext], PTb[:, 0:Eext - 1024], AF.Copy, inc="SE")
                pt_se = p.cnt["SE"]
                POx = PO1 if qi % 2 == 0 else PO2
                p.wait("tensor", "SE", pt_se)
                p.wait("tensor", "VE", out_ve[qi % 2])
                for j in range(qi + 1):
                    MM(POx[:, 0:128], vtm[:, 128 * j:128 * (j + 1)],
                       ptsb[:, 128 * j:128 * (j + 1)], j == 0, j == qi,
                       inc=("PE" if j == qi else None))
                p.wait("vector", "PE")
                CP("vector",
                   attnout[:, 2048 * h + 128 * qi:2048 * h + 128 * (qi + 1)],
                   POx[:, 0:128], inc="VE")
                out_ve[qi % 2] = p.cnt["VE"]
                if qi % 4 == 3:
                    wprep_step(1)
    dump("attnout", attnout[:])
    if stop_at <= 3:
        return _finish(p, nc, dbg_out, out, early=True)

    phase_barrier()
    stk_qa.close(); p.open_stacks.remove(stk_qa)

    with ExitStack() as stE:
        PB = stE.enter_context(nc.psum_tensor("PBE", [128, 2048], F32))
        PTa = stE.enter_context(nc.psum_tensor("PTaE", [128, 1024], BF16))
        PTb = stE.enter_context(nc.psum_tensor("PTbE", [128, 1024], BF16))
        wo_bf = stE.enter_context(nc.sbuf_tensor("wo_bf", [128, 16 * 2048], BF16))
        xstg1 = stE.enter_context(nc.sbuf_tensor("xstgE", [128, 2048], F32))
        xstg = [xstg1, xstg1]
        x2t = stE.enter_context(nc.sbuf_tensor("x2t", [128, 2048], F32))
        segA = stE.enter_context(nc.sbuf_tensor("segAE", [128, 2048], F32))
        segB = stE.enter_context(nc.sbuf_tensor("segBE", [128, 2048], F32))
        segI = stE.enter_context(nc.sbuf_tensor("segIE", [128, 2048], I32))
        i1tm = stE.enter_context(nc.sbuf_tensor("i1tm", [128, 2048], BF16))
        i1st = stE.enter_context(nc.sbuf_tensor("i1st", [128, 2048], BF16))

        p.wait("tensor", "VE")
        p.wait("tensor", "SE")
        p.wait("vector", "WF4", p.cnt.get("WF4", 0))
        wost = segB
        wo_cast = []
        for c in range(16):
            p.wait("sync", "POw", p.cnt.get("POw", 0))
            p.dma("sync", wost[:], ap["wo"][128 * c:128 * (c + 1), :], inc="WOL")
            p.wait("scalar", "WOL", 16 * (c + 1))
            ACT(wo_bf[:, 2048 * c:2048 * (c + 1)], wost[:], AF.Copy, inc="POw")
            wo_cast.append(p.cnt["POw"])
            wprep_step(2)

        xe_guard = [0, 0]
        pb_free = 0
        pt_guard = [0, 0]
        for T in range(NT):
            sl = 0
            semx = "XEA"
            p.wait("sync", "VE", xe_guard[sl])
            p.dma("sync", xstg[sl][:], ap["x"][128 * T:128 * (T + 1), :], inc=semx)
            p.wait("tensor", "VE", pb_free)
            if T == 0:
                p.wait("tensor", "POw", wo_cast[15])
            for c in range(16):
                for s4 in range(4):
                    MM(PB[:, 512 * s4:512 * (s4 + 1)],
                       attnout[:, 2048 * c + 128 * T:2048 * c + 128 * (T + 1)],
                       wo_bf[:, 2048 * c + 512 * s4:2048 * c + 512 * (s4 + 1)],
                       c == 0, c == 15,
                       inc=("PE" if (c == 15 and s4 == 3) else None))
            p.wait("vector", semx)
            p.wait("vector", "PE")
            p.wait("vector", "X2S", p.cnt.get("X2S", 0))
            TT("vector", x2t[:], PB[:], xstg[sl][:], ALU.add, inc="VE")
            pb_free = p.cnt["VE"]
            xe_guard[sl] = p.cnt["VE"]
            p.wait("scalar", "VE")
            p.dma("scalar", x2s[128 * T:128 * (T + 1), :], x2t[:], inc="X2S")
            if DBG("x2"):
                p.dma("scalar", dbg_out["x2"][128 * T:128 * (T + 1), :], x2t[:],
                      inc="ST")
            lnstats(x2t[:], segA[:])
            TS("vector", segA[:], x2t[:], scol("mu"), ALU.subtract, scol("rv"),
               ALU.mult, inc="VE")
            p.wait("scalar", "VE")
            ACTACC(segB[:], segA[:], AF.Square, scol("q2"))
            p.wait("vector", "SE")
            TS("vector", scol("m2l"), scol("q2"), 1.0 / D, ALU.mult)
            VEM()
            p.wait("scalar", "VE")
            ACT(scol("sdl"), scol("m2l"), AF.Sqrt, bias=col_eps5, inc="SE")
            p.wait("vector", "SE")
            RECIP(scol("r2"), scol("sdl"))
            RED("vector", scol("mx"), segA[:], ALU.max, absval=True)
            SP()
            TT("vector", scol("rm"), scol("r2"), scol("mx"), ALU.mult)
            SP()
            TS("vector", scol("rmc"), scol("rm"), 1e-5, ALU.max)
            SP()
            RECIP(scol("irmc"), scol("rmc"))
            SP()
            TS("vector", osc1c[:, T:T + 1], scol("rmc"), scol("m127_1"), ALU.mult)
            STT("vector", scol("c1"), scol("irmc"), 127.0, scol("r2"),
                ALU.mult, ALU.mult)
            SP()
            TS("vector", segI[:], segA[:], scol("c1"), ALU.mult)
            CP("vector", i1tm[:], segI[:])
            STT("vector", segB[:], segA[:], scol("c1"), i1tm[:], ALU.mult,
                ALU.subtract)
            TS("vector", segA[:], segB[:], col_h, ALU.is_gt)
            TS("vector", segB[:], segB[:], col_nh, ALU.is_lt)
            TT("vector", segA[:], segA[:], segB[:], ALU.subtract)
            TT("vector", i1tm[:], i1tm[:], segA[:], ALU.add, inc="VE")
            p.wait("tensor", "VE")
            for g, PT in ((0, PTa), (1, PTb)):
                p.wait("tensor", "SE", pt_guard[g])
                for c in range(8):
                    cc = 8 * g + c
                    TR(PT[:, 128 * c:128 * (c + 1)], i1tm[:, 128 * cc:128 * (cc + 1)],
                       identb[:], inc=("PE" if c == 7 else None))
                p.wait("scalar", "PE")
                p.wait("scalar", "I1S", p.cnt.get("I1S", 0))
                ACT(i1st[:, 1024 * g:1024 * (g + 1)], PT[:], AF.Copy, inc="SE")
                pt_guard[g] = p.cnt["SE"]
            p.wait("scalar", "SE")
            p.dma("scalar",
                  int1s.rearrange("(c i) t -> i c t", i=128)[:, :, 128 * T:128 * (T + 1)],
                  i1st[:].rearrange("i (c t) -> i c t", t=128), inc="I1S")
            wprep_step(3)
    dump("osc1", osc1c[:])
    if DBG("int1"):
        pass
    if stop_at <= 4:
        return _finish(p, nc, dbg_out, out, early=True)

    while wstate["i"] < len(wtB):
        wprep_step(4)
    phase_barrier()
    stk_w.close(); p.open_stacks.remove(stk_w)
    stk_ao.close(); p.open_stacks.remove(stk_ao)
    stk_cs.close()

    with ExitStack() as stF:
        PB = stF.enter_context(nc.psum_tensor("PBF", [128, 2048], F32))
        PTa = stF.enter_context(nc.psum_tensor("PTaF", [128, 1024], BF16))
        PTb = stF.enter_context(nc.psum_tensor("PTbF", [128, 1024], BF16))
        PH = stF.enter_context(nc.psum_tensor("PHF", [128, 1024], F32))
        hbuf = stF.enter_context(nc.sbuf_tensor("hbuf", [128, 64 * QW], BF16))
        int1h = stF.enter_context(nc.sbuf_tensor("int1h", [128, 16 * QW], BF16))
        big = [stF.enter_context(nc.sbuf_tensor(f"bigF{i}", [128, INNER], BF16))
               for i in range(2)]
        htm = stF.enter_context(nc.sbuf_tensor("htmF", [128, INNER], BF16))
        zbuf = stF.enter_context(nc.sbuf_tensor("zbuf", [128, INNER], BF16))
        segA = stF.enter_context(nc.sbuf_tensor("segAF", [128, 2048], F32))
        segB = stF.enter_context(nc.sbuf_tensor("segBF", [128, 2048], F32))
        segI = stF.enter_context(nc.sbuf_tensor("segIF", [128, 2048], I32))
        w1sb = [stF.enter_context(nc.sbuf_tensor(f"w1sb{i}", [128, 2048], BF16))
                for i in range(2)]
        s1b = stF.enter_context(nc.sbuf_tensor("s1b", [128, QW], F32))
        out2 = stF.enter_context(nc.sbuf_tensor("out2F", [128, 16 * QW], BF16))
        x3t = segA
        b2bc = stF.enter_context(nc.sbuf_tensor("b2bc", [128, 2048], BF16))
        rrow = None

        p.dma("sync", segA[0:1, 0:2048], ap["b2row"], inc="GL")
        w1q_done = p.cnt["WSS"]
        i1_done = p.cnt["I1S"]
        x2_done = p.cnt["X2S"]
        p.wait("tensor", "GL")
        p.wait("tensor", "VE")
        p.wait("tensor", "SE")
        for s4 in range(4):
            MM(PB[:, 512 * s4:512 * (s4 + 1)], onesrow[:],
               segA[0:1, 512 * s4:512 * (s4 + 1)], True, True,
               inc=("PE" if s4 == 3 else None))
        p.wait("vector", "PE")
        CP("vector", b2bc[:], PB[:], inc="VE")
        p.wait("tensor", "VE")
        TR(PB[0:16, 0:128], osc1c[:], ident[:], inc="PE")
        p.wait("vector", "PE")
        CP("vector", mstage[:], PB[0:16, 0:128], inc="VE")
        p.wait("scalar", "VE")
        p.dma("scalar", rowscr, mstage[:], inc="RS")

        def build_bcast(dst_bf, src_dram):
            for blk in range(4):
                p.wait("sync", "SE", p.cnt["SE"])
                p.wait("sync", "VE", p.cnt["VE"])
                p.dma("sync", segA[0:1, 0:2048],
                      src_dram[0:1, 2048 * blk:2048 * (blk + 1)], inc="GBL")
                p.wait("tensor", "GBL")
                p.wait("tensor", "SE", p.cnt["SE"])
                p.wait("tensor", "VE", p.cnt["VE"])
                for s4 in range(4):
                    MM(PB[:, 512 * s4:512 * (s4 + 1)], onesrow[:],
                       segA[0:1, 512 * s4:512 * (s4 + 1)], True, True,
                       inc=("PE" if s4 == 3 else None))
                p.wait("scalar", "PE")
                ACT(dst_bf[:, 2048 * blk:2048 * (blk + 1)], PB[:], AF.Copy, inc="SE")
            p.wait("tensor", "SE")

        mm1_guard = [0, 0]
        silu_guard = 0
        pt_guard = [0, 0]
        f3_guard = [0, 0]
        merge_done = [0]
        for q in range(NQ):
            t0 = QW * q
            p.wait("sync", "RS")
            p.wait("sync", "SE", p.cnt["SE"])
            p.dma("sync", segB[0:1, 0:QW],
                  rowscr.rearrange("a b -> (a b)").rearrange("(o n) -> o n", o=1)[
                      0:1, t0:t0 + QW], inc="RR")
            p.wait("tensor", "RR")
            p.wait("tensor", "VE", p.cnt["VE"])
            MM(PB[:, 0:512], onesrow[:], segB[0:1, 0:512], True, True, inc="PE")
            p.wait("vector", "PE")
            CP("vector", s1b[:], PB[:, 0:512], inc="VE")
            if q == 0:
                p.wait("sync", "I1S", i1_done)
                p.wait("sync", "WSS", w1q_done)
            p.wait("sync", "VE", merge_done[0])
            for kc in range(16):
                p.dma("sync", int1h[:, QW * kc:QW * (kc + 1)],
                      int1s[128 * kc:128 * (kc + 1), t0:t0 + QW], inc="I1R")
            for oc in range(64):
                sl = oc % 2
                p.wait("sync", "PE", mm1_guard[sl])
                p.dma("sync", w1sb[sl][:], w1q[oc],
                      inc=("W1A" if sl == 0 else "W1B"))
                p.wait("tensor", "W1A" if sl == 0 else "W1B")
                if oc == 0:
                    p.wait("tensor", "I1R", 16 * 16 * (q + 1))
                    p.wait("tensor", "VE", p.cnt["VE"])
                p.wait("tensor", "SE", silu_guard)
                ph = PH[:, 512 * sl:512 * (sl + 1)]
                for kc in range(16):
                    MM(ph, w1sb[sl][:, 128 * kc:128 * (kc + 1)],
                       int1h[:, QW * kc:QW * (kc + 1)], kc == 0, kc == 15,
                       inc=("PE" if kc == 15 else None))
                mm1_guard[sl] = p.cnt["PE"]
                p.wait("vector", "PE")
                TT("vector", segA[:, 0:512], ph, s1b[:], ALU.mult, inc="VE")
                p.wait("scalar", "VE")
                ACT(segB[:, 0:512], segA[:, 0:512], AF.Silu, bias=b1c[:, oc:oc + 1])
                ACT(hbuf[:, QW * oc:QW * (oc + 1)], segB[:, 0:512], AF.Gelu,
                    inc="SE")
                silu_guard = p.cnt["SE"]
            if q == 0:
                dump("h0", hbuf[:])
            p.wait("tensor", "W2A", p.cnt.get("W2A", 0))
            p.wait("tensor", "W2B", p.cnt.get("W2B", 0))
            p.wait("tensor", "PO", p.cnt.get("PO", 0))
            build_bcast(big[0], ap["gln"])
            build_bcast(big[1], ap["bln"])
            for Ti in range(QW // 128):
                for c in range(64):
                    g = (c // 8) % 2
                    PT = PTa if g == 0 else PTb
                    if c % 8 == 0:
                        p.wait("tensor", "SE", pt_guard[g])
                    TR(PT[:, 128 * (c % 8):128 * (c % 8) + 128],
                       hbuf[:, QW * c + 128 * Ti:QW * c + 128 * (Ti + 1)],
                       identb[:], inc=("PE" if c % 8 == 7 else None))
                    if c % 8 == 7:
                        p.wait("scalar", "PE")
                        ACT(htm[:, 1024 * (c // 8):1024 * (c // 8 + 1)], PT[:],
                            AF.Copy, inc="SE")
                        pt_guard[g] = p.cnt["SE"]
                p.wait("vector", "SE")
                for sgi in range(4):
                    sg = htm[:, 2048 * sgi:2048 * (sgi + 1)]
                    RED("vector", scol("p1"), sg, ALU.add, inc="VE")
                    p.wait("scalar", "VE")
                    ACTACC(segA[:], sg, AF.Square, scol("p2"))
                    p.wait("vector", "SE")
                    SP()
                    if sgi == 0:
                        CP("vector", scol("sx"), scol("p1"))
                        CP("vector", scol("sq"), scol("p2"))
                    else:
                        TT("vector", scol("sx"), scol("sx"), scol("p1"), ALU.add)
                        TT("vector", scol("sq"), scol("sq"), scol("p2"), ALU.add)
                    SP()
                TS("vector", scol("mu"), scol("sx"), 1.0 / INNER, ALU.mult)
                TS("vector", scol("m2"), scol("sq"), 1.0 / INNER, ALU.mult)
                SP()
                STT("vector", scol("v2"), scol("mu"), scol("mu"), scol("m2"),
                    ALU.mult, ALU.subtract)
                VEM()
                p.wait("scalar", "VE")
                ACT(scol("v1"), scol("v2"), AF.Sqrt, bias=col_eps5, scale=-1.0,
                    inc="SE")
                p.wait("vector", "SE")
                RECIP(scol("rv"), scol("v1"))
                VEM()
                for sgi in range(4):
                    sg = htm[:, 2048 * sgi:2048 * (sgi + 1)]
                    zs = zbuf[:, 2048 * sgi:2048 * (sgi + 1)]
                    TS("vector", segA[:], sg, scol("mu"), ALU.subtract, scol("rv"),
                       ALU.mult)
                    TT("vector", segA[:], segA[:], big[0][:, 2048 * sgi:2048 * (sgi + 1)],
                       ALU.mult)
                    TT("vector", zs, segA[:], big[1][:, 2048 * sgi:2048 * (sgi + 1)],
                       ALU.add, inc="VE")
                    RED("vector", scol("p1"), zs, ALU.max, absval=True)
                    p.wait("scalar", "VE")
                    ACTACC(segB[:], zs, AF.Square, scol("p2"))
                    p.wait("vector", "SE")
                    SP()
                    if sgi == 0:
                        CP("vector", scol("zq"), scol("p2"))
                        CP("vector", scol("zm"), scol("p1"))
                    else:
                        TT("vector", scol("zq"), scol("zq"), scol("p2"), ALU.add)
                        TT("vector", scol("zm"), scol("zm"), scol("p1"), ALU.max)
                    SP()
                TS("vector", scol("m2z"), scol("zq"), 1.0 / INNER, ALU.mult)
                VEM()
                p.wait("scalar", "VE")
                ACT(scol("sdz"), scol("m2z"), AF.Sqrt, bias=col_eps5, inc="SE")
                p.wait("vector", "SE")
                RECIP(scol("rho"), scol("sdz"))
                SP()
                TT("vector", scol("rm"), scol("rho"), scol("zm"), ALU.mult)
                SP()
                TS("vector", scol("rmc"), scol("rm"), 1e-5, ALU.max)
                SP()
                RECIP(scol("irmc"), scol("rmc"))
                SP()
                TS("vector", osc2c[:, 4 * q + Ti:4 * q + Ti + 1], scol("rmc"),
                   scol("m127_2"), ALU.mult)
                STT("vector", scol("c2"), scol("irmc"), 127.0, scol("rho"),
                    ALU.mult, ALU.mult)
                SP()
                for sgi in range(4):
                    sg = htm[:, 2048 * sgi:2048 * (sgi + 1)]
                    zs = zbuf[:, 2048 * sgi:2048 * (sgi + 1)]
                    TS("vector", segI[:], zs, scol("c2"), ALU.mult)
                    CP("vector", sg, segI[:])
                    STT("vector", segB[:], zs, scol("c2"), sg, ALU.mult,
                        ALU.subtract)
                    TS("vector", segA[:], segB[:], col_h, ALU.is_gt)
                    TS("vector", segB[:], segB[:], col_nh, ALU.is_lt)
                    TT("vector", segA[:], segA[:], segB[:], ALU.subtract)
                    TT("vector", sg, sg, segA[:], ALU.add, inc="VE")
                p.wait("tensor", "VE")
                for c in range(64):
                    g = (c // 8) % 2
                    PT = PTa if g == 0 else PTb
                    if c % 8 == 0:
                        p.wait("tensor", "SE", pt_guard[g])
                    TR(PT[:, 128 * (c % 8):128 * (c % 8) + 128],
                       htm[:, 128 * c:128 * (c + 1)], identb[:],
                       inc=("PE" if c % 8 == 7 else None))
                    if c % 8 == 7:
                        p.wait("scalar", "PE")
                        c0 = c - 7
                        dst = hbuf[:].rearrange("i (a t) -> i a t", t=QW)[
                            :, c0:c0 + 8, 128 * Ti:128 * (Ti + 1)]
                        ACT(dst, PT[:].rearrange("i (a t) -> i a t", t=128),
                            AF.Copy, inc="SE")
                        pt_guard[g] = p.cnt["SE"]
            if q == 0:
                dump("int2_0", hbuf[:])
                dump("osc2", osc2c[:])
            f3v_guard = 0
            f2_done = p.cnt["VE"]
            for m in range(16):
                sl = m % 2
                semw = "W2A" if sl == 0 else "W2B"
                p.wait("sync", "PE", f3_guard[sl])
                if m < 2:
                    p.wait("sync", "VE", f2_done)
                p.dma("sync", big[sl][:], w2q[m], inc=semw)
                p.wait("tensor", semw)
                p.wait("tensor", "VE", p.cnt["VE"] if m < 2 else f3v_guard)
                for kc in range(64):
                    MM(PB[:, 512 * sl:512 * (sl + 1)],
                       big[sl][:, 128 * kc:128 * (kc + 1)],
                       hbuf[:].rearrange("i (a t) -> i a t", t=QW)[:, kc, :],
                       kc == 0, kc == 63, inc=("PE" if kc == 63 else None))
                f3_guard[sl] = p.cnt["PE"]
                p.wait("vector", "PE")
                CP("vector", out2[:, QW * m:QW * (m + 1)], PB[:, 512 * sl:512 * (sl + 1)],
                   inc="VE")
                f3v_guard = p.cnt["VE"]
            p.wait("sync", "X2S", x2_done)
            for Ti in range(QW // 128):
                trow = t0 + 128 * Ti
                p.wait("sync", "STO", p.cnt.get("STO", 0))
                p.dma("sync", x3t[:], x2s[trow:trow + 128, :], inc="XM")
                p.wait("vector", "XM")
                TT("vector", x3t[:], x3t[:], b2bc[:], ALU.add, inc="VE")
                for mg in range(2):
                    PT = PTa if mg == 0 else PTb
                    p.wait("tensor", "VE", p.cnt["VE"])
                    for mm_ in range(8):
                        m = 8 * mg + mm_
                        TR(PT[:, 128 * mm_:128 * (mm_ + 1)],
                           out2[:, QW * m + 128 * Ti:QW * m + 128 * (Ti + 1)],
                           identb[:], inc=("PE" if mm_ == 7 else None))
                    p.wait("vector", "PE")
                    for mm_ in range(8):
                        m = 8 * mg + mm_
                        STT("vector", x3t[:, 128 * m:128 * (m + 1)],
                            PT[:, 128 * mm_:128 * (mm_ + 1)],
                            osc2c[:, 4 * q + Ti:4 * q + Ti + 1],
                            x3t[:, 128 * m:128 * (m + 1)], ALU.mult, ALU.add,
                            inc=("VE" if mm_ == 7 else None))
                p.wait("scalar", "VE")
                p.dma("scalar", out[trow:trow + 128, :], x3t[:], inc="STO")
                merge_done[0] = p.cnt["VE"]

    return _finish(p, nc, dbg_out, out, early=False)


def _finish(p, nc, dbg_out, out, early):
    if "STO" in p.cnt:
        p.wait("gpsimd", "STO")
    if "ST" in p.cnt:
        p.wait("gpsimd", "ST")
    blk = p.ctx.enter_context(nc.Block())
    p.finalize(blk)
    for stk in reversed(p.open_stacks):
        stk.close()
    p.ctx.close()
    return nc, dbg_out


def make_inmap(inputs, core):
    hs = np.ascontiguousarray(np.asarray(inputs["L0_attn_hidden_states"])[core])
    w1t = np.ascontiguousarray(np.asarray(inputs["L0_ffn_w1"]).T)
    w2t = np.ascontiguousarray(np.asarray(inputs["L0_ffn_w2"]).T)
    m = dict(
        x=hs.astype(np.float32),
        wqa=np.ascontiguousarray(np.asarray(inputs["L0_attn_Wqa"], np.float32)),
        wqb=np.ascontiguousarray(np.asarray(inputs["L0_attn_Wqb"], np.float32)),
        wkva=np.ascontiguousarray(np.asarray(inputs["L0_attn_Wkva"], np.float32)),
        wkvb=np.ascontiguousarray(np.asarray(inputs["L0_attn_Wkvb"], np.float32)),
        wo=np.ascontiguousarray(np.asarray(inputs["L0_attn_Wo"], np.float32)),
        qlnw=np.ascontiguousarray(np.asarray(inputs["L0_attn_q_a_ln_w"],
                                             np.float32).reshape(12, 128)),
        kvlnw=np.ascontiguousarray(np.asarray(inputs["L0_attn_kv_a_ln_w"],
                                              np.float32).reshape(4, 128)),
        w1t=w1t.astype(np.float32), w2t=w2t.astype(np.float32),
        b1=np.ascontiguousarray(np.asarray(inputs["L0_ffn_b1"],
                                           np.float32).reshape(64, 128)),
        b2=np.ascontiguousarray(np.asarray(inputs["L0_ffn_b2"],
                                           np.float32).reshape(16, 128)),
        b2row=np.ascontiguousarray(np.asarray(inputs["L0_ffn_b2"],
                                              np.float32).reshape(1, 2048)),
        gln=np.ascontiguousarray(np.asarray(inputs["L0_ffn_ln_g"],
                                            np.float32).reshape(1, INNER)),
        bln=np.ascontiguousarray(np.asarray(inputs["L0_ffn_ln_b"],
                                            np.float32).reshape(1, INNER)),
    )
    m.update(host_consts())
    return m


def kernel(**inputs):
    nc, _ = build_program(dbg=(), upto="F")
    in_maps = [make_inmap(inputs, c) for c in range(8)]
    res = run_bass_kernel_spmd(nc, in_maps, core_ids=list(range(8)))
    outs = [np.asarray(r["out"], dtype=np.float32) for r in res.results]
    return np.stack(outs, axis=0)
